# version 01
# baseline (speedup 1.0000x reference)
import numpy as np
from contextlib import ExitStack

import concourse.bass as bass
import concourse.mybir as mybir
from concourse.alu_op_type import AluOpType as OP
from concourse.bass_utils import run_bass_kernel_spmd

F32 = mybir.dt.float32
BF16 = mybir.dt.bfloat16
AF = mybir.ActivationFunctionType
AX = mybir.AxisListType.X

B, T, D = 4, 4096, 1024
NT = T // 128
NF = D // 128
PAD = 16
BIG = 1.0e30

ENGINES = ("sync", "tensor", "vector", "scalar", "gpsimd")


class Prog:

    def __init__(self, nc):
        self.nc = nc
        self.ops = []

    def emit(self, engine, fn, deps=(), dma=False):
        i = len(self.ops)
        ds = set()
        for d in deps:
            if d is None:
                continue
            ds.add(d)
        self.ops.append(dict(e=engine, fn=fn, deps=sorted(ds), dma=dma, needed=dma, idx=i))
        return i

    def finalize(self, es):
        nc = self.nc
        for op in self.ops:
            for d in op["deps"]:
                self.ops[d]["needed"] = True
        csem = {e: es.enter_context(nc.semaphore(f"c_{e}")) for e in ENGINES}
        dsem = {e: es.enter_context(nc.semaphore(f"d_{e}")) for e in ENGINES}
        ccnt = {e: 0 for e in ENGINES}
        dcnt = {e: 0 for e in ENGINES}
        for op in self.ops:
            e = op["e"]
            if op["dma"]:
                dcnt[e] += 16
                op["ev"] = (e, True, dcnt[e])
            elif op["needed"]:
                ccnt[e] += 1
                op["ev"] = (e, False, ccnt[e])
            else:
                op["ev"] = None
        per_eng = {e: [op for op in self.ops if op["e"] == e] for e in ENGINES}
        dma_before = {e: [] for e in ENGINES}
        run = {e: 0 for e in ENGINES}
        cum = []
        for op in self.ops:
            if op["dma"]:
                run[op["e"]] += 1
            cum.append(dict(run))
        block = es.enter_context(nc.Block())

        def mk(eng_name):
            ops = per_eng[eng_name]

            def body(eng):
                hw = {}
                for op in ops:
                    my_idx = op["idx"]
                    waits = {}
                    if op["dma"]:
                        _, _, myval = op["ev"]
                        prior = myval - 16
                        if prior > 0:
                            waits[(eng_name, True)] = prior
                    for d in op["deps"]:
                        ev = self.ops[d]["ev"]
                        en, is_d, val = ev
                        key = (en, is_d)
                        if hw.get(key, 0) < val:
                            waits[key] = max(waits.get(key, 0), val)
                    for (en, is_d), val in sorted(waits.items()):
                        eng.wait_ge(dsem[en] if is_d else csem[en], val)
                        hw[(en, is_d)] = val
                    inst = op["fn"](eng)
                    ev = op["ev"]
                    if ev is not None:
                        en, is_d, _ = ev
                        inst.then_inc(dsem[en] if is_d else csem[en], 16 if is_d else 1)
            return body

        block.sync(mk("sync"))
        block.tensor(mk("tensor"))
        block.vector(mk("vector"))
        block.scalar(mk("scalar"))
        block.gpsimd(mk("gpsimd"))

    def barrier_deps(self):
        return list(range(len(self.ops)))

    def fence(self):
        deps = self.barrier_deps()
        return [self.emit(e, lambda en: en.nop(), deps=deps) for e in ENGINES]


class Buf:

    __slots__ = ("ap", "w", "r", "open")

    def __init__(self, ap, init=None):
        self.ap = ap
        self.w = list(init) if init else []
        self.r = []
        self.open = False

    def rd(self):
        return list(self.w)

    def wr(self):
        return self.r + self.w

    def did_read(self, opid):
        self.r.append(opid)
        self.open = False

    def did_write(self, opid, accum=False):
        if accum and self.open:
            self.w.append(opid)
        else:
            self.w = [opid]
            self.r = []
            self.open = bool(accum)


def sbuf(nc, es, name, shape, dt, init=None):
    return Buf(es.enter_context(nc.sbuf_tensor(name, shape, dt)), init=init)


def psum(nc, es, name, shape, dt):
    return Buf(es.enter_context(nc.psum_tensor(name, shape, dt)))


def dbuf(nc, name, shape, dt):
    return Buf(nc.dram_tensor(name, shape, dt).ap())


def _bf16():
    import ml_dtypes
    return ml_dtypes.bfloat16


def t_(x):
    return np.ascontiguousarray(np.asarray(x).T)


def vec8(x):
    x = np.asarray(x)
    return np.ascontiguousarray(x.reshape(-1, 128).T)


def brow(x, rows=128):
    x = np.asarray(x)
    return np.ascontiguousarray(np.broadcast_to(x[None, :], (rows, x.shape[0])))


def alloc_common(nc, es, proj_pools=True):
    C = {}
    C["psS1"] = psum(nc, es, "psS1", [128, 1024], F32)
    C["psS2"] = psum(nc, es, "psS2", [128, 1024], F32)
    C["psT1"] = psum(nc, es, "psT1", [128, 1024], BF16)
    C["psT2"] = psum(nc, es, "psT2", [128, 1024], BF16)
    C["ps0"] = psum(nc, es, "ps0", [128, 512], F32)
    C["ps1"] = psum(nc, es, "ps1", [128, 512], F32)
    C["stA"] = [sbuf(nc, es, f"stA{i}", [128, 1024], F32) for i in range(2)]
    C["stB"] = [sbuf(nc, es, f"stB{i}", [128, 1024], BF16) for i in range(2)]
    C["stC"] = [sbuf(nc, es, f"stC{i}", [128, 1024], BF16) for i in range(2)]
    C["stD"] = [sbuf(nc, es, f"stD{i}", [128, 1024], F32) for i in range(2)]
    C["stE"] = [sbuf(nc, es, f"stE{i}", [128, 1024], F32) for i in range(2)]
    C["sm"] = [sbuf(nc, es, f"sm{i}", [128, 8], F32) for i in range(4)]
    C["ident"] = sbuf(nc, es, "ident_sb", [128, 128], BF16)
    if proj_pools:
        C["rhsP"] = [[sbuf(nc, es, f"rhsP{i}_{c}", [128, 512], BF16) for c in range(NF)] for i in range(2)]
        C["lhsP"] = [[sbuf(nc, es, f"lhsP{i}_{c}", [128, 128], BF16) for c in range(NF)] for i in range(2)]
        C["vstg"] = [sbuf(nc, es, f"vstg{i}", [128, 520], BF16) for i in range(2)]
    return C


def load_ident(P, C, ident_d):
    op = P.emit("gpsimd", lambda e, o=C["ident"].ap, i=ident_d: e.dma_start(out=o[:, :], in_=i, accum_op=OP.bypass), dma=True)
    C["ident"].did_write(op)


def tr128(P, C, src, ssl, dst, dsl, extra=()):
    idb = C["ident"]
    op = P.emit("tensor",
                lambda e, o=dst.ap, s=src.ap, i=idb.ap, a=dsl, b=ssl:
                e.transpose(o[:, a:a + 128], s[:, b:b + 128], i[:, :]),
                deps=list(extra) + src.rd() + idb.rd() + dst.wr())
    src.did_read(op)
    idb.did_read(op)
    dst.did_write(op, accum=True)
    return op


def load_w(P, nc, es, name, w_dram, cin, cout):
    buf = sbuf(nc, es, name, [128, (cin // 128) * cout], BF16)
    for c in range(cin // 128):
        op = P.emit("gpsimd",
                    lambda e, o=buf.ap, i=w_dram, c=c, co=cout:
                    e.dma_start(out=o[:, c * co:(c + 1) * co], in_=i[c * 128:(c + 1) * 128, :], accum_op=OP.bypass),
                    deps=buf.r, dma=True)
        buf.did_write(op, accum=True)
    return buf


def load_f32(P, nc, es, name, v_dram, shape, eng="sync"):
    buf = sbuf(nc, es, name, shape, F32)
    op = P.emit(eng, lambda e, o=buf.ap, i=v_dram: e.dma_start(out=o[:, :], in_=i), deps=buf.wr(), dma=True)
    buf.did_write(op)
    return buf


def load_bf16(P, nc, es, name, v_dram, shape):
    buf = sbuf(nc, es, name, shape, BF16)
    op = P.emit("gpsimd", lambda e, o=buf.ap, i=v_dram: e.dma_start(out=o[:, :], in_=i, accum_op=OP.bypass), deps=buf.wr(), dma=True)
    buf.did_write(op)
    return buf


def emit_norm(P, C, src, xnT, xn_tok, kind="ln", gamma_bc=None, eps=1e-5,
              n_tok=T, src_bf16=False):
    psT = [C["psT1"], C["psT2"]]
    for t in range(n_tok // 128):
        a = C["stA"][t % 2]
        d2 = C["stD"][t % 2]
        b = C["stB"][t % 2]
        c = C["stC"][t % 2]
        s = C["sm"][t % 4]
        if src_bf16:
            ld = P.emit("gpsimd", lambda e, o=a.ap, i=src.ap, t=t: e.dma_start(out=o[:, :], in_=i[t * 128:(t + 1) * 128, :], accum_op=OP.bypass), deps=a.wr() + src.rd(), dma=True)
        else:
            ld = P.emit("sync", lambda e, o=a.ap, i=src.ap, t=t: e.dma_start(out=o[:, :], in_=i[t * 128:(t + 1) * 128, :]), deps=a.wr() + src.rd(), dma=True)
        a.did_write(ld)
        src.did_read(ld)
        o_sq = P.emit("scalar", lambda e, o=d2.ap, i=a.ap: e.activation(o[:, :], i[:, :], AF.Square), deps=a.rd() + d2.wr())
        d2.did_write(o_sq)
        a.did_read(o_sq)
        o_ss = P.emit("vector", lambda e, s=s.ap, i=d2.ap: e.reduce_sum(s[:, 1:2], i[:, :], axis=AX), deps=[o_sq] + s.wr())
        s.did_write(o_ss)
        d2.did_read(o_ss)
        if kind == "ln":
            o_sm = P.emit("vector", lambda e, s=s.ap, i=a.ap: e.reduce_sum(s[:, 0:1], i[:, :], axis=AX), deps=a.rd() + [o_ss])
            a.did_read(o_sm)
            o_nm = P.emit("vector", lambda e, s=s.ap: e.tensor_scalar(s[:, 2:3], s[:, 0:1], -1.0 / D, None, OP.mult), deps=[o_sm])
            o_m2 = P.emit("vector", lambda e, s=s.ap: e.tensor_tensor(s[:, 3:4], s[:, 2:3], s[:, 2:3], OP.mult), deps=[o_nm])
            o_s4 = P.emit("vector", lambda e, s=s.ap: e.tensor_scalar(s[:, 4:5], s[:, 1:2], 1.0 / D, None, OP.mult), deps=[o_ss, o_m2])
            o_v = P.emit("vector", lambda e, s=s.ap: e.tensor_tensor(s[:, 4:5], s[:, 4:5], s[:, 3:4], OP.subtract), deps=[o_s4])
            var_col, pre = 4, [o_v]
        else:
            o_v = P.emit("vector", lambda e, s=s.ap: e.tensor_scalar(s[:, 4:5], s[:, 1:2], 1.0 / D, None, OP.mult), deps=[o_ss])
            var_col, pre = 4, [o_v]
        o_ep = P.emit("vector", lambda e, s=s.ap, vc=var_col, eps=eps: e.tensor_scalar(s[:, 7:8], s[:, vc:vc + 1], eps, None, OP.add), deps=pre)
        o_sd = P.emit("scalar", lambda e, s=s.ap: e.activation(s[:, 5:6], s[:, 7:8], AF.Sqrt), deps=[o_ep])
        o_r = P.emit("vector", lambda e, s=s.ap: e.reciprocal(s[:, 6:7], s[:, 5:6]), deps=[o_sd])
        s.did_write(o_r)
        if kind == "ln":
            o_c = P.emit("vector", lambda e, o=d2.ap, i=a.ap, s=s.ap: e.tensor_scalar(o[:, :], i[:, :], s[:, 2:3], None, OP.add), deps=a.rd() + [o_r] + d2.wr() + [o_ss])
            d2.did_write(o_c)
            a.did_read(o_c)
            o_n = P.emit("vector", lambda e, o=b.ap, i=d2.ap, s=s.ap: e.tensor_scalar(o[:, :], i[:, :], s[:, 6:7], None, OP.mult), deps=[o_c, o_r] + b.wr())
            b.did_write(o_n)
            d2.did_read(o_n)
            s.did_read(o_n)
        else:
            if gamma_bc is None:
                o_n = P.emit("vector", lambda e, o=b.ap, i=a.ap, s=s.ap: e.tensor_scalar(o[:, :], i[:, :], s[:, 6:7], None, OP.mult), deps=a.rd() + [o_r] + b.wr())
            else:
                o_n = P.emit("vector", lambda e, o=b.ap, i=a.ap, s=s.ap, g=gamma_bc.ap:
                             e.scalar_tensor_tensor(o[:, :], i[:, :], s[:, 6:7], g[:, :], OP.mult, OP.mult),
                             deps=a.rd() + [o_r] + b.wr() + gamma_bc.rd())
                gamma_bc.did_read(o_n)
            b.did_write(o_n)
            a.did_read(o_n)
            s.did_read(o_n)
        if xn_tok is not None:
            st = P.emit("scalar", lambda e, i=b.ap, o=xn_tok.ap, t=t: e.dma_start(out=o[t * 128:(t + 1) * 128, :], in_=i[:, :]), deps=b.rd() + [o_n] + xn_tok.r, dma=True)
            b.did_read(st)
            xn_tok.did_write(st, accum=True)
        pt = psT[t % 2]
        trs = [tr128(P, C, b, f * 128, pt, f * 128, extra=[o_n]) for f in range(NF)]
        ev = P.emit("vector", lambda e, o=c.ap, i=pt.ap: e.tensor_copy(o[:, :], i[:, :]), deps=[trs[-1]] + c.wr())
        c.did_write(ev)
        pt.did_read(ev)
        for f in range(NF):
            st = P.emit("scalar",
                        lambda e, i=c.ap, o=xnT.ap, t=t, f=f:
                        e.dma_start(out=o[f * 128:(f + 1) * 128, PAD + t * 128:PAD + (t + 1) * 128], in_=i[:, f * 128:(f + 1) * 128]),
                        deps=c.rd() + [ev] + xnT.r, dma=True)
            c.did_read(st)
            xnT.did_write(st, accum=True)


def emit_zero_pad(P, C, targets):
    z = C["stB"][0]
    zop = P.emit("gpsimd", lambda e, z=z.ap: e.memset(z[:, 0:PAD], 0.0), deps=z.wr())
    z.did_write(zop)
    for d, rows in targets:
        for f in range(rows // 128):
            op = P.emit("gpsimd", lambda e, z=z.ap, d=d.ap, f=f: e.dma_start(out=d[f * 128:(f + 1) * 128, 0:PAD], in_=z[:, 0:PAD]), deps=z.rd() + d.r, dma=True)
            z.did_read(op)
            d.did_write(op, accum=True)


def proj_fm(P, C, src, w_sb, dst, n_out, bias_vec=None, scale=1.0, act=None,
            cin=D, n_tok=T, src_pad=PAD, dst_pad=0, post_cb=None):
    ncin = cin // 128
    nout = n_out // 128
    ps = [C["ps0"], C["ps1"]]
    af = act or AF.Copy
    for tc in range(n_tok // 512):
        rh = C["rhsP"][tc % 2]
        for c in range(ncin):
            op = P.emit("sync", lambda e, o=rh[c].ap, i=src.ap, c=c, tc=tc, sp=src_pad:
                        e.dma_start(out=o[:, :], in_=i[c * 128:(c + 1) * 128, sp + tc * 512: sp + (tc + 1) * 512]),
                        deps=rh[c].wr() + src.rd(), dma=True)
            rh[c].did_write(op)
            src.did_read(op)
        for do in range(nout):
            p = ps[do % 2]
            mms = []
            for c in range(ncin):
                op = P.emit("tensor",
                            lambda e, p=p.ap, w=w_sb.ap, r=rh[c].ap, c=c, do=do, no=n_out, first=(c == 0), last=(c == ncin - 1):
                            e.matmul(p[:, :], w[:, c * no + do * 128: c * no + do * 128 + 128], r[:, :], start=first, stop=last),
                            deps=(p.wr() if c == 0 else []) + w_sb.rd() + rh[c].rd() + (mms[-1:] if mms else []))
                mms.append(op)
                rh[c].did_read(op)
                w_sb.did_read(op)
            p.did_write(mms[-1])
            eb = C["stC"][do % 2]
            deps = [mms[-1]] + eb.wr() + (bias_vec.rd() if bias_vec is not None else [])
            if bias_vec is not None and act is None:
                s2 = None if scale == 1.0 else scale
                o1 = None if s2 is None else OP.mult
                op = P.emit("vector", lambda e, o=eb.ap, i=p.ap, b=bias_vec.ap, do=do, s2=s2, o1=o1:
                            e.tensor_scalar(o[:, 0:512], i[:, :], b[:, do:do + 1], s2, OP.add, o1) if o1 is not None
                            else e.tensor_scalar(o[:, 0:512], i[:, :], b[:, do:do + 1], None, OP.add), deps=deps)
                bias_vec.did_read(op)
            elif bias_vec is not None:
                op = P.emit("scalar", lambda e, o=eb.ap, i=p.ap, b=bias_vec.ap, do=do, af=af, sc=scale:
                            e.activation(o[:, 0:512], i[:, :], af, bias=b[:, do:do + 1], scale=sc), deps=deps)
                bias_vec.did_read(op)
            else:
                op = P.emit("scalar", lambda e, o=eb.ap, i=p.ap, af=af, sc=scale:
                            e.activation(o[:, 0:512], i[:, :], af, scale=sc), deps=deps)
            eb.did_write(op)
            p.did_read(op)
            if post_cb is not None:
                eb, op = post_cb(do, tc, eb, op)
            st = P.emit("scalar", lambda e, i=eb.ap, o=dst.ap, do=do, tc=tc, dp=dst_pad:
                        e.dma_start(out=o[do * 128:(do + 1) * 128, dp + tc * 512: dp + (tc + 1) * 512], in_=i[:, 0:512]),
                        deps=eb.rd() + [op] + dst.r, dma=True)
            eb.did_read(st)
            dst.did_write(st, accum=True)


def proj_tm(P, C, srcT, w_sb, dst_cb, cin=D, cout=D, n_tok=T, src_pad=PAD):
    ncin = cin // 128
    ndc = cout // 512
    ps = [C["ps0"], C["ps1"]]
    for tt in range(n_tok // 128):
        lh = C["lhsP"][tt % 2]
        for c in range(ncin):
            op = P.emit("sync", lambda e, o=lh[c].ap, i=srcT.ap, c=c, tt=tt, sp=src_pad:
                        e.dma_start(out=o[:, :], in_=i[c * 128:(c + 1) * 128, sp + tt * 128: sp + (tt + 1) * 128]),
                        deps=lh[c].wr() + srcT.rd(), dma=True)
            lh[c].did_write(op)
            srcT.did_read(op)
        for dc in range(ndc):
            p = ps[dc % 2]
            mms = []
            for c in range(ncin):
                op = P.emit("tensor",
                            lambda e, p=p.ap, l=lh[c].ap, w=w_sb.ap, c=c, dc=dc, co=cout, first=(c == 0), last=(c == ncin - 1):
                            e.matmul(p[:, :], l[:, :], w[:, c * co + dc * 512: c * co + (dc + 1) * 512], start=first, stop=last),
                            deps=(p.wr() if c == 0 else []) + lh[c].rd() + w_sb.rd() + (mms[-1:] if mms else []))
                mms.append(op)
                lh[c].did_read(op)
                w_sb.did_read(op)
            p.did_write(mms[-1])
            dst_cb(tt, dc, p, mms[-1])


def emit_attention(P, C, nc, es, qT, kT, vaug, oT, masks, tag):
    NH = 16
    psS = [C["psS1"], C["psS2"]]
    psT = [C["psT1"], C["psT2"]]
    psAV = [C["ps0"], C["ps1"]]
    JG = 8
    qsb = [sbuf(nc, es, f"{tag}q{f}", [128, 1024], BF16) for f in range(NF)]
    ksb = [[sbuf(nc, es, f"{tag}k{i}_{f}", [128, 1024], BF16) for f in range(NF)] for i in range(1)]
    vsb = [[sbuf(nc, es, f"{tag}v{i}_{f}", [128, 1040], BF16) for f in range(NF)] for i in range(1)]
    Orun = [sbuf(nc, es, f"{tag}or{h}", [128, 520], F32) for h in range(NH)]
    Psb = [sbuf(nc, es, f"{tag}p{i}", [128, 1024], BF16) for i in range(2)]
    PTsb = [sbuf(nc, es, f"{tag}pt{i}", [128, 1024], BF16) for i in range(2)]
    stats = [sbuf(nc, es, f"{tag}st{h}", [128, 32], F32) for h in range(NH)]
    obuf = [sbuf(nc, es, f"{tag}ob{i}", [128, 1024], BF16) for i in range(2)]

    for jg in range(NT // JG):
        jlist = list(range(jg * JG, (jg + 1) * JG))
        nch = (jlist[-1] // 4 + 2) // 2
        for f in range(NF):
            op = P.emit("sync", lambda e, o=qsb[f].ap, i=qT.ap, f=f, jg=jg:
                        e.dma_start(out=o[:, :], in_=i[f * 128:(f + 1) * 128, jg * 1024:(jg + 1) * 1024]),
                        deps=qsb[f].wr() + qT.rd(), dma=True)
            qsb[f].did_write(op)
            qT.did_read(op)
        for h in range(NH):
            op = P.emit("gpsimd", lambda e, s=stats[h].ap: e.memset(s[:, :], BIG), deps=stats[h].wr())
            stats[h].did_write(op)
            op = P.emit("gpsimd", lambda e, o=Orun[h].ap: e.memset(o[:, :], 0.0), deps=Orun[h].wr())
            Orun[h].did_write(op)
        for ch in range(nch):
            kv = 0
            for f in range(NF):
                op = P.emit("sync", lambda e, o=ksb[kv][f].ap, i=kT.ap, f=f, ch=ch:
                            e.dma_start(out=o[:, :], in_=i[f * 128:(f + 1) * 128, ch * 1024:(ch + 1) * 1024]),
                            deps=ksb[kv][f].wr() + kT.rd(), dma=True)
                ksb[kv][f].did_write(op)
                kT.did_read(op)
                op = P.emit("sync", lambda e, o=vsb[kv][f].ap, i=vaug.ap, f=f, ch=ch:
                            e.dma_start(out=o[:, :], in_=i[ch * 1024 + f * 128: ch * 1024 + (f + 1) * 128, :]),
                            deps=vsb[kv][f].wr() + vaug.rd(), dma=True)
                vsb[kv][f].did_write(op)
                vaug.did_read(op)
            for ji, j in enumerate(jlist):
                kt_j = j // 4 + 1
                if ch * 2 >= kt_j:
                    continue
                tiles = [0, 1] if ch * 2 + 1 < kt_j else [0]
                width = len(tiles) * 512
                diag = (ch * 2 + len(tiles) == kt_j)
                for h in range(NH):
                    fq = h // 2
                    ro = (h % 2) * 64
                    S = stats[h]
                    c0, c1, c2 = ji * 4, ji * 4 + 1, ji * 4 + 2
                    pS = psS[h % 2]
                    mms = []
                    for ti in tiles:
                        op = P.emit("tensor",
                                    lambda e, p=pS.ap, q=qsb[fq].ap, k=ksb[kv][fq].ap, ji=ji, ro=ro, ti=ti:
                                    e.matmul(p[:, ti * 512:(ti + 1) * 512],
                                             q[ro:ro + 64, ji * 128:(ji + 1) * 128],
                                             k[ro:ro + 64, ti * 512:(ti + 1) * 512], start=True, stop=True),
                                    deps=pS.wr() + qsb[fq].rd() + ksb[kv][fq].rd() + (mms[-1:] if mms else []))
                        mms.append(op)
                        qsb[fq].did_read(op)
                        ksb[kv][fq].did_read(op)
                    pS.did_write(mms[-1])
                    o1 = P.emit("vector", lambda e, s=S.ap, p=pS.ap, c1=c1, w=width:
                                e.reduce_max(s[:, c1:c1 + 1], p[:, 0:w], axis=AX), deps=[mms[-1]] + S.wr())
                    pS.did_read(o1)
                    S.did_write(o1)
                    o2 = P.emit("vector", lambda e, s=S.ap, c1=c1: e.tensor_scalar(s[:, c1:c1 + 1], s[:, c1:c1 + 1], -1.0, None, OP.mult), deps=[o1])
                    o3 = P.emit("vector", lambda e, s=S.ap, c0=c0, c1=c1: e.tensor_tensor(s[:, c1:c1 + 1], s[:, c0:c0 + 1], s[:, c1:c1 + 1], OP.min), deps=[o2])
                    o4 = P.emit("vector", lambda e, s=S.ap, c0=c0, c1=c1, c2=c2: e.tensor_tensor(s[:, c2:c2 + 1], s[:, c1:c1 + 1], s[:, c0:c0 + 1], OP.subtract), deps=[o3])
                    o5 = P.emit("scalar", lambda e, s=S.ap, c2=c2: e.activation(s[:, c2:c2 + 1], s[:, c2:c2 + 1], AF.Exp), deps=[o4])
                    o6 = P.emit("vector", lambda e, s=S.ap, c0=c0, c1=c1: e.tensor_copy(s[:, c0:c0 + 1], s[:, c1:c1 + 1]), deps=[o4])
                    S.did_write(o6)
                    pb = Psb[h % 2]
                    o7 = P.emit("scalar", lambda e, o=pb.ap, p=pS.ap, s=S.ap, c1=c1, w=width:
                                e.activation(o[:, 0:w], p[:, 0:w], AF.Exp, bias=s[:, c1:c1 + 1]),
                                deps=[mms[-1], o3] + pb.wr())
                    pb.did_write(o7)
                    pS.did_read(o7)
                    S.did_read(o7)
                    if diag:
                        mk = masks[j % 4]
                        lo = (len(tiles) - 1) * 512
                        o8 = P.emit("vector", lambda e, o=pb.ap, m=mk.ap, lo=lo:
                                    e.tensor_tensor(o[:, lo:lo + 512], o[:, lo:lo + 512], m[:, :], OP.mult),
                                    deps=[o7] + mk.rd())
                        mk.did_read(o8)
                        pb.did_write(o8)
                    pt = psT[h % 2]
                    trs = []
                    for bk in range(width // 128):
                        trs.append(tr128(P, C, pb, bk * 128, pt, bk * 128))
                    ptb = PTsb[h % 2]
                    o9 = P.emit("vector", lambda e, o=ptb.ap, i=pt.ap, w=width: e.tensor_copy(o[:, 0:w], i[:, 0:w]),
                                deps=[trs[-1]] + ptb.wr())
                    ptb.did_write(o9)
                    pt.did_read(o9)
                    pav = psAV[h % 2]
                    avs = []
                    for bk in range(width // 128):
                        op = P.emit("tensor",
                                    lambda e, p=pav.ap, v=vsb[kv][bk].ap, pt_=ptb.ap, h=h, bk=bk, first=(bk == 0), last=(bk == width // 128 - 1):
                                    e.matmul(p[:, 0:65], pt_[:, bk * 128:(bk + 1) * 128], v[:, h * 65:(h + 1) * 65], start=first, stop=last),
                                    deps=(pav.wr() if bk == 0 else []) + [o9] + vsb[kv][bk].rd() + (avs[-1:] if avs else []))
                        avs.append(op)
                        ptb.did_read(op)
                        vsb[kv][bk].did_read(op)
                    pav.did_write(avs[-1])
                    orun = Orun[h]
                    o10 = P.emit("vector",
                                 lambda e, o=orun.ap, p=pav.ap, s=S.ap, ji=ji, c2=c2:
                                 e.scalar_tensor_tensor(o[:, ji * 65:(ji + 1) * 65], o[:, ji * 65:(ji + 1) * 65], s[:, c2:c2 + 1], p[:, 0:65], OP.mult, OP.add),
                                 deps=orun.wr() + [avs[-1], o5])
                    orun.did_write(o10)
                    pav.did_read(o10)
                    S.did_read(o10)
        for ji, j in enumerate(jlist):
            ob = obuf[ji % 2]
            for h in range(NH):
                orun = Orun[h]
                S = stats[h]
                c3 = ji * 4 + 3
                o1 = P.emit("vector", lambda e, s=S.ap, o=orun.ap, ji=ji, c3=c3:
                            e.reciprocal(s[:, c3:c3 + 1], o[:, ji * 65 + 64: ji * 65 + 65]),
                            deps=orun.rd() + S.wr())
                S.did_write(o1)
                orun.did_read(o1)
                o2 = P.emit("vector", lambda e, o=ob.ap, i=orun.ap, s=S.ap, ji=ji, h=h, c3=c3:
                            e.tensor_scalar(o[:, h * 64:(h + 1) * 64], i[:, ji * 65: ji * 65 + 64], s[:, c3:c3 + 1], None, OP.mult),
                            deps=[o1] + ob.wr())
                ob.did_write(o2, accum=True)
                orun.did_read(o2)
                S.did_read(o2)
            pt = psT[ji % 2]
            trs = [tr128(P, C, ob, f * 128, pt, f * 128) for f in range(NF)]
            ev = C["stC"][ji % 2]
            o3 = P.emit("vector", lambda e, o=ev.ap, i=pt.ap: e.tensor_copy(o[:, :], i[:, :]), deps=[trs[-1]] + ev.wr())
            ev.did_write(o3)
            pt.did_read(o3)
            for f in range(NF):
                st = P.emit("scalar", lambda e, i=ev.ap, o=oT.ap, f=f, j=j:
                            e.dma_start(out=o[f * 128:(f + 1) * 128, j * 128:(j + 1) * 128], in_=i[:, f * 128:(f + 1) * 128]),
                            deps=ev.rd() + [o3] + oT.r, dma=True)
                ev.did_read(st)
                oT.did_write(st, accum=True)


def emit_vproj(P, C, srcT, w_sb, bias_bc, vaug, n_tok=T, src_pad=PAD):
    ones_ops = []
    for i in range(2):
        vs = C["vstg"][i]
        op = P.emit("gpsimd", lambda e, v=vs.ap: e.memset(v[:, :], 1.0), deps=vs.wr())
        vs.did_write(op)
        ones_ops.append(op)

    def cb(tt, dc, p, mm):
        vs = C["vstg"][tt % 2]
        w_ops = []
        for h8 in range(8):
            if bias_bc is not None:
                op = P.emit("vector",
                            lambda e, o=vs.ap, i=p.ap, b=bias_bc.ap, h8=h8, dc=dc:
                            e.tensor_tensor(o[:, h8 * 65: h8 * 65 + 64], i[:, h8 * 64:(h8 + 1) * 64], b[:, dc * 512 + h8 * 64: dc * 512 + (h8 + 1) * 64], OP.add),
                            deps=[mm] + vs.wr() + bias_bc.rd())
                bias_bc.did_read(op)
            else:
                op = P.emit("vector",
                            lambda e, o=vs.ap, i=p.ap, h8=h8:
                            e.tensor_copy(o[:, h8 * 65: h8 * 65 + 64], i[:, h8 * 64:(h8 + 1) * 64]),
                            deps=[mm] + vs.wr())
            vs.did_write(op, accum=True)
            w_ops.append(op)
        p.did_read(w_ops[-1])
        st = P.emit("scalar",
                    lambda e, i=vs.ap, o=vaug.ap, tt=tt, dc=dc:
                    e.dma_start(out=o[tt * 128:(tt + 1) * 128, dc * 520:(dc + 1) * 520], in_=i[:, :]),
                    deps=w_ops + vaug.r, dma=True)
        vs.did_read(st)
        vaug.did_write(st, accum=True)

    proj_tm(P, C, srcT, w_sb, cb, cin=D, cout=D, n_tok=n_tok, src_pad=src_pad)


def emit_epilogue(P, C, x_in, aout, fout, x_out, n_tok=T):
    for t in range(n_tok // 128):
        a = C["stA"][t % 2]
        u = C["stB"][t % 2]
        v = C["stC"][t % 2]
        d = C["stD"][t % 2]
        l1 = P.emit("sync", lambda e, o=a.ap, i=x_in.ap, t=t: e.dma_start(out=o[:, :], in_=i[t * 128:(t + 1) * 128, :]), deps=a.wr() + x_in.rd(), dma=True)
        a.did_write(l1)
        x_in.did_read(l1)
        l2 = P.emit("sync", lambda e, o=u.ap, i=aout.ap, t=t: e.dma_start(out=o[:, :], in_=i[t * 128:(t + 1) * 128, :]), deps=u.wr() + aout.rd(), dma=True)
        u.did_write(l2)
        aout.did_read(l2)
        l3 = P.emit("sync", lambda e, o=v.ap, i=fout.ap, t=t: e.dma_start(out=o[:, :], in_=i[t * 128:(t + 1) * 128, :]), deps=v.wr() + fout.rd(), dma=True)
        v.did_write(l3)
        fout.did_read(l3)
        f2 = C["stE"][t % 2]
        o1 = P.emit("vector", lambda e, o=d.ap, i0=u.ap, i1=v.ap: e.tensor_tensor(o[:, :], i0[:, :], i1[:, :], OP.add), deps=[l2, l3] + d.wr())
        d.did_write(o1)
        u.did_read(o1)
        v.did_read(o1)
        o2 = P.emit("vector", lambda e, o=f2.ap, i0=d.ap, i1=a.ap: e.tensor_tensor(o[:, :], i0[:, :], i1[:, :], OP.add), deps=[o1, l1] + f2.wr())
        f2.did_write(o2)
        d.did_read(o2)
        a.did_read(o2)
        st = P.emit("scalar", lambda e, i=f2.ap, o=x_out.ap, t=t: e.dma_start(out=o[t * 128:(t + 1) * 128, :], in_=i[:, :]), deps=f2.rd() + [o2] + x_out.r, dma=True)
        f2.did_read(st)
        x_out.did_write(st, accum=True)


def final_waits(P):
    dma_ops = [i for i, op in enumerate(P.ops) if op["dma"]]
    P.emit("sync", lambda e: e.nop(), deps=dma_ops)


_WPOOL_N = [0]


def alloc_wpool(nc, es, n=2, cols=8192):
    _WPOOL_N[0] += 1
    tag = _WPOOL_N[0]
    return [sbuf(nc, es, f"wpool{tag}_{i}", [128, cols], BF16) for i in range(n)]


def load_w_into(P, wbuf, w_dram, cin, cout):
    deps0 = wbuf.wr()
    ops = []
    for c in range(cin // 128):
        op = P.emit("gpsimd",
                    lambda e, o=wbuf.ap, i=w_dram, c=c, co=cout:
                    e.dma_start(out=o[:, c * co:(c + 1) * co], in_=i[c * 128:(c + 1) * 128, :], accum_op=OP.bypass),
                    deps=deps0, dma=True)
        ops.append(op)
    wbuf.w = ops
    wbuf.r = []
    return wbuf


def emit_shift_lerp(P, C, nc, es, xnT, xrT, xkT, mur, muk):
    om_r = sbuf(nc, es, "om_r", [128, 8], F32)
    om_k = sbuf(nc, es, "om_k", [128, 8], F32)
    for src, dst in ((mur, om_r), (muk, om_k)):
        o1 = P.emit("vector", lambda e, o=dst.ap, i=src.ap: e.tensor_scalar(o[:, :], i[:, :], -1.0, None, OP.mult), deps=src.rd() + dst.wr())
        src.did_read(o1)
        o2 = P.emit("vector", lambda e, o=dst.ap: e.tensor_scalar(o[:, :], o[:, :], 1.0, None, OP.add), deps=[o1])
        dst.did_write(o2)
    for f in range(NF):
        for tc in range(T // 512):
            b = C["stB"][tc % 2]
            d = C["stD"][tc % 2]
            c = C["stC"][tc % 2]
            ld = P.emit("sync", lambda e, o=b.ap, i=xnT.ap, f=f, tc=tc:
                        e.dma_start(out=o[:, 0:513], in_=i[f * 128:(f + 1) * 128, PAD + tc * 512 - 1: PAD + (tc + 1) * 512]),
                        deps=b.wr() + xnT.rd(), dma=True)
            b.did_write(ld)
            xnT.did_read(ld)
            for (mu, om, half, dstd) in ((mur, om_r, 0, xrT), (muk, om_k, 1, xkT)):
                o1 = P.emit("vector", lambda e, o=d.ap, i=b.ap, m=om.ap, f=f:
                            e.tensor_scalar(o[:, 0:512], i[:, 0:512], m[:, f:f + 1], None, OP.mult),
                            deps=b.rd() + om.rd() + d.wr())
                d.did_write(o1)
                b.did_read(o1)
                om.did_read(o1)
                o2 = P.emit("vector", lambda e, o=c.ap, i=b.ap, m=mu.ap, d2=d.ap, f=f, half=half:
                            e.scalar_tensor_tensor(o[:, half * 512:half * 512 + 512], i[:, 1:513], m[:, f:f + 1], d2[:, 0:512], OP.mult, OP.add),
                            deps=[o1] + b.rd() + mu.rd() + c.wr())
                c.did_write(o2, accum=True)
                b.did_read(o2)
                mu.did_read(o2)
                d.did_read(o2)
                st = P.emit("scalar", lambda e, i=c.ap, o=dstd.ap, f=f, tc=tc, half=half:
                            e.dma_start(out=o[f * 128:(f + 1) * 128, tc * 512:(tc + 1) * 512], in_=i[:, half * 512:half * 512 + 512]),
                            deps=[o2] + dstd.r, dma=True)
                c.did_read(st)
                dstd.did_write(st, accum=True)


def build_layer0():
    nc = bass.Bass("TRN2", target_bir_lowering=False)
    di = lambda n, s: Buf(nc.dram_tensor(n, s, F32, kind="ExternalInput").ap())
    x_in = di("x", [T, D])
    ident_d = di("ident", [128, 128])
    masks_d = di("masks", [512, 512])
    wq_d = di("wq", [D, D]); wk_d = di("wk", [D, D]); wv_d = di("wv", [D, D]); wo_d = di("wo", [D, D])
    bq_d = di("bq", [128, 8]); bk_d = di("bk", [128, 8])
    bv_d = di("bvb", [128, D]); bo_d = di("bob", [128, D])
    wrT_d = di("wrT", [D, D]); wkT_d = di("wkT", [D, D]); wvT_d = di("wvT", [D, D])
    mur_d = di("mur", [128, 8]); muk_d = di("muk", [128, 8])
    y = Buf(nc.dram_tensor("y", [T, D], F32, kind="ExternalOutput").ap())

    xnT = dbuf(nc, "xnT", [D, PAD + T], BF16)
    xn_tok = dbuf(nc, "xn_tok", [T, D], BF16)
    qT = dbuf(nc, "qT", [D, T], BF16)
    kT = dbuf(nc, "kT", [D, T], BF16)
    vaug = dbuf(nc, "vaug", [T, 1040], BF16)
    oT = dbuf(nc, "oT", [D, T], BF16)
    aout = dbuf(nc, "aout", [T, D], BF16)
    fout = dbuf(nc, "fout", [T, D], BF16)
    xrT = dbuf(nc, "xrT", [D, T], BF16)
    xkT = dbuf(nc, "xkT", [D, T], BF16)
    kact = dbuf(nc, "kact", [D, T], BF16)
    r_d = dbuf(nc, "r_d", [T, D], BF16)

    es = ExitStack()
    with es:
        P = Prog(nc)
        C = alloc_common(nc, es)
        W = alloc_wpool(nc, es, 2)
        load_ident(P, C, ident_d.ap)
        masks = []
        for m in range(4):
            mb = sbuf(nc, es, f"mask{m}", [128, 512], BF16)
            op = P.emit("gpsimd", lambda e, o=mb.ap, i=masks_d.ap, m=m: e.dma_start(out=o[:, :], in_=i[m * 128:(m + 1) * 128, :], accum_op=OP.bypass), dma=True)
            mb.did_write(op)
            masks.append(mb)
        emit_zero_pad(P, C, [(xnT, D)])
        emit_norm(P, C, x_in, xnT, xn_tok, kind="ln")
        bq = load_f32(P, nc, es, "bq_s", bq_d.ap, [128, 8])
        bqs = sbuf(nc, es, "bqs", [128, 8], F32)
        o = P.emit("vector", lambda e, o=bqs.ap, i=bq.ap: e.tensor_scalar(o[:, :], i[:, :], 0.125, None, OP.mult), deps=bq.rd() + bqs.wr())
        bq.did_read(o)
        bqs.did_write(o)
        bk = load_f32(P, nc, es, "bk_s", bk_d.ap, [128, 8])
        load_w_into(P, W[0], wq_d.ap, D, D)
        proj_fm(P, C, xnT, W[0], qT, D, bias_vec=bqs, scale=0.125)
        load_w_into(P, W[1], wk_d.ap, D, D)
        proj_fm(P, C, xnT, W[1], kT, D, bias_vec=bk, scale=1.0)
        bvb = load_f32(P, nc, es, "bv_s", bv_d.ap, [128, D])
        load_w_into(P, W[0], wv_d.ap, D, D)
        emit_vproj(P, C, xnT, W[0], bvb, vaug)
        emit_attention(P, C, nc, es, qT, kT, vaug, oT, masks, tag="a0")
        bob = load_f32(P, nc, es, "bo_s", bo_d.ap, [128, D])
        load_w_into(P, W[1], wo_d.ap, D, D)

        def cb_o(tt, dc, p, mm):
            eb = C["stC"][dc % 2]
            o1 = P.emit("vector", lambda e, o=eb.ap, i=p.ap, b=bob.ap, dc=dc:
                        e.tensor_tensor(o[:, 0:512], i[:, :], b[:, dc * 512:(dc + 1) * 512], OP.add),
                        deps=[mm] + eb.wr() + bob.rd())
            bob.did_read(o1)
            eb.did_write(o1)
            p.did_read(o1)
            st = P.emit("scalar", lambda e, i=eb.ap, o=aout.ap, tt=tt, dc=dc:
                        e.dma_start(out=o[tt * 128:(tt + 1) * 128, dc * 512:(dc + 1) * 512], in_=i[:, 0:512]),
                        deps=[o1] + aout.r, dma=True)
            eb.did_read(st)
            aout.did_write(st, accum=True)

        proj_tm(P, C, oT, W[1], cb_o, src_pad=0)
        mur = load_f32(P, nc, es, "mur_s", mur_d.ap, [128, 8])
        muk = load_f32(P, nc, es, "muk_s", muk_d.ap, [128, 8])
        emit_shift_lerp(P, C, nc, es, xnT, xrT, xkT, mur, muk)
        load_w_into(P, W[0], wrT_d.ap, D, D)

        def cb_r(tt, dc, p, mm):
            eb = C["stC"][dc % 2]
            o1 = P.emit("scalar", lambda e, o=eb.ap, i=p.ap: e.activation(o[:, 0:512], i[:, :], AF.Sigmoid), deps=[mm] + eb.wr())
            eb.did_write(o1)
            p.did_read(o1)
            st = P.emit("scalar", lambda e, i=eb.ap, o=r_d.ap, tt=tt, dc=dc:
                        e.dma_start(out=o[tt * 128:(tt + 1) * 128, dc * 512:(dc + 1) * 512], in_=i[:, 0:512]),
                        deps=[o1] + r_d.r, dma=True)
            eb.did_read(st)
            r_d.did_write(st, accum=True)

        proj_tm(P, C, xrT, W[0], cb_r, src_pad=0)
        load_w_into(P, W[1], wkT_d.ap, D, D)

        def cb_sq(do, tc, eb, op):
            nb = C["stB"][do % 2]
            o1 = P.emit("scalar", lambda e, o=nb.ap, i=eb.ap: e.activation(o[:, 0:512], i[:, 0:512], AF.Square), deps=[op] + nb.wr())
            nb.did_write(o1)
            eb.did_read(o1)
            return nb, o1

        proj_fm(P, C, xrT if False else xkT, W[1], kact, D, act=AF.Relu, src_pad=0, post_cb=cb_sq)
        load_w_into(P, W[0], wvT_d.ap, D, D)
        rtile = [sbuf(nc, es, f"rt{i}", [128, 512], BF16) for i in range(2)]

        def cb_v(tt, dc, p, mm):
            rb = rtile[dc % 2]
            ld = P.emit("sync", lambda e, o=rb.ap, i=r_d.ap, tt=tt, dc=dc:
                        e.dma_start(out=o[:, :], in_=i[tt * 128:(tt + 1) * 128, dc * 512:(dc + 1) * 512]),
                        deps=rb.wr() + r_d.rd(), dma=True)
            rb.did_write(ld)
            r_d.did_read(ld)
            eb = C["stC"][dc % 2]
            o1 = P.emit("vector", lambda e, o=eb.ap, i=p.ap, r=rb.ap:
                        e.tensor_tensor(o[:, 0:512], i[:, :], r[:, :], OP.mult),
                        deps=[mm, ld] + eb.wr())
            eb.did_write(o1)
            rb.did_read(o1)
            p.did_read(o1)
            st = P.emit("scalar", lambda e, i=eb.ap, o=fout.ap, tt=tt, dc=dc:
                        e.dma_start(out=o[tt * 128:(tt + 1) * 128, dc * 512:(dc + 1) * 512], in_=i[:, 0:512]),
                        deps=[o1] + fout.r, dma=True)
            eb.did_read(st)
            fout.did_write(st, accum=True)

        proj_tm(P, C, kact, W[0], cb_v, src_pad=0)
        emit_epilogue(P, C, x_in, aout, fout, y)
        final_waits(P)
        P.finalize(es)
    return nc


def _l0_consts():
    ident = np.eye(128, dtype=np.float32)
    i512 = np.arange(512, dtype=np.int64)[None, :]
    i128 = np.arange(128, dtype=np.int64)[:, None]
    masks = np.concatenate([(i512 <= 128 * m + i128).astype(np.float32) for m in range(4)], axis=0)
    return ident, masks


def prep_l0_inputs(ins, xb):
    ident, masks = _l0_consts()
    return {
        "x": np.ascontiguousarray(xb, dtype=np.float32),
        "ident": ident, "masks": masks,
        "wq": np.asarray(ins["L0_attn_wq"], np.float32),
        "wk": np.asarray(ins["L0_attn_wk"], np.float32),
        "wv": np.asarray(ins["L0_attn_wv"], np.float32),
        "wo": np.asarray(ins["L0_attn_wo"], np.float32),
        "bq": vec8(np.asarray(ins["L0_attn_bq"], np.float32)),
        "bk": vec8(np.asarray(ins["L0_attn_bk"], np.float32)),
        "bvb": brow(np.asarray(ins["L0_attn_bv"], np.float32)),
        "bob": brow(np.asarray(ins["L0_attn_bo"], np.float32)),
        "wrT": t_(np.asarray(ins["L0_ffn_W_r"], np.float32)),
        "wkT": t_(np.asarray(ins["L0_ffn_W_k"], np.float32)),
        "wvT": t_(np.asarray(ins["L0_ffn_W_v"], np.float32)),
        "mur": vec8(np.asarray(ins["L0_ffn_mu_r"], np.float32)),
        "muk": vec8(np.asarray(ins["L0_ffn_mu_k"], np.float32)),
    }


def kernel(**inputs):
    x = np.asarray(inputs["L0_attn_x"], np.float32)
    stages = [
        (build_layer0, prep_l0_inputs),
        (build_layer1, prep_l1_inputs),
        (build_layer2, prep_l2_inputs),
        (build_layer3, prep_l3_inputs),
    ]
    cur = x
    for build, prep in stages:
        nc = build()
        in_maps = [prep(inputs, cur[c // 2]) for c in range(8)]
        res = run_bass_kernel_spmd(nc, in_maps, core_ids=list(range(8)))
        cur = np.stack([res.results[2 * b]["y"] for b in range(B)], axis=0)
    return cur


def make_out_cb(P, C, dst, bias_bc):
    def cb(tt, dc, p, mm):
        eb = C["stC"][dc % 2]
        o1 = P.emit("vector", lambda e, o=eb.ap, i=p.ap, b=bias_bc.ap, dc=dc:
                    e.tensor_tensor(o[:, 0:512], i[:, :], b[:, dc * 512:(dc + 1) * 512], OP.add),
                    deps=[mm] + eb.wr() + bias_bc.rd())
        bias_bc.did_read(o1)
        eb.did_write(o1)
        p.did_read(o1)
        st = P.emit("scalar", lambda e, i=eb.ap, o=dst.ap, tt=tt, dc=dc:
                    e.dma_start(out=o[tt * 128:(tt + 1) * 128, dc * 512:(dc + 1) * 512], in_=i[:, 0:512]),
                    deps=[o1] + dst.r, dma=True)
        eb.did_read(st)
        dst.did_write(st, accum=True)
    return cb


def emit_elu1(P, C, buf, op, width=512):
    d = C["stD"][0]
    ee = C["stE"][0]
    o1 = P.emit("vector", lambda e, o=d.ap, i=buf.ap, w=width: e.tensor_scalar(o[:, 0:w], i[:, 0:w], 0.0, None, OP.min), deps=buf.rd() + [op] + d.wr())
    d.did_write(o1)
    buf.did_read(o1)
    o2 = P.emit("scalar", lambda e, o=ee.ap, i=d.ap, w=width: e.activation(o[:, 0:w], i[:, 0:w], AF.Exp), deps=[o1] + ee.wr())
    ee.did_write(o2)
    d.did_read(o2)
    o3 = P.emit("vector", lambda e, o=buf.ap, i=buf.ap, x=ee.ap, w=width:
                e.scalar_tensor_tensor(o[:, 0:w], i[:, 0:w], 0.0, x[:, 0:w], OP.max, OP.add),
                deps=[o2, op] + buf.wr())
    buf.did_write(o3)
    ee.did_read(o3)
    return buf, o3


def build_layer1():
    nc = bass.Bass("TRN2", target_bir_lowering=False)
    di = lambda n, s: Buf(nc.dram_tensor(n, s, F32, kind="ExternalInput").ap())
    x_in = di("x", [T, D])
    ident_d = di("ident", [128, 128])
    wq_d = di("wq", [D, D]); wk_d = di("wk", [D, D]); wv_d = di("wv", [D, D]); wp_d = di("wp", [D, D])
    bq_d = di("bq", [128, 8])
    bkb_d = di("bkb", [128, D]); bvb_d = di("bvb", [128, D]); bpb_d = di("bpb", [128, D])
    wg_d = di("wg", [D, 8])
    A_d = di("Aall", [D, 128])
    Bm_d = di("Ball", [128, D])
    rep_d = di("rep", [8, 128])
    y = Buf(nc.dram_tensor("y", [T, D], F32, kind="ExternalOutput").ap())

    xnT = dbuf(nc, "xnT", [D, PAD + T], BF16)
    qT = dbuf(nc, "qT", [D, T], BF16)
    fk_tok = dbuf(nc, "fk_tok", [T, D], BF16)
    v_tok = dbuf(nc, "v_tok", [T, D], BF16)
    oT = dbuf(nc, "oT", [D, T], BF16)
    aout = dbuf(nc, "aout", [T, D], BF16)
    fout = dbuf(nc, "fout", [T, D], BF16)

    es = ExitStack()
    with es:
        P = Prog(nc)
        C = alloc_common(nc, es)
        W = alloc_wpool(nc, es, 2)
        load_ident(P, C, ident_d.ap)
        emit_zero_pad(P, C, [(xnT, D)])
        emit_norm(P, C, x_in, xnT, None, kind="ln")
        bq = load_f32(P, nc, es, "bq_s", bq_d.ap, [128, 8])
        load_w_into(P, W[0], wq_d.ap, D, D)

        def cb_fq(do, tc, eb, op):
            return emit_elu1(P, C, eb, op)

        proj_fm(P, C, xnT, W[0], qT, D, bias_vec=bq, post_cb=cb_fq)
        bkb = load_f32(P, nc, es, "bkb_s", bkb_d.ap, [128, D])
        bvb = load_f32(P, nc, es, "bvb_s", bvb_d.ap, [128, D])
        load_w_into(P, W[1], wk_d.ap, D, D)

        def cb_fk(tt, dc, p, mm):
            eb = C["stC"][dc % 2]
            o1 = P.emit("vector", lambda e, o=eb.ap, i=p.ap, b=bkb.ap, dc=dc:
                        e.tensor_tensor(o[:, 0:512], i[:, :], b[:, dc * 512:(dc + 1) * 512], OP.add),
                        deps=[mm] + eb.wr() + bkb.rd())
            bkb.did_read(o1)
            eb.did_write(o1)
            p.did_read(o1)
            eb2, o2 = emit_elu1(P, C, eb, o1)
            st = P.emit("scalar", lambda e, i=eb2.ap, o=fk_tok.ap, tt=tt, dc=dc:
                        e.dma_start(out=o[tt * 128:(tt + 1) * 128, dc * 512:(dc + 1) * 512], in_=i[:, 0:512]),
                        deps=[o2] + fk_tok.r, dma=True)
            eb2.did_read(st)
            fk_tok.did_write(st, accum=True)

        proj_tm(P, C, xnT, W[1], cb_fk)
        load_w_into(P, W[0], wv_d.ap, D, D)
        proj_tm(P, C, xnT, W[0], make_out_cb(P, C, v_tok, bvb))
        kv_sb = sbuf(nc, es, "kv_sb", [64, 1024], BF16)
        kv_dram = dbuf(nc, "kv_dram", [64, 1024], BF16)
        kv2_sb = sbuf(nc, es, "kv2_sb", [128, 512], BF16)
        fk_sb = [sbuf(nc, es, f"fk{i}", [128, 1024], BF16) for i in range(2)]
        v_sb = [sbuf(nc, es, f"vv{i}", [128, 1024], BF16) for i in range(2)]
        slots = [(C["psS1"], 0), (C["psS1"], 512), (C["psS2"], 0), (C["psS2"], 512), (C["ps0"], 0), (C["ps1"], 0)]
        for wave in range(3):
            heads = list(range(wave * 6, min(16, wave * 6 + 6)))
            mm_last = {}
            for tc in range(NT):
                fk = fk_sb[tc % 2]
                vv = v_sb[tc % 2]
                l1 = P.emit("sync", lambda e, o=fk.ap, i=fk_tok.ap, tc=tc: e.dma_start(out=o[:, :], in_=i[tc * 128:(tc + 1) * 128, :]), deps=fk.wr() + fk_tok.rd(), dma=True)
                fk.did_write(l1)
                fk_tok.did_read(l1)
                l2 = P.emit("sync", lambda e, o=vv.ap, i=v_tok.ap, tc=tc: e.dma_start(out=o[:, :], in_=i[tc * 128:(tc + 1) * 128, :]), deps=vv.wr() + v_tok.rd(), dma=True)
                vv.did_write(l2)
                v_tok.did_read(l2)
                for hi, h in enumerate(heads):
                    pb, off = slots[hi]
                    op = P.emit("tensor",
                                lambda e, p=pb.ap, l=fk.ap, r=vv.ap, h=h, off=off, first=(tc == 0), last=(tc == NT - 1):
                                e.matmul(p[0:64, off:off + 64], l[:, h * 64:(h + 1) * 64], r[:, h * 64:(h + 1) * 64], start=first, stop=last),
                                deps=(pb.wr() if tc == 0 else []) + fk.rd() + vv.rd() + ([mm_last[hi]] if hi in mm_last else []))
                    mm_last[hi] = op
                    fk.did_read(op)
                    vv.did_read(op)
                    pb.did_write(op, accum=True)
            for hi, h in enumerate(heads):
                pb, off = slots[hi]
                ev = P.emit("vector", lambda e, o=kv_sb.ap, i=pb.ap, h=h, off=off:
                            e.tensor_copy(o[0:64, h * 64:(h + 1) * 64], i[0:64, off:off + 64]),
                            deps=[mm_last[hi]] + kv_sb.r)
                kv_sb.did_write(ev, accum=True)
                pb.did_read(ev)
        stkv = P.emit("scalar", lambda e, i=kv_sb.ap, o=kv_dram.ap: e.dma_start(out=o[0:64, :], in_=i[0:64, :]), deps=kv_sb.rd() + kv_dram.wr(), dma=True)
        kv_sb.did_read(stkv)
        kv_dram.did_write(stkv)
        kvv = kv_dram.ap.rearrange("d (f par c) -> d f par c", par=2, c=64)
        for par in range(2):
            op = P.emit("sync", lambda e, o=kv2_sb.ap, i=kvv, par=par:
                        e.dma_start(out=o[par * 64:(par + 1) * 64, :].rearrange("d (f c) -> d f c", c=64), in_=i[:, :, par, :]),
                        deps=kv_dram.rd() + kv2_sb.wr() + kv2_sb.r, dma=True)
            kv_dram.did_read(op)
            kv2_sb.did_write(op, accum=True)
        fq_sb = [sbuf(nc, es, f"fq{i}", [128, 512], BF16) for i in range(2)]
        o_ev = [sbuf(nc, es, f"oev{i}", [64, 512], BF16) for i in range(2)]
        for tc in range(T // 512):
            for fp in range(NF):
                fq = fq_sb[fp % 2]
                ld = P.emit("sync", lambda e, o=fq.ap, i=qT.ap, fp=fp, tc=tc:
                            e.dma_start(out=o[:, :], in_=i[fp * 128:(fp + 1) * 128, tc * 512:(tc + 1) * 512]),
                            deps=fq.wr() + qT.rd(), dma=True)
                fq.did_write(ld)
                qT.did_read(ld)
                for sub in range(2):
                    h = fp * 2 + sub
                    pb = [C["ps0"], C["ps1"]][sub]
                    ro = sub * 64
                    mm = P.emit("tensor",
                                lambda e, p=pb.ap, l=kv2_sb.ap, r=fq.ap, fp=fp, ro=ro:
                                e.matmul(p[0:64, :], l[ro:ro + 64, fp * 64:(fp + 1) * 64], r[ro:ro + 64, :], start=True, stop=True),
                                deps=pb.wr() + kv2_sb.rd() + fq.rd())
                    kv2_sb.did_read(mm)
                    fq.did_read(mm)
                    pb.did_write(mm)
                    oe = o_ev[sub]
                    ev = P.emit("vector", lambda e, o=oe.ap, i=pb.ap: e.tensor_copy(o[0:64, :], i[0:64, :]), deps=[mm] + oe.wr())
                    oe.did_write(ev)
                    pb.did_read(ev)
                    st = P.emit("scalar", lambda e, i=oe.ap, o=oT.ap, h=h, tc=tc:
                                e.dma_start(out=o[h * 64:(h + 1) * 64, tc * 512:(tc + 1) * 512], in_=i[0:64, :]),
                                deps=[ev] + oT.r, dma=True)
                    oe.did_read(st)
                    oT.did_write(st, accum=True)
        bpb = load_f32(P, nc, es, "bpb_s", bpb_d.ap, [128, D])
        load_w_into(P, W[1], wp_d.ap, D, D)
        proj_tm(P, C, oT, W[1], make_out_cb(P, C, aout, bpb), src_pad=0)

        wg_sb = load_w(P, nc, es, "wg_sb", wg_d.ap, D, 8)
        A_sb = load_w(P, nc, es, "A_sb", A_d.ap, D, 128)
        B_sb = load_w(P, nc, es, "B_sb", Bm_d.ap, 128, D)
        rep_sb = load_bf16(P, nc, es, "rep_sb", rep_d.ap, [8, 128])
        lg_sb = sbuf(nc, es, "lg_sb", [128, 8], F32)
        m8_sb = sbuf(nc, es, "m8_sb", [128, 8], F32)
        w12 = sbuf(nc, es, "w12", [128, 2], F32)
        mk1 = sbuf(nc, es, "mk1", [128, 8], F32)
        mk2 = sbuf(nc, es, "mk2", [128, 8], F32)
        cwb = sbuf(nc, es, "cwb", [128, 128], BF16)
        cwT = sbuf(nc, es, "cwT", [8, T], BF16)
        cwrep = [sbuf(nc, es, f"cwrep{i}", [128, 512], BF16) for i in range(2)]
        hwT = [sbuf(nc, es, f"hwT{i}", [128, 512], BF16) for i in range(2)]
        for tt in range(NT):
            lh = C["lhsP"][tt % 2]
            for c in range(NF):
                op = P.emit("sync", lambda e, o=lh[c].ap, i=xnT.ap, c=c, tt=tt:
                            e.dma_start(out=o[:, :], in_=i[c * 128:(c + 1) * 128, PAD + tt * 128: PAD + (tt + 1) * 128]),
                            deps=lh[c].wr() + xnT.rd(), dma=True)
                lh[c].did_write(op)
                xnT.did_read(op)
            p = C["ps0"]
            mms = []
            for c in range(NF):
                op = P.emit("tensor", lambda e, p=p.ap, l=lh[c].ap, w=wg_sb.ap, c=c, first=(c == 0), last=(c == NF - 1):
                            e.matmul(p[:, 0:8], l[:, :], w[:, c * 8:(c + 1) * 8], start=first, stop=last),
                            deps=(p.wr() if c == 0 else []) + lh[c].rd() + wg_sb.rd() + (mms[-1:] if mms else []))
                mms.append(op)
                lh[c].did_read(op)
                wg_sb.did_read(op)
            p.did_write(mms[-1])
            o1 = P.emit("vector", lambda e, o=lg_sb.ap, i=p.ap: e.tensor_copy(o[:, :], i[:, 0:8]), deps=[mms[-1]] + lg_sb.wr())
            lg_sb.did_write(o1)
            p.did_read(o1)
            o2 = P.emit("vector", lambda e, o=m8_sb.ap, i=lg_sb.ap: e.max(out=o[:, :], in_=i[:, :]), deps=[o1] + m8_sb.wr())
            m8_sb.did_write(o2)
            lg_sb.did_read(o2)
            o3 = P.emit("vector", lambda e, o=w12.ap, i=m8_sb.ap: e.tensor_tensor(o[:, 0:1], i[:, 0:1], i[:, 1:2], OP.subtract), deps=[o2] + w12.wr())
            o4 = P.emit("vector", lambda e, o=w12.ap, i=m8_sb.ap: e.tensor_tensor(o[:, 1:2], i[:, 1:2], i[:, 0:1], OP.subtract), deps=[o2, o3])
            w12.did_write(o4)
            m8_sb.did_read(o4)
            o5 = P.emit("scalar", lambda e, o=w12.ap: e.activation(o[:, 0:1], o[:, 0:1], AF.Sigmoid), deps=[o3, o4])
            o6 = P.emit("scalar", lambda e, o=w12.ap: e.activation(o[:, 1:2], o[:, 1:2], AF.Sigmoid), deps=[o4, o5])
            w12.did_write(o6)
            o7 = P.emit("vector", lambda e, o=mk1.ap, i=lg_sb.ap, m=m8_sb.ap: e.tensor_scalar(o[:, :], i[:, :], m[:, 0:1], None, OP.is_ge), deps=[o2] + mk1.wr())
            mk1.did_write(o7)
            o8 = P.emit("vector", lambda e, o=mk2.ap, i=lg_sb.ap, m=m8_sb.ap: e.tensor_scalar(o[:, :], i[:, :], m[:, 1:2], None, OP.is_ge), deps=[o2, o7] + mk2.wr())
            mk2.did_write(o8)
            lg_sb.did_read(o8)
            m8_sb.did_read(o8)
            o9 = P.emit("vector", lambda e, o=mk2.ap, i=mk1.ap: e.tensor_tensor(o[:, :], o[:, :], i[:, :], OP.subtract), deps=[o7, o8])
            mk2.did_write(o9)
            mk1.did_read(o9)
            oa = P.emit("vector", lambda e, o=mk1.ap, w=w12.ap: e.tensor_scalar(o[:, :], o[:, :], w[:, 0:1], None, OP.mult), deps=[o9, o6])
            mk1.did_write(oa)
            w12.did_read(oa)
            ob = P.emit("vector", lambda e, o=mk2.ap, w=w12.ap: e.tensor_scalar(o[:, :], o[:, :], w[:, 1:2], None, OP.mult), deps=[o9, o6, oa])
            mk2.did_write(ob)
            w12.did_read(ob)
            oc = P.emit("vector", lambda e, o=cwb.ap, a=mk1.ap, b=mk2.ap: e.tensor_tensor(o[:, 0:8], a[:, :], b[:, :], OP.add), deps=[oa, ob] + cwb.wr())
            cwb.did_write(oc)
            mk1.did_read(oc)
            mk2.did_read(oc)
            pt = C["psT1"]
            tr = tr128(P, C, cwb, 0, pt, 0, extra=[oc])
            ev = P.emit("vector", lambda e, o=cwT.ap, i=pt.ap, tt=tt: e.tensor_copy(o[0:8, tt * 128:(tt + 1) * 128], i[0:8, 0:128]), deps=[tr] + cwT.r)
            cwT.did_write(ev, accum=True)
            pt.did_read(ev)
        for tc in range(T // 512):
            rh = C["rhsP"][tc % 2]
            for c in range(NF):
                op = P.emit("sync", lambda e, o=rh[c].ap, i=xnT.ap, c=c, tc=tc:
                            e.dma_start(out=o[:, :], in_=i[c * 128:(c + 1) * 128, PAD + tc * 512: PAD + (tc + 1) * 512]),
                            deps=rh[c].wr() + xnT.rd(), dma=True)
                rh[c].did_write(op)
                xnT.did_read(op)
            p0 = C["ps0"]
            m1 = P.emit("tensor", lambda e, p=p0.ap, l=rep_sb.ap, r=cwT.ap, tc=tc:
                        e.matmul(p[:, :], l[0:8, :], r[0:8, tc * 512:(tc + 1) * 512], start=True, stop=True),
                        deps=p0.wr() + rep_sb.rd() + cwT.rd())
            rep_sb.did_read(m1)
            cwT.did_read(m1)
            p0.did_write(m1)
            cw = cwrep[tc % 2]
            e1 = P.emit("vector", lambda e, o=cw.ap, i=p0.ap: e.tensor_copy(o[:, :], i[:, :]), deps=[m1] + cw.wr())
            cw.did_write(e1)
            p0.did_read(e1)
            pS = C["psS1"]
            mms = []
            for c in range(NF):
                op = P.emit("tensor", lambda e, p=pS.ap, w=A_sb.ap, r=rh[c].ap, c=c, first=(c == 0), last=(c == NF - 1):
                            e.matmul(p[:, 0:512], w[:, c * 128:(c + 1) * 128], r[:, :], start=first, stop=last),
                            deps=(pS.wr() if c == 0 else []) + A_sb.rd() + rh[c].rd() + (mms[-1:] if mms else []))
                mms.append(op)
                A_sb.did_read(op)
                rh[c].did_read(op)
            pS.did_write(mms[-1])
            hw = hwT[tc % 2]
            e2 = P.emit("vector", lambda e, o=hw.ap, i=pS.ap, c=cw.ap: e.tensor_tensor(o[:, :], i[:, 0:512], c[:, :], OP.mult),
                        deps=[mms[-1], e1] + hw.wr())
            hw.did_write(e2)
            pS.did_read(e2)
            cw.did_read(e2)
            for sub in range(4):
                for dc in range(2):
                    p1 = C["ps1"]
                    m2 = P.emit("tensor", lambda e, p=p1.ap, l=hw.ap, w=B_sb.ap, sub=sub, dc=dc:
                                e.matmul(p[:, :], l[:, sub * 128:(sub + 1) * 128], w[:, dc * 512:(dc + 1) * 512], start=True, stop=True),
                                deps=p1.wr() + [e2] + B_sb.rd())
                    hw.did_read(m2)
                    B_sb.did_read(m2)
                    p1.did_write(m2)
                    eb = C["stC"][sub % 2]
                    e3 = P.emit("vector", lambda e, o=eb.ap, i=p1.ap: e.tensor_copy(o[:, 0:512], i[:, :]), deps=[m2] + eb.wr())
                    eb.did_write(e3)
                    p1.did_read(e3)
                    st = P.emit("scalar", lambda e, i=eb.ap, o=fout.ap, tc=tc, sub=sub, dc=dc:
                                e.dma_start(out=o[tc * 512 + sub * 128: tc * 512 + (sub + 1) * 128, dc * 512:(dc + 1) * 512], in_=i[:, 0:512]),
                                deps=[e3] + fout.r, dma=True)
                    eb.did_read(st)
                    fout.did_write(st, accum=True)
        emit_epilogue(P, C, x_in, aout, fout, y)
        final_waits(P)
        P.finalize(es)
    return nc


def prep_l1_inputs(ins, xb):
    ident, _ = _l0_consts()
    A = np.asarray(ins["L1_ffn_A"], np.float32)
    Bm = np.asarray(ins["L1_ffn_B"], np.float32)
    rep = np.zeros((8, 128), np.float32)
    for e in range(8):
        rep[e, e * 16:(e + 1) * 16] = 1.0
    return {
        "x": np.ascontiguousarray(xb, dtype=np.float32),
        "ident": ident,
        "wq": np.asarray(ins["L1_attn_Wq"], np.float32),
        "wk": np.asarray(ins["L1_attn_Wk"], np.float32),
        "wv": np.asarray(ins["L1_attn_Wv"], np.float32),
        "wp": np.asarray(ins["L1_attn_Wp"], np.float32),
        "bq": vec8(np.asarray(ins["L1_attn_bq"], np.float32)),
        "bkb": brow(np.asarray(ins["L1_attn_bk"], np.float32)),
        "bvb": brow(np.asarray(ins["L1_attn_bv"], np.float32)),
        "bpb": brow(np.asarray(ins["L1_attn_bp"], np.float32)),
        "wg": np.asarray(ins["L1_ffn_Wg"], np.float32),
        "Aall": np.ascontiguousarray(np.transpose(A, (1, 0, 2)).reshape(D, 128)),
        "Ball": np.ascontiguousarray(Bm.reshape(128, D)),
        "rep": rep,
    }


def emit_precast(P, C, src_d, dst_d, rows, cols):
    stg = C["stB"]
    nco = (cols + 1023) // 1024
    k = 0
    for r in range(rows // 128):
        for co in range(nco):
            cw = min(1024, cols - co * 1024)
            s = stg[k % 2]
            k += 1
            ld = P.emit("gpsimd", lambda e, o=s.ap, i=src_d.ap, r=r, co=co, cw=cw:
                        e.dma_start(out=o[:, 0:cw], in_=i[r * 128:(r + 1) * 128, co * 1024:co * 1024 + cw], accum_op=OP.bypass),
                        deps=s.wr() + src_d.rd(), dma=True)
            s.did_write(ld)
            src_d.did_read(ld)
            st = P.emit("scalar", lambda e, i=s.ap, o=dst_d.ap, r=r, co=co, cw=cw:
                        e.dma_start(out=o[r * 128:(r + 1) * 128, co * 1024:co * 1024 + cw], in_=i[:, 0:cw]),
                        deps=[ld] + dst_d.r, dma=True)
            s.did_read(st)
            dst_d.did_write(st, accum=True)


def emit_rownorm_scale(P, C, v_d, g_sb, out_sb, n_rows, n_cols, eps=1e-12):
    nco = n_cols // 1024
    for r in range(n_rows // 128):
        parts = []
        for co in range(nco):
            s = C["stB"][co % 2]
            q = C["stD"][co % 2]
            ld = P.emit("gpsimd", lambda e, o=s.ap, i=v_d.ap, r=r, co=co:
                        e.dma_start(out=o[:, :], in_=i[r * 128:(r + 1) * 128, co * 1024:(co + 1) * 1024], accum_op=OP.bypass),
                        deps=s.wr() + v_d.rd(), dma=True)
            s.did_write(ld)
            v_d.did_read(ld)
            o1 = P.emit("gpsimd", lambda e, o=q.ap, i=s.ap: e.tensor_tensor(o[:, :], i[:, :], i[:, :], OP.mult), deps=[ld] + q.wr())
            q.did_write(o1)
            s.did_read(o1)
            sm = C["sm"][r % 4]
            o2 = P.emit("vector", lambda e, o=sm.ap, i=q.ap, co=co: e.tensor_reduce(o[:, co:co + 1], i[:, :], axis=AX, op=OP.add), deps=[o1] + sm.r + (sm.w if co == 0 else [parts[-1]]))
            sm.did_write(o2, accum=(co > 0))
            q.did_read(o2)
            parts.append(o2)
        sm = C["sm"][r % 4]
        o3 = P.emit("vector", lambda e, o=out_sb.ap, i=sm.ap, r=r, nco=nco: e.reduce_sum(o[:, r:r + 1], i[:, 0:nco], axis=AX), deps=parts + out_sb.r)
        out_sb.did_write(o3, accum=True)
        sm.did_read(o3)
    o4 = P.emit("scalar", lambda e, o=out_sb.ap: e.activation(o[:, :], o[:, :], AF.Sqrt), deps=out_sb.wr() + out_sb.rd())
    out_sb.did_write(o4)
    o5 = P.emit("vector", lambda e, o=out_sb.ap, eps=eps: e.tensor_scalar(o[:, :], o[:, :], eps, None, OP.max), deps=[o4])
    out_sb.did_write(o5)
    o6 = P.emit("vector", lambda e, o=out_sb.ap: e.reciprocal(o[:, :], o[:, :]), deps=[o5])
    out_sb.did_write(o6)
    o7 = P.emit("vector", lambda e, o=out_sb.ap, g=g_sb.ap: e.tensor_tensor(o[:, :], o[:, :], g[:, :], OP.mult), deps=[o6] + g_sb.rd())
    g_sb.did_read(o7)
    out_sb.did_write(o7)


def emit_conv_taps(P, C, src_sb, dst_sb, w_sb, b_sb, wcol0, bcol0, ncols, ntaps, src_off, dst_off, np_tiles, src_ops):
    ops_out = []
    for p in range(np_tiles):
        s = src_sb[p]
        d = dst_sb[p]
        prev = None
        for k in range(ntaps):
            off = src_off - (ntaps - 1) + k
            if k == 0:
                op = P.emit("vector", lambda e, o=d.ap, i=s.ap, w=w_sb.ap, p=p, k=k, off=off, n=ncols, w0=wcol0, do=dst_off:
                            e.tensor_scalar(o[:, do:do + n], i[:, off:off + n], w[:, w0 + p * ntaps + k: w0 + p * ntaps + k + 1], None, OP.mult),
                            deps=src_ops + s.rd() + w_sb.rd() + d.wr())
            else:
                op = P.emit("vector", lambda e, o=d.ap, i=s.ap, w=w_sb.ap, p=p, k=k, off=off, n=ncols, w0=wcol0, do=dst_off:
                            e.scalar_tensor_tensor(o[:, do:do + n], i[:, off:off + n], w[:, w0 + p * ntaps + k: w0 + p * ntaps + k + 1], o[:, do:do + n], OP.mult, OP.add),
                            deps=[prev] + s.rd() + w_sb.rd())
            prev = op
            s.did_read(op)
            w_sb.did_read(op)
            d.did_write(op)
        op = P.emit("vector", lambda e, o=d.ap, b=b_sb.ap, n=ncols, bp=bcol0 + p, do=dst_off:
                    e.tensor_scalar(o[:, do:do + n], o[:, do:do + n], b[:, bp:bp + 1], None, OP.add),
                    deps=[prev] + b_sb.rd())
        b_sb.did_read(op)
        d.did_write(op)
        ops_out.append(op)
    return ops_out




def build_layer3():
    nc = bass.Bass("TRN2", target_bir_lowering=False)
    di = lambda n, s: Buf(nc.dram_tensor(n, s, F32, kind="ExternalInput").ap())
    x_in = di("x", [T, D])
    ident_d = di("ident", [128, 128])
    win_d = di("win", [D, 3 * D])
    binv_d = di("binv", [128, 24])
    shw_d = di("shw", [128, 72])
    shb_d = di("shb", [128, 24])
    inw_d = di("inw", [128, 32])
    inb_d = di("inb", [128, 8])
    wout_d = di("wout", [D, D])
    boutb_d = di("boutb", [128, D])
    gv_d = di("gv", [8, D])
    gg_d = di("gg", [8, 1])
    gbb_d = di("gbb", [128, 8])
    sel_d = di("sel", [8, 1024])
    db16 = lambda n, s: Buf(nc.dram_tensor(n, s, BF16, kind="ExternalInput").ap())
    w1T_d = db16("w1T", [8 * D, 4096])
    w1g_d = di("w1g", [128, 256])
    b1_d = di("b1", [128, 256])
    w2T_d = db16("w2T", [8 * 4096, D])
    w2g_d = di("w2g", [128, 64])
    b2r_d = di("b2r", [8, D])
    oner_d = di("oner", [128, 1])
    y = Buf(nc.dram_tensor("y", [T, D], F32, kind="ExternalOutput").ap())

    xnT = dbuf(nc, "xnT", [D, PAD + T], BF16)
    aout = dbuf(nc, "aout", [T, D], BF16)
    ucq_d = dbuf(nc, "ucq", [D, 8 + T], BF16)
    uck_d = dbuf(nc, "uck", [D, 8 + T], BF16)
    ucv_d = dbuf(nc, "ucv", [D, 8 + T], BF16)
    nrm1_d = dbuf(nc, "nrm1", [8, 4096], F32)
    nrm2_d = dbuf(nc, "nrm2", [8, D], F32)
    yT_d = dbuf(nc, "yT", [D, T], F32)
    import os as _os
    dbg = bool(_os.environ.get("L3_DEBUG"))
    if dbg:
        dbg_a = Buf(nc.dram_tensor("dbg_a", [T, D], BF16, kind="ExternalOutput").ap())
        dbg_y = Buf(nc.dram_tensor("dbg_y", [D, T], F32, kind="ExternalOutput").ap())
        dbg_uq = Buf(nc.dram_tensor("dbg_uq", [D, 8 + T], BF16, kind="ExternalOutput").ap())
        dbg_xn = Buf(nc.dram_tensor("dbg_xn", [D, PAD + T], BF16, kind="ExternalOutput").ap())
    CH = 512

    es = ExitStack()
    with es:
        P = Prog(nc)
        C = alloc_common(nc, es, proj_pools=False)
        load_ident(P, C, ident_d.ap)
        emit_zero_pad(P, C, [(xnT, D), (ucq_d, D), (uck_d, D), (ucv_d, D)])
        emit_norm(P, C, x_in, xnT, None, kind="ln")
        xnf = [sbuf(nc, es, f"xnf{f}", [128, PAD + T], BF16) for f in range(NF)]
        for f in range(NF):
            op = P.emit("sync", lambda e, o=xnf[f].ap, i=xnT.ap, f=f: e.dma_start(out=o[:, :], in_=i[f * 128:(f + 1) * 128, :]), deps=xnf[f].wr() + xnT.rd(), dma=True)
            xnf[f].did_write(op)
            xnT.did_read(op)

        esA = ExitStack()
        with esA:
            W = alloc_wpool(nc, esA, 2)
            binv = load_f32(P, nc, esA, "binv_s", binv_d.ap, [128, 24])
            shw = load_f32(P, nc, esA, "shw_s", shw_d.ap, [128, 72])
            shb = load_f32(P, nc, esA, "shb_s", shb_d.ap, [128, 24])
            inw = load_f32(P, nc, esA, "inw_s", inw_d.ap, [128, 32])
            inb = load_f32(P, nc, esA, "inb_s", inb_d.ap, [128, 8])
            boutb = load_f32(P, nc, esA, "boutb_s", boutb_d.ap, [128, D])
            x1g = [sbuf(nc, esA, f"x1g{p}", [128, 528], BF16) for p in range(NF)]
            ucg = [sbuf(nc, esA, f"ucg{p}", [128, 512], BF16) for p in range(NF)]

            def load_w_group(wbuf, grp):
                deps0 = wbuf.wr()
                ops = []
                for c in range(NF):
                    op = P.emit("gpsimd",
                                lambda e, o=wbuf.ap, i=win_d.ap, c=c, grp=grp:
                                e.dma_start(out=o[:, c * D:(c + 1) * D], in_=i[c * 128:(c + 1) * 128, grp * D:(grp + 1) * D], accum_op=OP.bypass),
                                deps=deps0, dma=True)
                    ops.append(op)
                wbuf.w = ops
                wbuf.r = []
                wbuf.open = False

            for gi, (grp, dst) in enumerate(((0, ucq_d), (1, uck_d), (2, ucv_d))):
                wb = W[gi % 2]
                load_w_group(wb, grp)
                for ch in range(T // CH):
                    pops = []
                    for eo in range(NF):
                        pps = C["ps0"] if eo % 2 == 0 else C["ps1"]
                        for part, (c0, cw) in enumerate(((0, 512), (512, 16))):
                            mms = []
                            for c in range(NF):
                                op = P.emit("tensor",
                                            lambda e, p=pps.ap, w=wb.ap, r=xnf[c].ap, c=c, eo=eo, ch=ch, c0=c0, cw=cw, first=(c == 0), last=(c == NF - 1):
                                            e.matmul(p[:, 0:cw], w[:, c * D + eo * 128: c * D + eo * 128 + 128],
                                                     r[:, PAD + ch * CH - 16 + c0: PAD + ch * CH - 16 + c0 + cw], start=first, stop=last),
                                            deps=(pps.wr() if c == 0 else []) + wb.rd() + xnf[c].rd() + (mms[-1:] if mms else []))
                                mms.append(op)
                                wb.did_read(op)
                                xnf[c].did_read(op)
                            pps.did_write(mms[-1])
                            ev = P.emit("vector", lambda e, o=x1g[eo].ap, i=pps.ap, b=binv.ap, grp=grp, eo=eo, c0=c0, cw=cw:
                                        e.tensor_scalar(o[:, c0:c0 + cw], i[:, 0:cw], b[:, grp * 8 + eo: grp * 8 + eo + 1], None, OP.add),
                                        deps=[mms[-1]] + binv.rd() + (x1g[eo].wr() if part == 0 else []))
                            binv.did_read(ev)
                            x1g[eo].did_write(ev, accum=True)
                            pps.did_read(ev)
                            pops.append(ev)
                    cops = emit_conv_taps(P, C, x1g, ucg, shw, shb, grp * 24, grp * 8, 512, 3, 16, 0, NF, pops)
                    for p in range(NF):
                        st = P.emit("scalar", lambda e, i=ucg[p].ap, o=dst.ap, p=p, ch=ch:
                                    e.dma_start(out=o[p * 128:(p + 1) * 128, 8 + ch * CH: 8 + ch * CH + 512], in_=i[:, :]),
                                    deps=[cops[p]] + dst.r, dma=True)
                        ucg[p].did_read(st)
                        dst.did_write(st, accum=True)
            load_w_into(P, W[0], wout_d.ap, D, D)
            ka = [sbuf(nc, esA, f"ka{p}", [128, 515], BF16) for p in range(NF)]
            vv2 = [sbuf(nc, esA, f"vv2{p}", [128, 512], BF16) for p in range(NF)]
            yb = [sbuf(nc, esA, f"yb{p}", [128, 512], BF16) for p in range(NF)]
            qb = [sbuf(nc, esA, f"qb{i}", [128, 512], BF16) for i in range(2)]
            for ch in range(T // CH):
                for p in range(NF):
                    a = ka[p]
                    l1 = P.emit("sync", lambda e, o=a.ap, i=ucv_d.ap, p=p, ch=ch:
                                e.dma_start(out=o[:, :], in_=i[p * 128:(p + 1) * 128, 5 + ch * CH: 5 + ch * CH + 515]),
                                deps=a.wr() + ucv_d.rd(), dma=True)
                    a.did_write(l1)
                    ucv_d.did_read(l1)
                    d2 = C["stB"][p % 2]
                    l2 = P.emit("sync", lambda e, o=d2.ap, i=uck_d.ap, p=p, ch=ch:
                                e.dma_start(out=o[:, 0:515], in_=i[p * 128:(p + 1) * 128, 5 + ch * CH: 5 + ch * CH + 515]),
                                deps=d2.wr() + uck_d.rd(), dma=True)
                    d2.did_write(l2)
                    uck_d.did_read(l2)
                    o1 = P.emit("vector", lambda e, o=a.ap, i=d2.ap: e.tensor_tensor(o[:, :], o[:, :], i[:, 0:515], OP.mult), deps=[l1, l2])
                    a.did_write(o1)
                    d2.did_read(o1)
                    cops = emit_conv_taps(P, C, [a], [vv2[p]], inw, inb, p * 4, p, 512, 4, 3, 0, 1, [o1])
                    q = qb[p % 2]
                    l3 = P.emit("sync", lambda e, o=q.ap, i=ucq_d.ap, p=p, ch=ch:
                                e.dma_start(out=o[:, :], in_=i[p * 128:(p + 1) * 128, 8 + ch * CH: 8 + ch * CH + 512]),
                                deps=q.wr() + ucq_d.rd(), dma=True)
                    q.did_write(l3)
                    ucq_d.did_read(l3)
                    o2 = P.emit("vector", lambda e, o=yb[p].ap, i=q.ap, v=vv2[p].ap: e.tensor_tensor(o[:, :], i[:, :], v[:, :], OP.mult),
                                deps=[l3, cops[0]] + yb[p].wr())
                    yb[p].did_write(o2)
                    q.did_read(o2)
                    vv2[p].did_read(o2)
                for tsub in range(4):
                    for dc in range(2):
                        pps = C["ps0"] if dc == 0 else C["ps1"]
                        mms = []
                        for p in range(NF):
                            op = P.emit("tensor",
                                        lambda e, ps_=pps.ap, l=yb[p].ap, w=W[0].ap, p=p, tsub=tsub, dc=dc, first=(p == 0), last=(p == NF - 1):
                                        e.matmul(ps_[:, :], l[:, tsub * 128:(tsub + 1) * 128], w[:, p * D + dc * 512: p * D + (dc + 1) * 512], start=first, stop=last),
                                        deps=(pps.wr() if p == 0 else []) + yb[p].rd() + W[0].rd() + (mms[-1:] if mms else []))
                            mms.append(op)
                            yb[p].did_read(op)
                            W[0].did_read(op)
                        pps.did_write(mms[-1])
                        eb = C["stC"][dc % 2]
                        o1 = P.emit("vector", lambda e, o=eb.ap, i=pps.ap, b=boutb.ap, dc=dc:
                                    e.tensor_tensor(o[:, 0:512], i[:, :], b[:, dc * 512:(dc + 1) * 512], OP.add),
                                    deps=[mms[-1]] + eb.wr() + boutb.rd())
                        boutb.did_read(o1)
                        eb.did_write(o1)
                        pps.did_read(o1)
                        st = P.emit("scalar", lambda e, i=eb.ap, o=aout.ap, ch=ch, tsub=tsub, dc=dc:
                                    e.dma_start(out=o[ch * CH + tsub * 128: ch * CH + (tsub + 1) * 128, dc * 512:(dc + 1) * 512], in_=i[:, 0:512]),
                                    deps=[o1] + aout.r, dma=True)
                        eb.did_read(st)
                        aout.did_write(st, accum=True)
            fA = P.fence()

        esB = ExitStack()
        with esB:
            sb2 = lambda n, s, d: sbuf(nc, esB, n, s, d, init=fA)
            gvf = sb2("gvf", [8, D], F32)
            op = P.emit("sync", lambda e, o=gvf.ap, i=gv_d.ap: e.dma_start(out=o[:, :], in_=i), deps=gvf.wr(), dma=True)
            gvf.did_write(op)
            ggs = sb2("ggs", [8, 1], F32)
            op = P.emit("sync", lambda e, o=ggs.ap, i=gg_d.ap: e.dma_start(out=o[:, :], in_=i), deps=ggs.wr(), dma=True)
            ggs.did_write(op)
            gbb = load_f32(P, nc, esB, "gbb_s", gbb_d.ap, [128, 8])
            gbb.w += fA
            gs = sb2("gs", [8, 8], F32)
            q1 = C["stD"][0]
            o1 = P.emit("vector", lambda e, o=q1.ap, i=gvf.ap: e.tensor_tensor(o[0:8, :], i[:, :], i[:, :], OP.mult), deps=gvf.rd() + q1.wr() + fA)
            q1.did_write(o1)
            gvf.did_read(o1)
            o2 = P.emit("vector", lambda e, o=gs.ap, i=q1.ap: e.reduce_sum(o[0:8, 0:1], i[0:8, :], axis=AX), deps=[o1] + gs.wr())
            q1.did_read(o2)
            o3 = P.emit("scalar", lambda e, o=gs.ap: e.activation(o[0:8, 1:2], o[0:8, 0:1], AF.Sqrt), deps=[o2])
            o4 = P.emit("vector", lambda e, o=gs.ap: e.tensor_scalar(o[0:8, 2:3], o[0:8, 1:2], 1e-12, None, OP.max), deps=[o3])
            o5 = P.emit("vector", lambda e, o=gs.ap: e.reciprocal(o[0:8, 3:4], o[0:8, 2:3]), deps=[o4])
            o6 = P.emit("vector", lambda e, o=gs.ap, g=ggs.ap: e.tensor_tensor(o[0:8, 4:5], o[0:8, 3:4], g[0:8, 0:1], OP.mult), deps=[o5] + ggs.rd())
            ggs.did_read(o6)
            gs.did_write(o6)
            gwn = sb2("gwn", [128, D], BF16)
            oz = P.emit("gpsimd", lambda e, o=gwn.ap: e.memset(o[:, :], 0.0), deps=gwn.wr() + fA)
            gwn.did_write(oz)
            o7 = P.emit("vector", lambda e, o=gwn.ap, i=gvf.ap, s=gs.ap: e.tensor_scalar(o[0:8, :], i[:, :], s[0:8, 4:5], None, OP.mult), deps=[oz, o6] + gvf.rd())
            gwn.did_write(o7, accum=False)
            gwn.w.append(oz)
            gvf.did_read(o7)
            gs.did_read(o7)
            gwT = sb2("gwT", [128, 64], BF16)
            for f in range(NF):
                pt = C["psT1"]
                tr = tr128(P, C, gwn, f * 128, pt, 0)
                ev = P.emit("vector", lambda e, o=gwT.ap, i=pt.ap, f=f: e.tensor_copy(o[:, f * 8:(f + 1) * 8], i[:, 0:8]), deps=[tr] + gwT.r)
                gwT.did_write(ev, accum=True)
                pt.did_read(ev)
            lg = sb2("lg3", [128, 8], F32)
            m8 = sb2("m83", [128, 8], F32)
            mtok = sb2("mtok", [128, 128], BF16)
            maskT = sb2("maskT", [8, T], BF16)
            for tt in range(NT):
                p = C["ps0"]
                mms = []
                for c in range(NF):
                    op = P.emit("tensor", lambda e, p_=p.ap, l=xnf[c].ap, w=gwT.ap, c=c, tt=tt, first=(c == 0), last=(c == NF - 1):
                                e.matmul(p_[:, 0:8], l[:, PAD + tt * 128: PAD + (tt + 1) * 128], w[:, c * 8:(c + 1) * 8], start=first, stop=last),
                                deps=(p.wr() if c == 0 else []) + xnf[c].rd() + gwT.rd() + (mms[-1:] if mms else []))
                    mms.append(op)
                    xnf[c].did_read(op)
                    gwT.did_read(op)
                p.did_write(mms[-1])
                o1 = P.emit("vector", lambda e, o=lg.ap, i=p.ap, b=gbb.ap: e.tensor_tensor(o[:, :], i[:, 0:8], b[:, :], OP.add), deps=[mms[-1]] + lg.wr() + gbb.rd())
                gbb.did_read(o1)
                lg.did_write(o1)
                p.did_read(o1)
                o2 = P.emit("vector", lambda e, o=m8.ap, i=lg.ap: e.max(out=o[:, :], in_=i[:, :]), deps=[o1] + m8.wr())
                m8.did_write(o2)
                lg.did_read(o2)
                o3 = P.emit("vector", lambda e, o=mtok.ap, i=lg.ap, m=m8.ap: e.tensor_scalar(o[:, 0:8], i[:, :], m[:, 1:2], None, OP.is_ge), deps=[o2] + mtok.wr())
                mtok.did_write(o3)
                lg.did_read(o3)
                m8.did_read(o3)
                pt = C["psT2"]
                tr = tr128(P, C, mtok, 0, pt, 0, extra=[o3])
                ev = P.emit("vector", lambda e, o=maskT.ap, i=pt.ap, tt=tt: e.tensor_copy(o[0:8, tt * 128:(tt + 1) * 128], i[0:8, 0:128]), deps=[tr] + maskT.r)
                maskT.did_write(ev, accum=True)
                pt.did_read(ev)
            b2s = sb2("b2s", [8, D], BF16)
            op = P.emit("gpsimd", lambda e, o=b2s.ap, i=b2r_d.ap: e.dma_start(out=o[:, :], in_=i, accum_op=OP.bypass), deps=b2s.wr() + fA, dma=True)
            b2s.did_write(op)
            for do in range(NF):
                for tc in range(T // 512):
                    p = C["psS2"]
                    mm = P.emit("tensor", lambda e, p_=p.ap, l=b2s.ap, r=maskT.ap, do=do, tc=tc:
                                e.matmul(p_[:, 0:512], l[0:8, do * 128:(do + 1) * 128], r[0:8, tc * 512:(tc + 1) * 512], start=True, stop=True),
                                deps=p.wr() + b2s.rd() + maskT.rd())
                    b2s.did_read(mm)
                    maskT.did_read(mm)
                    p.did_write(mm)
                    eb = C["stE"][tc % 2]
                    ev = P.emit("vector", lambda e, o=eb.ap, i=p.ap: e.tensor_copy(o[:, 0:512], i[:, 0:512]), deps=[mm] + eb.wr())
                    eb.did_write(ev)
                    p.did_read(ev)
                    st = P.emit("scalar", lambda e, i=eb.ap, o=yT_d.ap, do=do, tc=tc:
                                e.dma_start(out=o[do * 128:(do + 1) * 128, tc * 512:(tc + 1) * 512], in_=i[:, 0:512]),
                                deps=[ev] + yT_d.r, dma=True)
                    eb.did_read(st)
                    yT_d.did_write(st, accum=True)
            w1g = load_f32(P, nc, esB, "w1g_s", w1g_d.ap, [128, 256])
            w1g.w += fA
            b1s = load_f32(P, nc, esB, "b1_s", b1_d.ap, [128, 256])
            b1s.w += fA
            w2g = load_f32(P, nc, esB, "w2g_s", w2g_d.ap, [128, 64])
            w2g.w += fA
            s1 = sb2("s1", [128, 256], F32)
            s2 = sb2("s2", [128, 64], F32)
            oner = load_bf16(P, nc, esB, "oner_s", oner_d.ap, [128, 1])
            oner.w += fA
            esB1 = ExitStack()
            esB1.__enter__()
            import os as _os
            if _os.environ.get("L3_SKIP_NORMS"):
                op1 = P.emit("gpsimd", lambda e, o=s1.ap: e.memset(o[:, :], 1.0), deps=s1.wr() + fA)
                s1.did_write(op1)
                op2 = P.emit("gpsimd", lambda e, o=s2.ap: e.memset(o[:, :], 1.0), deps=s2.wr() + fA)
                s2.did_write(op2)
            else:
                emit_norms_from_T(P, C, nc, esB1, w1T_d, w1g, s1, oner, nrm1_d, D, 4096, 8, init=fA)
                emit_norms_from_T(P, C, nc, esB1, w2T_d, w2g, s2, oner, nrm2_d, 4096, D, 8, init=fA)
            esB1.__exit__(None, None, None)
            fB1 = P.fence()
            sb3 = lambda n, s, d: sbuf(nc, esB, n, s, d, init=fB1)
            sel_s = load_bf16(P, nc, esB, "sel_s", sel_d.ap, [8, 1024])
            sel_s.w += fB1
            sT = [sb3(f"sT{i}", [128, 1024], BF16) for i in range(32)]
            w1t = sb3("w1t", [128, 1024], BF16)
            w2t = sb3("w2t", [128, 4096], BF16)
            mkb = sb3("mkb", [128, 1024], BF16)
            ypv = [sb3(f"ypv{i}", [128, 512], F32) for i in range(2)]
            TB = 1024
            for tb in range(T // TB):
                for e_ in range(8):
                    for ho in range(32):
                        lw = P.emit("sync", lambda e, o=w1t.ap, i=w1T_d.ap, e_=e_, ho=ho:
                                    e.dma_start(out=o[:, :].rearrange("p (c w) -> p c w", w=128),
                                                in_=i[e_ * 1024:(e_ + 1) * 1024, ho * 128:(ho + 1) * 128].rearrange("(c p) w -> p c w", p=128)),
                                    deps=w1t.wr() + w1T_d.rd(), dma=True)
                        w1t.did_write(lw)
                        w1T_d.did_read(lw)
                        for tc2 in range(2):
                            p = [C["ps0"], C["ps1"]][tc2]
                            mms = []
                            for c in range(NF):
                                op = P.emit("tensor",
                                            lambda e, p_=p.ap, l=w1t.ap, r=xnf[c].ap, c=c, tb=tb, tc2=tc2, first=(c == 0), last=(c == NF - 1):
                                            e.matmul(p_[:, :], l[:, c * 128:(c + 1) * 128],
                                                     r[:, PAD + tb * TB + tc2 * 512: PAD + tb * TB + (tc2 + 1) * 512], start=first, stop=last),
                                            deps=(p.wr() if c == 0 else []) + [lw] + xnf[c].rd() + (mms[-1:] if mms else []))
                                mms.append(op)
                                w1t.did_read(op)
                                xnf[c].did_read(op)
                            p.did_write(mms[-1])
                            ev = P.emit("scalar", lambda e, o=sT[ho].ap, i=p.ap, s=s1.ap, b=b1s.ap, e_=e_, ho=ho, tc2=tc2:
                                        e.activation(o[:, tc2 * 512:(tc2 + 1) * 512], i[:, :], AF.Silu,
                                                     bias=b[:, e_ * 32 + ho: e_ * 32 + ho + 1], scale=s[:, e_ * 32 + ho: e_ * 32 + ho + 1]),
                                        deps=[mms[-1]] + sT[ho].wr() + s1.rd() + b1s.rd())
                            sT[ho].did_write(ev, accum=True)
                            s1.did_read(ev)
                            b1s.did_read(ev)
                            p.did_read(ev)
                    for tc2 in range(2):
                        p = C["psS2"]
                        mm = P.emit("tensor", lambda e, p_=p.ap, l=sel_s.ap, r=maskT.ap, e_=e_, tb=tb, tc2=tc2:
                                    e.matmul(p_[:, 0:512], l[0:8, e_ * 128:(e_ + 1) * 128], r[0:8, tb * TB + tc2 * 512: tb * TB + (tc2 + 1) * 512], start=True, stop=True),
                                    deps=p.wr() + sel_s.rd() + maskT.rd())
                        sel_s.did_read(mm)
                        maskT.did_read(mm)
                        p.did_write(mm)
                        ev = P.emit("vector", lambda e, o=mkb.ap, i=p.ap, tc2=tc2: e.tensor_copy(o[:, tc2 * 512:(tc2 + 1) * 512], i[:, 0:512]),
                                    deps=[mm] + (mkb.wr() if tc2 == 0 else []))
                        mkb.did_write(ev, accum=True)
                        p.did_read(ev)
                    for do in range(NF):
                        lw = P.emit("sync", lambda e, o=w2t.ap, i=w2T_d.ap, e_=e_, do=do:
                                    e.dma_start(out=o[:, :].rearrange("p (c w) -> p c w", w=128),
                                                in_=i[e_ * 4096:(e_ + 1) * 4096, do * 128:(do + 1) * 128].rearrange("(c p) w -> p c w", p=128)),
                                    deps=w2t.wr() + w2T_d.rd(), dma=True)
                        w2t.did_write(lw)
                        w2T_d.did_read(lw)
                        for tc2 in range(2):
                            p = C["psS1"]
                            mms = []
                            for ho in range(32):
                                op = P.emit("tensor",
                                            lambda e, p_=p.ap, l=w2t.ap, r=sT[ho].ap, ho=ho, tc2=tc2, first=(ho == 0), last=(ho == 31):
                                            e.matmul(p_[:, 0:512], l[:, ho * 128:(ho + 1) * 128], r[:, tc2 * 512:(tc2 + 1) * 512], start=first, stop=last),
                                            deps=(p.wr() if ho == 0 else []) + [lw] + sT[ho].rd() + (mms[-1:] if mms else []))
                                mms.append(op)
                                w2t.did_read(op)
                                sT[ho].did_read(op)
                            p.did_write(mms[-1])
                            yp = ypv[tc2]
                            lp = P.emit("sync", lambda e, o=yp.ap, i=yT_d.ap, do=do, tb=tb, tc2=tc2:
                                        e.dma_start(out=o[:, :], in_=i[do * 128:(do + 1) * 128, tb * TB + tc2 * 512: tb * TB + (tc2 + 1) * 512]),
                                        deps=yp.wr() + yT_d.rd(), dma=True)
                            yp.did_write(lp)
                            yT_d.did_read(lp)
                            eb = C["stE"][tc2]
                            o1 = P.emit("vector", lambda e, o=eb.ap, i=p.ap, s=s2.ap, m=mkb.ap, e_=e_, do=do, tc2=tc2:
                                        e.scalar_tensor_tensor(o[:, 0:512], i[:, 0:512], s[:, e_ * 8 + do: e_ * 8 + do + 1], m[:, tc2 * 512:(tc2 + 1) * 512], OP.mult, OP.mult),
                                        deps=[mms[-1]] + eb.wr() + s2.rd() + mkb.rd())
                            eb.did_write(o1)
                            p.did_read(o1)
                            s2.did_read(o1)
                            mkb.did_read(o1)
                            o2 = P.emit("vector", lambda e, o=eb.ap, i=yp.ap: e.tensor_tensor(o[:, 0:512], o[:, 0:512], i[:, :], OP.add), deps=[o1, lp])
                            eb.did_write(o2)
                            yp.did_read(o2)
                            st = P.emit("scalar", lambda e, i=eb.ap, o=yT_d.ap, do=do, tb=tb, tc2=tc2:
                                        e.dma_start(out=o[do * 128:(do + 1) * 128, tb * TB + tc2 * 512: tb * TB + (tc2 + 1) * 512], in_=i[:, 0:512]),
                                        deps=[o2] + yT_d.r + yT_d.w, dma=True)
                            eb.did_read(st)
                            yT_d.w.append(st)
            fB = P.fence()

        for t in range(NT):
            a = C["stA"][t % 2]
            u = C["stB"][t % 2]
            ft = C["stD"][t % 2]
            res = C["stE"][t % 2]
            l1 = P.emit("sync", lambda e, o=a.ap, i=x_in.ap, t=t: e.dma_start(out=o[:, :], in_=i[t * 128:(t + 1) * 128, :]), deps=a.wr() + x_in.rd() + fB, dma=True)
            a.did_write(l1)
            x_in.did_read(l1)
            l2 = P.emit("sync", lambda e, o=u.ap, i=aout.ap, t=t: e.dma_start(out=o[:, :], in_=i[t * 128:(t + 1) * 128, :]), deps=u.wr() + aout.rd() + fB, dma=True)
            u.did_write(l2)
            aout.did_read(l2)
            bstage = C["stC"][t % 2]
            pt = [C["psT1"], C["psT2"]][t % 2]
            trs = []
            for f in range(NF):
                l3 = P.emit("sync", lambda e, o=C["stD"][(t + 1) % 2].ap, i=yT_d.ap, f=f, t=t:
                            e.dma_start(out=o[:, 0:128], in_=i[f * 128:(f + 1) * 128, t * 128:(t + 1) * 128]),
                            deps=C["stD"][(t + 1) % 2].wr() + yT_d.rd() + fB, dma=True)
                C["stD"][(t + 1) % 2].did_write(l3)
                yT_d.did_read(l3)
                cv = P.emit("vector", lambda e, o=bstage.ap, i=C["stD"][(t + 1) % 2].ap, f=f: e.tensor_copy(o[:, f * 128:(f + 1) * 128], i[:, 0:128]),
                            deps=[l3] + (bstage.wr() if f == 0 else []))
                bstage.did_write(cv, accum=True)
                C["stD"][(t + 1) % 2].did_read(cv)
                trs.append(tr128(P, C, bstage, f * 128, pt, f * 128, extra=[cv]))
            ev = P.emit("vector", lambda e, o=ft.ap, i=pt.ap: e.tensor_copy(o[:, :], i[:, :]), deps=[trs[-1]] + ft.wr())
            ft.did_write(ev)
            pt.did_read(ev)
            o1 = P.emit("vector", lambda e, o=res.ap, i0=a.ap, i1=u.ap: e.tensor_tensor(o[:, :], i0[:, :], i1[:, :], OP.add), deps=[l1, l2] + res.wr())
            res.did_write(o1)
            a.did_read(o1)
            u.did_read(o1)
            o2 = P.emit("vector", lambda e, o=res.ap, i1=ft.ap: e.tensor_tensor(o[:, :], o[:, :], i1[:, :], OP.add), deps=[o1, ev])
            res.did_write(o2)
            ft.did_read(o2)
            st = P.emit("scalar", lambda e, i=res.ap, o=y.ap, t=t: e.dma_start(out=o[t * 128:(t + 1) * 128, :], in_=i[:, :]), deps=[o2] + y.r, dma=True)
            res.did_read(st)
            y.did_write(st, accum=True)
        if dbg:
            bd = P.barrier_deps()
            for (dsrc, ddst) in ((aout, dbg_a), (yT_d, dbg_y), (ucq_d, dbg_uq), (xnT, dbg_xn)):
                op = P.emit("gpsimd", lambda e, o=ddst.ap, i=dsrc.ap: e.dma_start(out=o, in_=i), deps=bd, dma=True)
        final_waits(P)
        P.finalize(es)
    return nc


def prep_l3_inputs(ins, xb):
    ident, _ = _l0_consts()
    shw = np.asarray(ins["L3_attn_short_w"], np.float32)
    inw = np.asarray(ins["L3_attn_inner_w"], np.float32)
    w1v = np.asarray(ins["L3_ffn_w1_v"], np.float32)
    w2v = np.asarray(ins["L3_ffn_w2_v"], np.float32)
    sel = np.zeros((8, 1024), np.float32)
    for e in range(8):
        sel[e, e * 128:(e + 1) * 128] = 1.0
    return {
        "x": np.ascontiguousarray(xb, dtype=np.float32),
        "ident": ident,
        "win": np.asarray(ins["L3_attn_W_in"], np.float32),
        "binv": vec8(np.asarray(ins["L3_attn_b_in"], np.float32)),
        "shw": np.ascontiguousarray(shw.reshape(24, 128, 3).transpose(1, 0, 2).reshape(128, 72)),
        "shb": vec8(np.asarray(ins["L3_attn_short_b"], np.float32)),
        "inw": np.ascontiguousarray(inw.reshape(8, 128, 4).transpose(1, 0, 2).reshape(128, 32)),
        "inb": vec8(np.asarray(ins["L3_attn_inner_b"], np.float32)),
        "wout": np.asarray(ins["L3_attn_W_out"], np.float32),
        "boutb": brow(np.asarray(ins["L3_attn_b_out"], np.float32)),
        "gv": np.asarray(ins["L3_ffn_gate_v"], np.float32),
        "gg": np.asarray(ins["L3_ffn_gate_g"], np.float32).reshape(8, 1),
        "gbb": brow(np.asarray(ins["L3_ffn_gate_b"], np.float32)),
        "sel": sel,
        "w1T": np.ascontiguousarray(np.transpose(w1v, (0, 2, 1)).reshape(8 * 1024, 4096)).astype(_bf16()),
        "w1g": np.concatenate([vec8(np.asarray(ins["L3_ffn_w1_g"], np.float32)[e]) for e in range(8)], axis=1),
        "b1": np.concatenate([vec8(np.asarray(ins["L3_ffn_b1"], np.float32)[e]) for e in range(8)], axis=1),
        "w2T": np.ascontiguousarray(np.transpose(w2v, (0, 2, 1)).reshape(8 * 4096, 1024)).astype(_bf16()),
        "oner": np.ones((128, 1), np.float32),
        "w2g": np.concatenate([vec8(np.asarray(ins["L3_ffn_w2_g"], np.float32)[e]) for e in range(8)], axis=1),
        "b2r": np.asarray(ins["L3_ffn_b2"], np.float32),
    }


def make_store_cb(P, C, dst):
    def cb(tt, dc, p, mm):
        eb = C["stC"][dc % 2]
        o1 = P.emit("vector", lambda e, o=eb.ap, i=p.ap: e.tensor_copy(o[:, 0:512], i[:, :]), deps=[mm] + eb.wr())
        eb.did_write(o1)
        p.did_read(o1)
        st = P.emit("scalar", lambda e, i=eb.ap, o=dst.ap, tt=tt, dc=dc:
                    e.dma_start(out=o[tt * 128:(tt + 1) * 128, dc * 512:(dc + 1) * 512], in_=i[:, 0:512]),
                    deps=[o1] + dst.r, dma=True)
        eb.did_read(st)
        dst.did_write(st, accum=True)
    return cb


def emit_norms_from_T(P, C, nc, es, wT_d, g_sb, out_sb, onerow, nrm_d, n_in, n_out, n_e, eps=1e-12, init=()):
    stg = C["stB"]
    sq = C["stC"]
    row = sbuf(nc, es, f"nrmrow{id(wT_d) % 99999}", [1, n_out], F32, init=list(init))
    ncc = n_in // 128
    for e_ in range(n_e):
        for co in range(n_out // 512):
            mm = None
            for c in range(ncc):
                s = stg[c % 2]
                q = sq[c % 2]
                ld = P.emit("sync", lambda e, o=s.ap, i=wT_d.ap, e_=e_, c=c, co=co, ni=n_in:
                            e.dma_start(out=o[:, 0:512], in_=i[e_ * ni + c * 128: e_ * ni + (c + 1) * 128, co * 512:(co + 1) * 512]),
                            deps=s.wr() + wT_d.rd(), dma=True)
                s.did_write(ld)
                wT_d.did_read(ld)
                o1 = P.emit("scalar", lambda e, o=q.ap, i=s.ap: e.activation(o[:, 0:512], i[:, 0:512], AF.Square), deps=[ld] + q.wr())
                q.did_write(o1)
                s.did_read(o1)
                first, last = (c == 0), (c == ncc - 1)
                mm = P.emit("tensor", lambda e, p=C["ps0"].ap, l=onerow.ap, r=q.ap, first=first, last=last:
                            e.matmul(p[0:1, 0:512], l[:, 0:1], r[:, 0:512], start=first, stop=last),
                            deps=(C["ps0"].wr() if first else [mm]) + [o1] + onerow.rd())
                onerow.did_read(mm)
                q.did_read(mm)
                C["ps0"].did_write(mm, accum=not first)
            ev = P.emit("vector", lambda e, o=row.ap, i=C["ps0"].ap, co=co: e.tensor_copy(o[0:1, co * 512:(co + 1) * 512], i[0:1, 0:512]),
                        deps=[mm] + (row.wr() if co == 0 else []))
            row.did_write(ev, accum=True)
            C["ps0"].did_read(ev)
        st = P.emit("scalar", lambda e, i=row.ap, o=nrm_d.ap, e_=e_, no=n_out:
                    e.dma_start(out=o[e_:e_ + 1, :], in_=i[0:1, 0:no]), deps=row.rd() + row.w + nrm_d.r, dma=True)
        row.did_read(st)
        nrm_d.did_write(st, accum=True)
    nv = nrm_d.ap.rearrange("e (c p) -> p e c", p=128)
    ld = P.emit("sync", lambda e, o=out_sb.ap, i=nv, ne=n_e:
                e.dma_start(out=out_sb.ap[:, :].rearrange("p (e c) -> p e c", e=ne), in_=i, allow_slow_non_contiguous=True),
                deps=nrm_d.rd() + out_sb.wr(), dma=True)
    nrm_d.did_read(ld)
    out_sb.did_write(ld)
    o4 = P.emit("scalar", lambda e, o=out_sb.ap: e.activation(o[:, :], o[:, :], AF.Sqrt), deps=out_sb.rd() + out_sb.w)
    out_sb.did_write(o4)
    o5 = P.emit("vector", lambda e, o=out_sb.ap, eps=eps: e.tensor_scalar(o[:, :], o[:, :], eps, None, OP.max), deps=[o4])
    out_sb.did_write(o5)
    o6 = P.emit("vector", lambda e, o=out_sb.ap: e.reciprocal(o[:, :], o[:, :]), deps=[o5])
    out_sb.did_write(o6)
    o7 = P.emit("vector", lambda e, o=out_sb.ap, g=g_sb.ap: e.tensor_tensor(o[:, :], o[:, :], g[:, :], OP.mult), deps=[o6] + g_sb.rd())
    g_sb.did_read(o7)
    out_sb.did_write(o7)


def build_layer2():
    nc = bass.Bass("TRN2", target_bir_lowering=False)
    di = lambda n, s: Buf(nc.dram_tensor(n, s, F32, kind="ExternalInput").ap())
    db16 = lambda n, s: Buf(nc.dram_tensor(n, s, BF16, kind="ExternalInput").ap())
    x_in = di("x", [T, D])
    ident_d = di("ident", [128, 128])
    masks_d = di("masks", [512, 512])
    triu_d = di("triu", [64, 64])
    wq_d = di("wq", [D, 512]); wk_d = di("wk", [D, 512])
    wv_d = di("wv", [D, D]); wg_d = di("wg", [D, D])
    wgk1_d = di("wgk1", [D, 16]); wgk2_d = di("wgk2", [16, 512])
    bgk2_d = di("bgk2", [128, 4])
    wo_d = di("wo", [D, D])
    gnwb_d = di("gnwb", [128, D])
    n1b_d = di("n1b", [128, D]); n2b_d = di("n2b", [128, D])
    wqiT_d = di("wqiT", [D, D]); wkiT_d = di("wkiT", [D, D]); wviT_d = di("wviT", [D, D])
    wpiT_d = di("wpiT", [D, D])
    rT_d = di("rT", [128, 128])
    cosb_d = di("cosb", [128, T]); sinb_d = di("sinb", [128, T])
    rwT_d = di("rwT", [D, 8])
    wgT_d = db16("wgmT", [8 * D, 4096])
    wuT_d = db16("wumT", [8 * D, 4096])
    wdT_d = db16("wdmT", [8 * 4096, D])
    sel_d = di("sel", [8, 1024])
    y = Buf(nc.dram_tensor("y", [T, D], F32, kind="ExternalOutput").ap())

    xnT = dbuf(nc, "xnT", [D, PAD + T], BF16)
    xn_tok = dbuf(nc, "xn_tok", [T, D], BF16)
    qT_g = dbuf(nc, "qT_g", [512, T], BF16)
    kT_g = dbuf(nc, "kT_g", [512, T], BF16)
    gkT_g = dbuf(nc, "gkT_g", [512, T], BF16)
    v_tok = dbuf(nc, "v_tok", [T, D], BF16)
    g_tok = dbuf(nc, "g_tok", [T, D], BF16)
    o_tok = dbuf(nc, "o_tok", [T, D], BF16)
    ooT_d = dbuf(nc, "ooT", [D, T], BF16)
    aout = dbuf(nc, "aout", [T, D], BF16)
    rms1T = dbuf(nc, "rms1T", [D, PAD + T], BF16)
    qTi = dbuf(nc, "qTi", [D, T], BF16)
    kTi = dbuf(nc, "kTi", [D, T], BF16)
    vaug = dbuf(nc, "vaug", [T, 1040], BF16)
    oTi = dbuf(nc, "oTi", [D, T], BF16)
    ai_d = dbuf(nc, "ai_d", [T, D], BF16)
    xmid = dbuf(nc, "xmid", [T, D], BF16)
    x2T = dbuf(nc, "x2T", [D, PAD + T], BF16)
    gates_d = dbuf(nc, "gates_d", [T, 8], F32)
    yT_d = dbuf(nc, "yT2", [D, T], F32)

    es = ExitStack()
    with es:
        P = Prog(nc)
        C = alloc_common(nc, es)
        load_ident(P, C, ident_d.ap)
        emit_zero_pad(P, C, [(xnT, D), (rms1T, D), (x2T, D)])
        emit_norm(P, C, x_in, xnT, xn_tok, kind="ln")

        esG = ExitStack()
        with esG:
            W = alloc_wpool(nc, esG, 2)
            load_w_into(P, W[0], wq_d.ap, D, 512)
            proj_fm(P, C, xnT, W[0], qT_g, 512, scale=float(128 ** -0.5))
            load_w_into(P, W[1], wk_d.ap, D, 512)
            proj_fm(P, C, xnT, W[1], kT_g, 512)
            load_w_into(P, W[0], wv_d.ap, D, D)
            proj_tm(P, C, xnT, W[0], make_store_cb(P, C, v_tok))
            load_w_into(P, W[1], wg_d.ap, D, D)
            proj_tm(P, C, xnT, W[1], make_store_cb(P, C, g_tok))
            wgk1_sb = load_w(P, nc, esG, "wgk1_sb", wgk1_d.ap, D, 16)
            t1T = sbuf(nc, esG, "t1T", [16, T], BF16)
            t1b = sbuf(nc, esG, "t1b", [128, 128], BF16)
            for tt in range(NT):
                lh = C["lhsP"][tt % 2]
                for c in range(NF):
                    op = P.emit("sync", lambda e, o=lh[c].ap, i=xnT.ap, c=c, tt=tt:
                                e.dma_start(out=o[:, :], in_=i[c * 128:(c + 1) * 128, PAD + tt * 128: PAD + (tt + 1) * 128]),
                                deps=lh[c].wr() + xnT.rd(), dma=True)
                    lh[c].did_write(op)
                    xnT.did_read(op)
                p = C["ps0"]
                mms = []
                for c in range(NF):
                    op = P.emit("tensor", lambda e, p_=p.ap, l=lh[c].ap, w=wgk1_sb.ap, c=c, first=(c == 0), last=(c == NF - 1):
                                e.matmul(p_[:, 0:16], l[:, :], w[:, c * 16:(c + 1) * 16], start=first, stop=last),
                                deps=(p.wr() if c == 0 else []) + lh[c].rd() + wgk1_sb.rd() + (mms[-1:] if mms else []))
                    mms.append(op)
                    lh[c].did_read(op)
                    wgk1_sb.did_read(op)
                p.did_write(mms[-1])
                ev = P.emit("vector", lambda e, o=t1b.ap, i=p.ap: e.tensor_copy(o[:, 0:16], i[:, 0:16]), deps=[mms[-1]] + t1b.wr())
                t1b.did_write(ev)
                p.did_read(ev)
                pt = C["psT1"]
                tr = tr128(P, C, t1b, 0, pt, 0, extra=[ev])
                ev2 = P.emit("vector", lambda e, o=t1T.ap, i=pt.ap, tt=tt: e.tensor_copy(o[0:16, tt * 128:(tt + 1) * 128], i[0:16, 0:128]), deps=[tr] + t1T.r)
                t1T.did_write(ev2, accum=True)
                pt.did_read(ev2)
            wgk2_sb = load_bf16(P, nc, esG, "wgk2_sb", wgk2_d.ap, [16, 512])
            bgk2_sb = load_f32(P, nc, esG, "bgk2_sb", bgk2_d.ap, [128, 4])
            for dk in range(4):
                for tc in range(T // 512):
                    p = C["ps1"]
                    mm = P.emit("tensor", lambda e, p_=p.ap, l=wgk2_sb.ap, r=t1T.ap, dk=dk, tc=tc:
                                e.matmul(p_[:, :], l[0:16, dk * 128:(dk + 1) * 128], r[0:16, tc * 512:(tc + 1) * 512], start=True, stop=True),
                                deps=p.wr() + wgk2_sb.rd() + t1T.rd())
                    wgk2_sb.did_read(mm)
                    t1T.did_read(mm)
                    p.did_write(mm)
                    xb = C["stD"][tc % 2]
                    o1 = P.emit("vector", lambda e, o=xb.ap, i=p.ap, b=bgk2_sb.ap, dk=dk:
                                e.tensor_scalar(o[:, 0:512], i[:, :], b[:, dk:dk + 1], None, OP.add), deps=[mm] + xb.wr() + bgk2_sb.rd())
                    bgk2_sb.did_read(o1)
                    xb.did_write(o1)
                    p.did_read(o1)
                    ab = C["stE"][tc % 2]
                    o2 = P.emit("scalar", lambda e, o=ab.ap, i=xb.ap: e.activation(o[:, 0:512], i[:, 0:512], AF.Abs), deps=[o1] + ab.wr())
                    ab.did_write(o2)
                    o3 = P.emit("scalar", lambda e, o=ab.ap: e.activation(o[:, 0:512], o[:, 0:512], AF.Exp, scale=-1.0), deps=[o2])
                    ab.did_write(o3)
                    o4 = P.emit("vector", lambda e, o=ab.ap: e.tensor_scalar(o[:, 0:512], o[:, 0:512], 1.0, None, OP.add), deps=[o3])
                    ab.did_write(o4)
                    o5 = P.emit("scalar", lambda e, o=ab.ap: e.activation(o[:, 0:512], o[:, 0:512], AF.Ln), deps=[o4])
                    ab.did_write(o5)
                    o6 = P.emit("vector", lambda e, o=xb.ap: e.tensor_scalar(o[:, 0:512], o[:, 0:512], 0.0, None, OP.min), deps=[o1, o2])
                    xb.did_write(o6)
                    o7 = P.emit("vector", lambda e, o=xb.ap, i=ab.ap: e.tensor_tensor(o[:, 0:512], o[:, 0:512], i[:, 0:512], OP.subtract), deps=[o5, o6])
                    xb.did_write(o7)
                    ab.did_read(o7)
                    gb = C["stB"][tc % 2]
                    o8 = P.emit("vector", lambda e, o=gb.ap, i=xb.ap: e.tensor_scalar(o[:, 0:512], i[:, 0:512], 1.0 / 16.0, None, OP.mult), deps=[o7] + gb.wr())
                    gb.did_write(o8)
                    xb.did_read(o8)
                    st = P.emit("scalar", lambda e, i=gb.ap, o=gkT_g.ap, dk=dk, tc=tc:
                                e.dma_start(out=o[dk * 128:(dk + 1) * 128, tc * 512:(tc + 1) * 512], in_=i[:, 0:512]),
                                deps=[o8] + gkT_g.r, dma=True)
                    gb.did_read(st)
                    gkT_g.did_write(st, accum=True)
            esG1 = ExitStack()
            esG1.__enter__()
            qTs = [sbuf(nc, esG1, f"qTs{h}", [128, T], BF16) for h in range(4)]
            kTs = [sbuf(nc, esG1, f"kTs{h}", [128, T], BF16) for h in range(4)]
            gTs = [sbuf(nc, esG1, f"gTs{h}", [128, T], BF16) for h in range(4)]
            for h in range(4):
                for src_d, dst in ((qT_g, qTs[h]), (kT_g, kTs[h]), (gkT_g, gTs[h])):
                    op = P.emit("sync", lambda e, o=dst.ap, i=src_d.ap, h=h: e.dma_start(out=o[:, :], in_=i[h * 128:(h + 1) * 128, :]),
                                deps=dst.wr() + src_d.rd(), dma=True)
                    dst.did_write(op)
                    src_d.did_read(op)
            Sst = [sbuf(nc, esG1, f"Sst{h}", [128, 256], F32) for h in range(4)]
            for h in range(4):
                op = P.emit("gpsimd", lambda e, o=Sst[h].ap: e.memset(o[:, :], 0.0), deps=Sst[h].wr())
                Sst[h].did_write(op)
            Sbf = sbuf(nc, esG1, "Sbf", [128, 256], BF16)
            zcol = sbuf(nc, esG1, "zcol", [128, 1], F32)
            zop = P.emit("gpsimd", lambda e, o=zcol.ap: e.memset(o[:, :], 0.0), deps=zcol.wr())
            zcol.did_write(zop)
            triu_sb = load_bf16(P, nc, esG1, "triu_sb", triu_d.ap, [64, 64])
            Gb = sbuf(nc, esG1, "Gb", [128, 64], F32)
            eGb = sbuf(nc, esG1, "eGb", [128, 64], F32)
            enGb = sbuf(nc, esG1, "enGb", [128, 64], F32)
            qg_sb = sbuf(nc, esG1, "qg_sb", [128, 64], BF16)
            ki_sb = sbuf(nc, esG1, "ki_sb", [128, 64], BF16)
            kg_sb = sbuf(nc, esG1, "kg_sb", [128, 64], BF16)
            kgt_sb = sbuf(nc, esG1, "kgt_sb", [64, 128], BF16)
            at_sb = sbuf(nc, esG1, "at_sb", [64, 64], BF16)
            vch = [sbuf(nc, esG1, f"vch{i}", [64, 1024], BF16) for i in range(2)]
            och = [sbuf(nc, esG1, f"och{i}", [64, 1024], BF16) for i in range(2)]
            for n in range(64):
                vc = vch[n % 2]
                lv = P.emit("sync", lambda e, o=vc.ap, i=v_tok.ap, n=n: e.dma_start(out=o[0:64, :], in_=i[n * 64:(n + 1) * 64, :]),
                            deps=vc.wr() + v_tok.rd(), dma=True)
                vc.did_write(lv)
                v_tok.did_read(lv)
                oc = och[n % 2]
                for h in range(4):
                    o1 = P.emit("vector", lambda e, o=Gb.ap, i=gTs[h].ap, z=zcol.ap, n=n:
                                e.tensor_tensor_scan(o[:, :], i[:, n * 64:(n + 1) * 64], i[:, n * 64:(n + 1) * 64], z[:, 0:1], OP.add, OP.bypass),
                                deps=gTs[h].rd() + Gb.wr() + zcol.rd())
                    zcol.did_read(o1)
                    gTs[h].did_read(o1)
                    Gb.did_write(o1)
                    o2 = P.emit("scalar", lambda e, o=eGb.ap, i=Gb.ap: e.activation(o[:, :], i[:, :], AF.Exp), deps=[o1] + eGb.wr())
                    eGb.did_write(o2)
                    o3 = P.emit("scalar", lambda e, o=enGb.ap, i=Gb.ap: e.activation(o[:, :], i[:, :], AF.Exp, scale=-1.0), deps=[o1] + enGb.wr())
                    enGb.did_write(o3)
                    Gb.did_read(o3)
                    o4 = P.emit("vector", lambda e, o=qg_sb.ap, i=qTs[h].ap, g=eGb.ap, n=n:
                                e.tensor_tensor(o[:, :], i[:, n * 64:(n + 1) * 64], g[:, :], OP.mult), deps=[o2] + qg_sb.wr() + qTs[h].rd())
                    qg_sb.did_write(o4)
                    qTs[h].did_read(o4)
                    eGb.did_read(o4)
                    o5 = P.emit("vector", lambda e, o=ki_sb.ap, i=kTs[h].ap, g=enGb.ap, n=n:
                                e.tensor_tensor(o[:, :], i[:, n * 64:(n + 1) * 64], g[:, :], OP.mult), deps=[o3] + ki_sb.wr() + kTs[h].rd())
                    ki_sb.did_write(o5)
                    kTs[h].did_read(o5)
                    enGb.did_read(o5)
                    o6 = P.emit("vector", lambda e, o=kg_sb.ap, i=ki_sb.ap, g=eGb.ap:
                                e.tensor_scalar(o[:, :], i[:, :], g[:, 63:64], None, OP.mult), deps=[o5, o2] + kg_sb.wr())
                    kg_sb.did_write(o6)
                    ki_sb.did_read(o6)
                    eGb.did_read(o6)
                    pt = C["psT1"]
                    o7 = P.emit("tensor", lambda e, o=pt.ap, i=kg_sb.ap, idd=C["ident"].ap:
                                e.transpose(o[0:64, 0:128], i[:, 0:64], idd[:, :]), deps=[o6] + pt.wr() + C["ident"].rd())
                    pt.did_write(o7)
                    kg_sb.did_read(o7)
                    o8 = P.emit("vector", lambda e, o=kgt_sb.ap, i=pt.ap: e.tensor_copy(o[0:64, :], i[0:64, 0:128]), deps=[o7] + kgt_sb.wr())
                    kgt_sb.did_write(o8)
                    pt.did_read(o8)
                    o9 = P.emit("tensor", lambda e, p=C["ps0"].ap, l=ki_sb.ap, r=qg_sb.ap:
                                e.matmul(p[0:64, 0:64], l[:, :], r[:, :], start=True, stop=True),
                                deps=C["ps0"].wr() + [o4, o5])
                    ki_sb.did_read(o9)
                    qg_sb.did_read(o9)
                    C["ps0"].did_write(o9)
                    o10 = P.emit("vector", lambda e, o=at_sb.ap, i=C["ps0"].ap, t=triu_sb.ap:
                                 e.tensor_tensor(o[0:64, :], i[0:64, 0:64], t[0:64, :], OP.mult), deps=[o9] + at_sb.wr() + triu_sb.rd())
                    at_sb.did_write(o10)
                    triu_sb.did_read(o10)
                    C["ps0"].did_read(o10)
                    o11 = P.emit("tensor", lambda e, p=C["ps1"].ap, l=at_sb.ap, r=vc.ap, h=h:
                                 e.matmul(p[0:64, 0:256], l[0:64, :], r[0:64, h * 256:(h + 1) * 256], start=True, stop=False),
                                 deps=C["ps1"].wr() + [o10] + vc.rd())
                    at_sb.did_read(o11)
                    vc.did_read(o11)
                    o12 = P.emit("vector", lambda e, o=Sbf.ap, i=Sst[h].ap: e.tensor_copy(o[:, :], i[:, :]), deps=Sst[h].rd() + Sbf.wr())
                    Sbf.did_write(o12)
                    Sst[h].did_read(o12)
                    o13 = P.emit("tensor", lambda e, p=C["ps1"].ap, l=qg_sb.ap, r=Sbf.ap:
                                 e.matmul(p[0:64, 0:256], l[:, 0:64], r[:, :], start=False, stop=True),
                                 deps=[o11, o12, o4])
                    qg_sb.did_read(o13)
                    Sbf.did_read(o13)
                    C["ps1"].did_write(o13)
                    o14 = P.emit("vector", lambda e, o=oc.ap, i=C["ps1"].ap, h=h: e.tensor_copy(o[0:64, h * 256:(h + 1) * 256], i[0:64, 0:256]),
                                 deps=[o13] + (oc.wr() if h == 0 else []))
                    oc.did_write(o14, accum=True)
                    C["ps1"].did_read(o14)
                    o15 = P.emit("tensor", lambda e, p=C["ps0"].ap, l=kgt_sb.ap, r=vc.ap, h=h:
                                 e.matmul(p[:, 0:256], l[0:64, :], r[0:64, h * 256:(h + 1) * 256], start=True, stop=True),
                                 deps=C["ps0"].wr() + [o8] + vc.rd())
                    kgt_sb.did_read(o15)
                    vc.did_read(o15)
                    C["ps0"].did_write(o15)
                    o16 = P.emit("vector", lambda e, o=Sst[h].ap, g=eGb.ap: e.tensor_scalar(o[:, :], o[:, :], g[:, 63:64], None, OP.mult),
                                 deps=[o12, o2] + Sst[h].wr())
                    Sst[h].did_write(o16)
                    eGb.did_read(o16)
                    o17 = P.emit("vector", lambda e, o=Sst[h].ap, i=C["ps0"].ap: e.tensor_tensor(o[:, :], o[:, :], i[:, 0:256], OP.add),
                                 deps=[o15, o16])
                    Sst[h].did_write(o17)
                    C["ps0"].did_read(o17)
                so = P.emit("scalar", lambda e, i=oc.ap, o=o_tok.ap, n=n: e.dma_start(out=o[n * 64:(n + 1) * 64, :], in_=i[0:64, :]),
                            deps=oc.rd() + oc.w + o_tok.r, dma=True)
                oc.did_read(so)
                o_tok.did_write(so, accum=True)
            esG1.__exit__(None, None, None)
            fG1 = P.fence()
            gnwb = load_f32(P, nc, esG, "gnwb_s", gnwb_d.ap, [128, D])
            gnwb.w += fG1
            for tt in range(NT):
                ot = C["stB"][tt % 2]
                gt = C["stC"][tt % 2]
                l1 = P.emit("sync", lambda e, o=ot.ap, i=o_tok.ap, tt=tt: e.dma_start(out=o[:, :], in_=i[tt * 128:(tt + 1) * 128, :]), deps=ot.wr() + o_tok.rd(), dma=True)
                ot.did_write(l1)
                o_tok.did_read(l1)
                l2 = P.emit("sync", lambda e, o=gt.ap, i=g_tok.ap, tt=tt: e.dma_start(out=o[:, :], in_=i[tt * 128:(tt + 1) * 128, :]), deps=gt.wr() + g_tok.rd(), dma=True)
                gt.did_write(l2)
                g_tok.did_read(l2)
                sqb = C["stD"][tt % 2]
                o1 = P.emit("scalar", lambda e, o=sqb.ap, i=ot.ap: e.activation(o[:, :], i[:, :], AF.Square), deps=[l1] + sqb.wr())
                sqb.did_write(o1)
                ot.did_read(o1)
                s = C["sm"][tt % 4]
                hops = []
                for h in range(4):
                    r1 = P.emit("vector", lambda e, o=s.ap, i=sqb.ap, h=h: e.reduce_sum(o[:, h:h + 1], i[:, h * 256:(h + 1) * 256], axis=AX),
                                deps=[o1] + (s.wr() if h == 0 else [hops[-1]]))
                    hops.append(r1)
                s.did_write(hops[-1])
                sqb.did_read(hops[-1])
                o2 = P.emit("vector", lambda e, o=s.ap: e.tensor_scalar(o[:, 4:8], o[:, 0:4], 1.0 / 256.0, 1e-5, OP.mult, OP.add), deps=hops)
                s.did_write(o2)
                o3 = P.emit("scalar", lambda e, o=s.ap: e.activation(o[:, 4:8], o[:, 4:8], AF.Sqrt), deps=[o2])
                s.did_write(o3)
                o4 = P.emit("vector", lambda e, o=s.ap: e.reciprocal(o[:, 4:8], o[:, 4:8]), deps=[o3])
                s.did_write(o4)
                sg = C["stE"][tt % 2]
                o5 = P.emit("scalar", lambda e, o=sg.ap, i=gt.ap: e.activation(o[:, :], i[:, :], AF.Silu), deps=[l2] + sg.wr())
                sg.did_write(o5)
                gt.did_read(o5)
                on = C["stA"][tt % 2]
                stps = []
                for h in range(4):
                    p1 = P.emit("vector", lambda e, o=on.ap, i=ot.ap, s_=s.ap, g=gnwb.ap, h=h:
                                e.scalar_tensor_tensor(o[:, h * 256:(h + 1) * 256], i[:, h * 256:(h + 1) * 256], s_[:, 4 + h:5 + h], g[:, h * 256:(h + 1) * 256], OP.mult, OP.mult),
                                deps=[o4, l1] + (on.wr() if h == 0 else []) + gnwb.rd())
                    gnwb.did_read(p1)
                    on.did_write(p1, accum=True)
                    ot.did_read(p1)
                    s.did_read(p1)
                    stps.append(p1)
                oo = C["stB"][(tt + 1) % 2]
                o6 = P.emit("vector", lambda e, o=oo.ap, i=on.ap, g=sg.ap: e.tensor_tensor(o[:, :], i[:, :], g[:, :], OP.mult),
                            deps=stps + [o5] + oo.wr())
                oo.did_write(o6)
                on.did_read(o6)
                sg.did_read(o6)
                pt = [C["psT1"], C["psT2"]][tt % 2]
                trs = [tr128(P, C, oo, f * 128, pt, f * 128, extra=[o6]) for f in range(NF)]
                evb = C["stC"][(tt + 1) % 2]
                o7 = P.emit("vector", lambda e, o=evb.ap, i=pt.ap: e.tensor_copy(o[:, :], i[:, :]), deps=[trs[-1]] + evb.wr())
                evb.did_write(o7)
                pt.did_read(o7)
                for f in range(NF):
                    st = P.emit("scalar", lambda e, i=evb.ap, o=ooT_d.ap, f=f, tt=tt:
                                e.dma_start(out=o[f * 128:(f + 1) * 128, tt * 128:(tt + 1) * 128], in_=i[:, f * 128:(f + 1) * 128]),
                                deps=evb.rd() + [o7] + ooT_d.r, dma=True)
                    evb.did_read(st)
                    ooT_d.did_write(st, accum=True)
            load_w_into(P, W[0], wo_d.ap, D, D)
            proj_tm(P, C, ooT_d, W[0], make_store_cb(P, C, aout), src_pad=0)
            fG = P.fence()

        esI = ExitStack()
        with esI:
            W = alloc_wpool(nc, esI, 2)
            for wb in W:
                wb.w += fG
            n1b = load_f32(P, nc, esI, "n1b_s", n1b_d.ap, [128, D])
            n1b.w += fG
            emit_norm(P, C, xn_tok, rms1T, None, kind="rms", gamma_bc=n1b, eps=1e-6, src_bf16=True)
            rT_sb = load_bf16(P, nc, esI, "rT_sb", rT_d.ap, [128, 128])
            rT_sb.w += fG
            cosb = load_bf16(P, nc, esI, "cosb_s", cosb_d.ap, [128, T])
            cosb.w += fG
            sinb = load_bf16(P, nc, esI, "sinb_s", sinb_d.ap, [128, T])
            sinb.w += fG
            masks = []
            for m in range(4):
                mb = sbuf(nc, esI, f"mask{m}", [128, 512], BF16, init=fG)
                op = P.emit("gpsimd", lambda e, o=mb.ap, i=masks_d.ap, m=m: e.dma_start(out=o[:, :], in_=i[m * 128:(m + 1) * 128, :], accum_op=OP.bypass), deps=mb.wr(), dma=True)
                mb.did_write(op)
                masks.append(mb)

            def rope_cb(do, tc, eb, op):
                ps = C["psS2"]
                m1 = P.emit("tensor", lambda e, p=ps.ap, l=rT_sb.ap, r=eb.ap:
                            e.matmul(p[:, 0:512], l[:, :], r[:, 0:512], start=True, stop=True),
                            deps=ps.wr() + rT_sb.rd() + [op])
                rT_sb.did_read(m1)
                eb.did_read(m1)
                ps.did_write(m1)
                t1 = C["stD"][do % 2]
                o1 = P.emit("vector", lambda e, o=t1.ap, i=ps.ap, s=sinb.ap, tc=tc:
                            e.tensor_tensor(o[:, 0:512], i[:, 0:512], s[:, tc * 512:(tc + 1) * 512], OP.mult),
                            deps=[m1] + t1.wr() + sinb.rd())
                sinb.did_read(o1)
                t1.did_write(o1)
                ps.did_read(o1)
                t2 = C["stE"][do % 2]
                o2 = P.emit("vector", lambda e, o=t2.ap, i=eb.ap, s=cosb.ap, tc=tc:
                            e.tensor_tensor(o[:, 0:512], i[:, 0:512], s[:, tc * 512:(tc + 1) * 512], OP.mult),
                            deps=[op] + t2.wr() + cosb.rd())
                cosb.did_read(o2)
                t2.did_write(o2)
                eb.did_read(o2)
                nb = C["stB"][do % 2]
                o3 = P.emit("vector", lambda e, o=nb.ap, a=t2.ap, b=t1.ap: e.tensor_tensor(o[:, 0:512], a[:, 0:512], b[:, 0:512], OP.add),
                            deps=[o1, o2] + nb.wr())
                nb.did_write(o3)
                t1.did_read(o3)
                t2.did_read(o3)
                return nb, o3

            load_w_into(P, W[0], wqiT_d.ap, D, D)
            proj_fm(P, C, rms1T, W[0], qTi, D, scale=0.125, post_cb=rope_cb)
            load_w_into(P, W[1], wkiT_d.ap, D, D)
            proj_fm(P, C, rms1T, W[1], kTi, D, post_cb=rope_cb)
            load_w_into(P, W[0], wviT_d.ap, D, D)
            emit_vproj(P, C, rms1T, W[0], None, vaug)
            emit_attention(P, C, nc, esI, qTi, kTi, vaug, oTi, masks, tag="a2")
            load_w_into(P, W[1], wpiT_d.ap, D, D)
            proj_tm(P, C, oTi, W[1], make_store_cb(P, C, ai_d), src_pad=0)
            for tt in range(NT):
                a = C["stB"][tt % 2]
                b = C["stC"][tt % 2]
                l1 = P.emit("sync", lambda e, o=a.ap, i=xn_tok.ap, tt=tt: e.dma_start(out=o[:, :], in_=i[tt * 128:(tt + 1) * 128, :]), deps=a.wr() + xn_tok.rd(), dma=True)
                a.did_write(l1)
                xn_tok.did_read(l1)
                l2 = P.emit("sync", lambda e, o=b.ap, i=ai_d.ap, tt=tt: e.dma_start(out=o[:, :], in_=i[tt * 128:(tt + 1) * 128, :]), deps=b.wr() + ai_d.rd(), dma=True)
                b.did_write(l2)
                ai_d.did_read(l2)
                o1 = P.emit("vector", lambda e, o=a.ap, i=b.ap: e.tensor_tensor(o[:, :], o[:, :], i[:, :], OP.add), deps=[l1, l2])
                a.did_write(o1)
                b.did_read(o1)
                st = P.emit("scalar", lambda e, i=a.ap, o=xmid.ap, tt=tt: e.dma_start(out=o[tt * 128:(tt + 1) * 128, :], in_=i[:, :]), deps=[o1] + xmid.r, dma=True)
                a.did_read(st)
                xmid.did_write(st, accum=True)
            fI = P.fence()

        esM = ExitStack()
        with esM:
            sb2 = lambda n, s, d: sbuf(nc, esM, n, s, d, init=fI)
            n2b = load_f32(P, nc, esM, "n2b_s", n2b_d.ap, [128, D])
            n2b.w += fI
            emit_norm(P, C, xmid, x2T, None, kind="rms", gamma_bc=n2b, eps=1e-6, src_bf16=True)
            rw_sb = load_w(P, nc, esM, "rw_sb", rwT_d.ap, D, 8)
            rw_sb.w += fI
            sel_s = load_bf16(P, nc, esM, "sel2_s", sel_d.ap, [8, 1024])
            sel_s.w += fI
            lg = sb2("lg2", [128, 8], F32)
            m8 = sb2("m82", [128, 8], F32)
            w12 = sb2("w122", [128, 2], F32)
            mk1 = sb2("mk12", [128, 8], F32)
            mk2 = sb2("mk22", [128, 8], F32)
            cwb = sb2("cwb2", [128, 128], BF16)
            gatesT = sb2("gatesT", [8, T], BF16)
            x2f = [sb2(f"x2f{c}", [128, 1024], BF16) for c in range(NF)]
            for tt in range(NT):
                lh = C["lhsP"][tt % 2]
                for c in range(NF):
                    op = P.emit("sync", lambda e, o=lh[c].ap, i=x2T.ap, c=c, tt=tt:
                                e.dma_start(out=o[:, :], in_=i[c * 128:(c + 1) * 128, PAD + tt * 128: PAD + (tt + 1) * 128]),
                                deps=lh[c].wr() + x2T.rd(), dma=True)
                    lh[c].did_write(op)
                    x2T.did_read(op)
                p = C["ps0"]
                mms = []
                for c in range(NF):
                    op = P.emit("tensor", lambda e, p_=p.ap, l=lh[c].ap, w=rw_sb.ap, c=c, first=(c == 0), last=(c == NF - 1):
                                e.matmul(p_[:, 0:8], l[:, :], w[:, c * 8:(c + 1) * 8], start=first, stop=last),
                                deps=(p.wr() if c == 0 else []) + lh[c].rd() + rw_sb.rd() + (mms[-1:] if mms else []))
                    mms.append(op)
                    lh[c].did_read(op)
                    rw_sb.did_read(op)
                p.did_write(mms[-1])
                o1 = P.emit("vector", lambda e, o=lg.ap, i=p.ap: e.tensor_copy(o[:, :], i[:, 0:8]), deps=[mms[-1]] + lg.wr())
                lg.did_write(o1)
                p.did_read(o1)
                o2 = P.emit("vector", lambda e, o=m8.ap, i=lg.ap: e.max(out=o[:, :], in_=i[:, :]), deps=[o1] + m8.wr())
                m8.did_write(o2)
                lg.did_read(o2)
                o3 = P.emit("vector", lambda e, o=w12.ap, i=m8.ap: e.tensor_tensor(o[:, 0:1], i[:, 0:1], i[:, 1:2], OP.subtract), deps=[o2] + w12.wr())
                o4 = P.emit("vector", lambda e, o=w12.ap, i=m8.ap: e.tensor_tensor(o[:, 1:2], i[:, 1:2], i[:, 0:1], OP.subtract), deps=[o2, o3])
                w12.did_write(o4)
                m8.did_read(o4)
                o5 = P.emit("scalar", lambda e, o=w12.ap: e.activation(o[:, 0:1], o[:, 0:1], AF.Sigmoid), deps=[o3, o4])
                o6 = P.emit("scalar", lambda e, o=w12.ap: e.activation(o[:, 1:2], o[:, 1:2], AF.Sigmoid), deps=[o4, o5])
                w12.did_write(o6)
                o7 = P.emit("vector", lambda e, o=mk1.ap, i=lg.ap, m=m8.ap: e.tensor_scalar(o[:, :], i[:, :], m[:, 0:1], None, OP.is_ge), deps=[o2] + mk1.wr())
                mk1.did_write(o7)
                o8 = P.emit("vector", lambda e, o=mk2.ap, i=lg.ap, m=m8.ap: e.tensor_scalar(o[:, :], i[:, :], m[:, 1:2], None, OP.is_ge), deps=[o2, o7] + mk2.wr())
                mk2.did_write(o8)
                lg.did_read(o8)
                m8.did_read(o8)
                o9 = P.emit("vector", lambda e, o=mk2.ap, i=mk1.ap: e.tensor_tensor(o[:, :], o[:, :], i[:, :], OP.subtract), deps=[o7, o8])
                mk2.did_write(o9)
                mk1.did_read(o9)
                oa = P.emit("vector", lambda e, o=mk1.ap, w=w12.ap: e.tensor_scalar(o[:, :], o[:, :], w[:, 0:1], None, OP.mult), deps=[o9, o6])
                mk1.did_write(oa)
                ob = P.emit("vector", lambda e, o=mk2.ap, w=w12.ap: e.tensor_scalar(o[:, :], o[:, :], w[:, 1:2], None, OP.mult), deps=[o9, o6, oa])
                mk2.did_write(ob)
                w12.did_read(ob)
                oc = P.emit("vector", lambda e, o=cwb.ap, a=mk1.ap, b=mk2.ap: e.tensor_tensor(o[:, 0:8], a[:, :], b[:, :], OP.add), deps=[oa, ob] + cwb.wr())
                cwb.did_write(oc)
                mk1.did_read(oc)
                mk2.did_read(oc)
                og = P.emit("vector", lambda e, o=lg.ap, a=mk1.ap, b=mk2.ap: e.tensor_tensor(o[:, :], a[:, :], b[:, :], OP.add), deps=[oa, ob] + lg.wr() + lg.r)
                lg.did_write(og)
                mk1.did_read(og)
                mk2.did_read(og)
                sg = P.emit("scalar", lambda e, i=lg.ap, o=gates_d.ap, tt=tt: e.dma_start(out=o[tt * 128:(tt + 1) * 128, :], in_=i[:, :]), deps=[og] + gates_d.r, dma=True)
                lg.did_read(sg)
                gates_d.did_write(sg, accum=True)
                pt = C["psT1"]
                tr = tr128(P, C, cwb, 0, pt, 0, extra=[oc])
                ev = P.emit("vector", lambda e, o=gatesT.ap, i=pt.ap, tt=tt: e.tensor_copy(o[0:8, tt * 128:(tt + 1) * 128], i[0:8, 0:128]), deps=[tr] + gatesT.r)
                gatesT.did_write(ev, accum=True)
                pt.did_read(ev)
            sT = [sb2(f"sT2_{i}", [128, 1024], BF16) for i in range(32)]
            wgt = sb2("wgt", [128, 1024], BF16)
            wut = sb2("wut", [128, 1024], BF16)
            wdt = sb2("wdt", [128, 4096], BF16)
            gbx = sb2("gbx", [128, 1024], BF16)
            ypv = [sb2(f"ypv2{i}", [128, 512], F32) for i in range(2)]
            TB = 1024
            for tb in range(T // TB):
                for c in range(NF):
                    op = P.emit("sync", lambda e, o=x2f[c].ap, i=x2T.ap, c=c, tb=tb:
                                e.dma_start(out=o[:, :], in_=i[c * 128:(c + 1) * 128, PAD + tb * TB: PAD + (tb + 1) * TB]),
                                deps=x2f[c].wr() + x2T.rd(), dma=True)
                    x2f[c].did_write(op)
                    x2T.did_read(op)
                for e_ in range(8):
                    for ho in range(32):
                        lw1 = P.emit("sync", lambda e, o=wgt.ap, i=wgT_d.ap, e_=e_, ho=ho:
                                     e.dma_start(out=o[:, :].rearrange("p (c w) -> p c w", w=128),
                                                 in_=i[e_ * 1024:(e_ + 1) * 1024, ho * 128:(ho + 1) * 128].rearrange("(c p) w -> p c w", p=128)),
                                     deps=wgt.wr() + wgT_d.rd(), dma=True)
                        wgt.did_write(lw1)
                        wgT_d.did_read(lw1)
                        lw2 = P.emit("sync", lambda e, o=wut.ap, i=wuT_d.ap, e_=e_, ho=ho:
                                     e.dma_start(out=o[:, :].rearrange("p (c w) -> p c w", w=128),
                                                 in_=i[e_ * 1024:(e_ + 1) * 1024, ho * 128:(ho + 1) * 128].rearrange("(c p) w -> p c w", p=128)),
                                     deps=wut.wr() + wuT_d.rd(), dma=True)
                        wut.did_write(lw2)
                        wuT_d.did_read(lw2)
                        for tc2 in range(2):
                            pg = C["ps0"]
                            pu = C["ps1"]
                            mg, mu = [], []
                            for c in range(NF):
                                op = P.emit("tensor", lambda e, p_=pg.ap, l=wgt.ap, r=x2f[c].ap, c=c, tc2=tc2, first=(c == 0), last=(c == NF - 1):
                                            e.matmul(p_[:, :], l[:, c * 128:(c + 1) * 128], r[:, tc2 * 512:(tc2 + 1) * 512], start=first, stop=last),
                                            deps=(pg.wr() if c == 0 else []) + [lw1] + x2f[c].rd() + (mg[-1:] if mg else []))
                                mg.append(op)
                                wgt.did_read(op)
                                x2f[c].did_read(op)
                            pg.did_write(mg[-1])
                            for c in range(NF):
                                op = P.emit("tensor", lambda e, p_=pu.ap, l=wut.ap, r=x2f[c].ap, c=c, tc2=tc2, first=(c == 0), last=(c == NF - 1):
                                            e.matmul(p_[:, :], l[:, c * 128:(c + 1) * 128], r[:, tc2 * 512:(tc2 + 1) * 512], start=first, stop=last),
                                            deps=(pu.wr() if c == 0 else []) + [lw2] + x2f[c].rd() + (mu[-1:] if mu else []))
                                mu.append(op)
                                wut.did_read(op)
                                x2f[c].did_read(op)
                            pu.did_write(mu[-1])
                            gsil = C["stB"][tc2]
                            e1 = P.emit("scalar", lambda e, o=gsil.ap, i=pg.ap: e.activation(o[:, 0:512], i[:, :], AF.Silu), deps=[mg[-1]] + gsil.wr())
                            gsil.did_write(e1)
                            pg.did_read(e1)
                            e2 = P.emit("vector", lambda e, o=sT[ho].ap, a=gsil.ap, b=pu.ap, tc2=tc2:
                                        e.tensor_tensor(o[:, tc2 * 512:(tc2 + 1) * 512], a[:, 0:512], b[:, :], OP.mult),
                                        deps=[e1, mu[-1]] + sT[ho].wr())
                            sT[ho].did_write(e2, accum=True)
                            gsil.did_read(e2)
                            pu.did_read(e2)
                    for tc2 in range(2):
                        p = C["psS2"]
                        mm = P.emit("tensor", lambda e, p_=p.ap, l=sel_s.ap, r=gatesT.ap, e_=e_, tb=tb, tc2=tc2:
                                    e.matmul(p_[:, 0:512], l[0:8, e_ * 128:(e_ + 1) * 128], r[0:8, tb * TB + tc2 * 512: tb * TB + (tc2 + 1) * 512], start=True, stop=True),
                                    deps=p.wr() + sel_s.rd() + gatesT.rd())
                        sel_s.did_read(mm)
                        gatesT.did_read(mm)
                        p.did_write(mm)
                        ev = P.emit("vector", lambda e, o=gbx.ap, i=p.ap, tc2=tc2: e.tensor_copy(o[:, tc2 * 512:(tc2 + 1) * 512], i[:, 0:512]),
                                    deps=[mm] + (gbx.wr() if tc2 == 0 else []))
                        gbx.did_write(ev, accum=True)
                        p.did_read(ev)
                    for do in range(NF):
                        lw3 = P.emit("sync", lambda e, o=wdt.ap, i=wdT_d.ap, e_=e_, do=do:
                                     e.dma_start(out=o[:, :].rearrange("p (c w) -> p c w", w=128),
                                                 in_=i[e_ * 4096:(e_ + 1) * 4096, do * 128:(do + 1) * 128].rearrange("(c p) w -> p c w", p=128)),
                                     deps=wdt.wr() + wdT_d.rd(), dma=True)
                        wdt.did_write(lw3)
                        wdT_d.did_read(lw3)
                        for tc2 in range(2):
                            p = C["psS1"]
                            mms = []
                            for ho in range(32):
                                op = P.emit("tensor", lambda e, p_=p.ap, l=wdt.ap, r=sT[ho].ap, ho=ho, tc2=tc2, first=(ho == 0), last=(ho == 31):
                                            e.matmul(p_[:, 0:512], l[:, ho * 128:(ho + 1) * 128], r[:, tc2 * 512:(tc2 + 1) * 512], start=first, stop=last),
                                            deps=(p.wr() if ho == 0 else []) + [lw3] + sT[ho].rd() + (mms[-1:] if mms else []))
                                mms.append(op)
                                wdt.did_read(op)
                                sT[ho].did_read(op)
                            p.did_write(mms[-1])
                            eb = C["stE"][tc2]
                            o1 = P.emit("vector", lambda e, o=eb.ap, i=p.ap, g=gbx.ap, tc2=tc2:
                                        e.tensor_tensor(o[:, 0:512], i[:, 0:512], g[:, tc2 * 512:(tc2 + 1) * 512], OP.mult),
                                        deps=[mms[-1]] + eb.wr() + gbx.rd())
                            gbx.did_read(o1)
                            eb.did_write(o1)
                            p.did_read(o1)
                            if e_ > 0:
                                yp = ypv[tc2]
                                lp = P.emit("sync", lambda e, o=yp.ap, i=yT_d.ap, do=do, tb=tb, tc2=tc2:
                                            e.dma_start(out=o[:, :], in_=i[do * 128:(do + 1) * 128, tb * TB + tc2 * 512: tb * TB + (tc2 + 1) * 512]),
                                            deps=yp.wr() + yT_d.rd(), dma=True)
                                yp.did_write(lp)
                                yT_d.did_read(lp)
                                o2 = P.emit("vector", lambda e, o=eb.ap, i=yp.ap: e.tensor_tensor(o[:, 0:512], o[:, 0:512], i[:, :], OP.add), deps=[o1, lp])
                                eb.did_write(o2)
                                yp.did_read(o2)
                            else:
                                o2 = o1
                            st = P.emit("scalar", lambda e, i=eb.ap, o=yT_d.ap, do=do, tb=tb, tc2=tc2:
                                        e.dma_start(out=o[do * 128:(do + 1) * 128, tb * TB + tc2 * 512: tb * TB + (tc2 + 1) * 512], in_=i[:, 0:512]),
                                        deps=[o2] + yT_d.r, dma=True)
                            eb.did_read(st)
                            yT_d.w.append(st)
            fM = P.fence()

        for t in range(NT):
            a = C["stA"][t % 2]
            u = C["stB"][t % 2]
            v = C["stC"][t % 2]
            ft = C["stD"][t % 2]
            res = C["stE"][t % 2]
            l1 = P.emit("sync", lambda e, o=a.ap, i=x_in.ap, t=t: e.dma_start(out=o[:, :], in_=i[t * 128:(t + 1) * 128, :]), deps=a.wr() + x_in.rd() + fM, dma=True)
            a.did_write(l1)
            x_in.did_read(l1)
            l2 = P.emit("sync", lambda e, o=u.ap, i=aout.ap, t=t: e.dma_start(out=o[:, :], in_=i[t * 128:(t + 1) * 128, :]), deps=u.wr() + aout.rd() + fM, dma=True)
            u.did_write(l2)
            aout.did_read(l2)
            l3 = P.emit("sync", lambda e, o=v.ap, i=xmid.ap, t=t: e.dma_start(out=o[:, :], in_=i[t * 128:(t + 1) * 128, :]), deps=v.wr() + xmid.rd() + fM, dma=True)
            v.did_write(l3)
            xmid.did_read(l3)
            bstage = C["vstg"][t % 2] if False else None
            pt = [C["psT1"], C["psT2"]][t % 2]
            trs = []
            yb16 = C["stC"][(t + 1) % 2]
            for f in range(NF):
                yst = C["stD"][(t + 1) % 2]
                l4 = P.emit("sync", lambda e, o=yst.ap, i=yT_d.ap, f=f, t=t:
                            e.dma_start(out=o[:, 0:128], in_=i[f * 128:(f + 1) * 128, t * 128:(t + 1) * 128]),
                            deps=yst.wr() + yT_d.rd() + fM, dma=True)
                yst.did_write(l4)
                yT_d.did_read(l4)
                cv = P.emit("vector", lambda e, o=yb16.ap, i=yst.ap, f=f: e.tensor_copy(o[:, f * 128:(f + 1) * 128], i[:, 0:128]),
                            deps=[l4] + (yb16.wr() if f == 0 else []))
                yb16.did_write(cv, accum=True)
                yst.did_read(cv)
                trs.append(tr128(P, C, yb16, f * 128, pt, f * 128, extra=[cv]))
            ev = P.emit("vector", lambda e, o=ft.ap, i=pt.ap: e.tensor_copy(o[:, :], i[:, :]), deps=[trs[-1]] + ft.wr())
            ft.did_write(ev)
            pt.did_read(ev)
            o1 = P.emit("vector", lambda e, o=res.ap, i0=a.ap, i1=u.ap: e.tensor_tensor(o[:, :], i0[:, :], i1[:, :], OP.add), deps=[l1, l2] + res.wr())
            res.did_write(o1)
            a.did_read(o1)
            u.did_read(o1)
            o2 = P.emit("vector", lambda e, o=res.ap, i1=v.ap: e.tensor_tensor(o[:, :], o[:, :], i1[:, :], OP.add), deps=[o1, l3])
            res.did_write(o2)
            v.did_read(o2)
            o3 = P.emit("vector", lambda e, o=res.ap, i1=ft.ap: e.tensor_tensor(o[:, :], o[:, :], i1[:, :], OP.add), deps=[o2, ev])
            res.did_write(o3)
            ft.did_read(o3)
            st = P.emit("scalar", lambda e, i=res.ap, o=y.ap, t=t: e.dma_start(out=o[t * 128:(t + 1) * 128, :], in_=i[:, :]), deps=[o3] + y.r, dma=True)
            res.did_read(st)
            y.did_write(st, accum=True)
        final_waits(P)
        P.finalize(es)
    return nc


def prep_l2_inputs(ins, xb):
    import ml_dtypes
    ident, masks = _l0_consts()
    HD = 64
    inv_freq = 1.0 / (10000.0 ** (np.arange(0, HD, 2, dtype=np.float64) / HD))
    tpos = np.arange(T, dtype=np.float64)
    freqs = np.einsum("i,j->ij", tpos, inv_freq)
    emb = np.concatenate([freqs, freqs], axis=-1)
    cosb = np.tile(np.cos(emb).T.astype(np.float32), (2, 1))
    sinb = np.tile(np.sin(emb).T.astype(np.float32), (2, 1))
    R = np.zeros((128, 128), np.float32)
    for base in (0, 64):
        for dd in range(32):
            R[base + dd, base + 32 + dd] = -1.0
            R[base + 32 + dd, base + dd] = 1.0
    triu = np.triu(np.ones((64, 64), np.float32))
    qkv = np.asarray(ins["L2_ffn_qkv_w"], np.float32)
    wg_m = np.asarray(ins["L2_ffn_wg"], np.float32)
    wu_m = np.asarray(ins["L2_ffn_wu"], np.float32)
    wd_m = np.asarray(ins["L2_ffn_wd"], np.float32)
    bf = ml_dtypes.bfloat16
    sel = np.zeros((8, 1024), np.float32)
    for e in range(8):
        sel[e, e * 128:(e + 1) * 128] = 1.0
    return {
        "x": np.ascontiguousarray(xb, dtype=np.float32),
        "ident": ident, "masks": masks, "triu": triu,
        "wq": np.asarray(ins["L2_attn_Wq"], np.float32),
        "wk": np.asarray(ins["L2_attn_Wk"], np.float32),
        "wv": np.asarray(ins["L2_attn_Wv"], np.float32),
        "wg": np.asarray(ins["L2_attn_Wg"], np.float32),
        "wgk1": np.asarray(ins["L2_attn_Wgk1"], np.float32),
        "wgk2": np.asarray(ins["L2_attn_Wgk2"], np.float32),
        "bgk2": np.ascontiguousarray(np.asarray(ins["L2_attn_bgk2"], np.float32).reshape(4, 128).T),
        "wo": np.asarray(ins["L2_attn_Wo"], np.float32),
        "gnwb": brow(np.tile(np.asarray(ins["L2_attn_gnw"], np.float32), 4)),
        "n1b": brow(np.asarray(ins["L2_ffn_norm1_w"], np.float32)),
        "n2b": brow(np.asarray(ins["L2_ffn_norm2_w"], np.float32)),
        "wqiT": t_(qkv[0:1024]),
        "wkiT": t_(qkv[1024:2048]),
        "wviT": t_(qkv[2048:3072]),
        "wpiT": t_(np.asarray(ins["L2_ffn_proj_w"], np.float32)),
        "rT": t_(R),
        "cosb": cosb, "sinb": sinb,
        "rwT": t_(np.asarray(ins["L2_ffn_router_w"], np.float32)),
        "wgmT": np.ascontiguousarray(np.transpose(wg_m, (0, 2, 1)).reshape(8 * 1024, 4096)).astype(bf),
        "wumT": np.ascontiguousarray(np.transpose(wu_m, (0, 2, 1)).reshape(8 * 1024, 4096)).astype(bf),
        "wdmT": np.ascontiguousarray(np.transpose(wd_m, (0, 2, 1)).reshape(8 * 4096, 1024)).astype(bf),
        "sel": sel,
    }


# version 05 (final)
# speedup vs baseline: 1.3560x; 1.3560x over previous
import os as _os_env
_os_env.environ.setdefault("GAUGE_NTFF_IGNORE_DMA", "1")

import numpy as np
from contextlib import ExitStack

import concourse.bass as bass
import concourse.mybir as mybir
from concourse.alu_op_type import AluOpType as OP
from concourse.bass_utils import run_bass_kernel_spmd

F32 = mybir.dt.float32
I32 = mybir.dt.int32
BF16 = mybir.dt.bfloat16
AF = mybir.ActivationFunctionType
AX = mybir.AxisListType.X

B, T, D = 4, 4096, 1024
NT = T // 128
NF = D // 128
PAD = 16
TL3 = 2560
TOFF = 256
BIG = 1.0e30

ENGINES = ("sync", "tensor", "vector", "scalar", "gpsimd")


class Prog:

    def __init__(self, nc):
        self.nc = nc
        self.ops = []

    def emit(self, engine, fn, deps=(), dma=False):
        i = len(self.ops)
        ds = set()
        for d in deps:
            if d is None:
                continue
            ds.add(d)
        self.ops.append(dict(e=engine, fn=fn, deps=sorted(ds), dma=dma, needed=dma, idx=i))
        return i

    def finalize(self, es):
        nc = self.nc
        for op in self.ops:
            for d in op["deps"]:
                self.ops[d]["needed"] = True
        csem = {e: es.enter_context(nc.semaphore(f"c_{e}")) for e in ENGINES}
        dsem = {e: es.enter_context(nc.semaphore(f"d_{e}")) for e in ENGINES}
        ccnt = {e: 0 for e in ENGINES}
        dcnt = {e: 0 for e in ENGINES}
        for op in self.ops:
            e = op["e"]
            if op["dma"]:
                dcnt[e] += 16
                op["ev"] = (e, True, dcnt[e])
            elif op["needed"]:
                ccnt[e] += 1
                op["ev"] = (e, False, ccnt[e])
            else:
                op["ev"] = None
        per_eng = {e: [op for op in self.ops if op["e"] == e] for e in ENGINES}
        dma_before = {e: [] for e in ENGINES}
        run = {e: 0 for e in ENGINES}
        cum = []
        for op in self.ops:
            if op["dma"]:
                run[op["e"]] += 1
            cum.append(dict(run))
        block = es.enter_context(nc.Block())

        def mk(eng_name):
            ops = per_eng[eng_name]

            def body(eng):
                hw = {}
                for op in ops:
                    my_idx = op["idx"]
                    waits = {}
                    if op["dma"]:
                        _, _, myval = op["ev"]
                        prior = myval - 16
                        if prior > 0:
                            waits[(eng_name, True)] = prior
                    for d in op["deps"]:
                        ev = self.ops[d]["ev"]
                        en, is_d, val = ev
                        key = (en, is_d)
                        if hw.get(key, 0) < val:
                            waits[key] = max(waits.get(key, 0), val)
                    for (en, is_d), val in sorted(waits.items()):
                        eng.wait_ge(dsem[en] if is_d else csem[en], val)
                        hw[(en, is_d)] = val
                    inst = op["fn"](eng)
                    ev = op["ev"]
                    if ev is not None:
                        en, is_d, _ = ev
                        inst.then_inc(dsem[en] if is_d else csem[en], 16 if is_d else 1)
            return body

        block.sync(mk("sync"))
        block.tensor(mk("tensor"))
        block.vector(mk("vector"))
        block.scalar(mk("scalar"))
        block.gpsimd(mk("gpsimd"))

    def barrier_deps(self):
        return list(range(len(self.ops)))

    def fence(self):
        deps = self.barrier_deps()
        return [self.emit(e, lambda en: en.nop(), deps=deps) for e in ENGINES]


class Buf:

    __slots__ = ("ap", "w", "r", "open")

    def __init__(self, ap, init=None):
        self.ap = ap
        self.w = list(init) if init else []
        self.r = []
        self.open = False

    def rd(self):
        return list(self.w)

    def wr(self):
        return self.r + self.w

    def did_read(self, opid):
        self.r.append(opid)
        self.open = False

    def did_write(self, opid, accum=False):
        if accum and self.open:
            self.w.append(opid)
        else:
            self.w.clear()
            self.w.append(opid)
            self.r.clear()
            self.open = bool(accum)

    def alias(self, ap):
        b = Buf(ap)
        b.w = self.w
        b.r = self.r
        return b


_NAME_N = [0]
CUR_FENCE = []


def _reset_build_state():
    global CUR_FENCE
    _NAME_N[0] = 0
    _WPOOL_N[0] = 0
    CUR_FENCE = []


def _uniq(name):
    _NAME_N[0] += 1
    return f"{name}_u{_NAME_N[0]}"


def sbuf(nc, es, name, shape, dt, init=None):
    if init is None:
        init = list(CUR_FENCE)
    return Buf(es.enter_context(nc.sbuf_tensor(_uniq(name), shape, dt)), init=init)


def psum(nc, es, name, shape, dt):
    return Buf(es.enter_context(nc.psum_tensor(_uniq(name), shape, dt)))


def dbuf(nc, name, shape, dt):
    return Buf(nc.dram_tensor(_uniq(name), shape, dt).ap())


def _bf16():
    import ml_dtypes
    return ml_dtypes.bfloat16


def t_(x):
    return np.ascontiguousarray(np.asarray(x).T)


def vec8(x):
    x = np.asarray(x)
    return np.ascontiguousarray(x.reshape(-1, 128).T)


def brow(x, rows=128):
    x = np.asarray(x)
    return np.ascontiguousarray(np.broadcast_to(x[None, :], (rows, x.shape[0])))


def alloc_common(nc, es, proj_pools=True):
    C = {}
    C["psS"] = psum(nc, es, "psS", [128, 2048], F32)
    C["psS1"] = C["psS"].alias(C["psS"].ap[:, 0:1024])
    C["psS2"] = C["psS"].alias(C["psS"].ap[:, 1024:2048])
    C["psT1"] = psum(nc, es, "psT1", [128, 1024], BF16)
    C["psT2"] = psum(nc, es, "psT2", [128, 1024], BF16)
    C["ps0"] = psum(nc, es, "ps0", [128, 512], F32)
    C["ps1"] = psum(nc, es, "ps1", [128, 512], F32)
    C["stA"] = [sbuf(nc, es, f"stA{i}", [128, 1024], F32) for i in range(2)]
    C["stB"] = [sbuf(nc, es, f"stB{i}", [128, 1024], BF16) for i in range(2)]
    C["stC"] = [sbuf(nc, es, f"stC{i}", [128, 1024], BF16) for i in range(2)]
    C["stD"] = [sbuf(nc, es, f"stD{i}", [128, 1024], F32) for i in range(2)]
    C["stE"] = [sbuf(nc, es, f"stE{i}", [128, 1024], F32) for i in range(2)]
    C["sm"] = [sbuf(nc, es, f"sm{i}", [128, 8], F32) for i in range(4)]
    C["ident"] = sbuf(nc, es, "ident_sb", [128, 128], BF16)
    if proj_pools:
        C["rhsP"] = [[sbuf(nc, es, f"rhsP{i}_{c}", [128, 512], BF16) for c in range(NF)] for i in range(2)]
        C["lhsP"] = [[sbuf(nc, es, f"lhsP{i}_{c}", [128, 128], BF16) for c in range(NF)] for i in range(2)]
        C["vstg"] = [sbuf(nc, es, f"vstg{i}", [128, 520], BF16) for i in range(2)]
    return C


def load_ident(P, C, ident_d):
    op = P.emit("gpsimd", lambda e, o=C["ident"].ap, i=ident_d: e.dma_start(out=o[:, :], in_=i, accum_op=OP.bypass), dma=True)
    C["ident"].did_write(op)


def tr128(P, C, src, ssl, dst, dsl, extra=()):
    idb = C["ident"]
    op = P.emit("tensor",
                lambda e, o=dst.ap, s=src.ap, i=idb.ap, a=dsl, b=ssl:
                e.transpose(o[:, a:a + 128], s[:, b:b + 128], i[:, :]),
                deps=list(extra) + src.rd() + idb.rd() + dst.wr())
    src.did_read(op)
    idb.did_read(op)
    dst.did_write(op, accum=True)
    return op


def load_w(P, nc, es, name, w_dram, cin, cout):
    buf = sbuf(nc, es, name, [128, (cin // 128) * cout], BF16)
    for c in range(cin // 128):
        op = P.emit("gpsimd",
                    lambda e, o=buf.ap, i=w_dram, c=c, co=cout:
                    e.dma_start(out=o[:, c * co:(c + 1) * co], in_=i[c * 128:(c + 1) * 128, :], accum_op=OP.bypass),
                    deps=buf.r, dma=True)
        buf.did_write(op, accum=True)
    return buf


def load_f32(P, nc, es, name, v_dram, shape, eng="sync"):
    buf = sbuf(nc, es, name, shape, F32)
    op = P.emit(eng, lambda e, o=buf.ap, i=v_dram: e.dma_start(out=o[:, :], in_=i), deps=buf.wr(), dma=True)
    buf.did_write(op)
    return buf


def load_bf16(P, nc, es, name, v_dram, shape):
    buf = sbuf(nc, es, name, shape, BF16)
    op = P.emit("gpsimd", lambda e, o=buf.ap, i=v_dram: e.dma_start(out=o[:, :], in_=i, accum_op=OP.bypass), deps=buf.wr(), dma=True)
    buf.did_write(op)
    return buf


def emit_norm(P, C, src, xnT, xn_tok, kind="ln", gamma_bc=None, eps=1e-5,
              n_tok=T, src_bf16=False):
    psT = [C["psT1"], C["psT2"]]
    for t in range(n_tok // 128):
        a = C["stA"][t % 2]
        d2 = C["stD"][t % 2]
        b = C["stB"][t % 2]
        c = C["stC"][t % 2]
        s = C["sm"][t % 4]
        if src_bf16:
            ld = P.emit("gpsimd", lambda e, o=a.ap, i=src.ap, t=t: e.dma_start(out=o[:, :], in_=i[t * 128:(t + 1) * 128, :], accum_op=OP.bypass), deps=a.wr() + src.rd(), dma=True)
        else:
            ld = P.emit("sync", lambda e, o=a.ap, i=src.ap, t=t: e.dma_start(out=o[:, :], in_=i[t * 128:(t + 1) * 128, :]), deps=a.wr() + src.rd(), dma=True)
        a.did_write(ld)
        src.did_read(ld)
        o_sq = P.emit("scalar", lambda e, o=d2.ap, i=a.ap: e.activation(o[:, :], i[:, :], AF.Square), deps=a.rd() + d2.wr())
        d2.did_write(o_sq)
        a.did_read(o_sq)
        o_ss = P.emit("vector", lambda e, s=s.ap, i=d2.ap: e.reduce_sum(s[:, 1:2], i[:, :], axis=AX), deps=[o_sq] + s.wr())
        s.did_write(o_ss)
        d2.did_read(o_ss)
        if kind == "ln":
            o_sm = P.emit("vector", lambda e, s=s.ap, i=a.ap: e.reduce_sum(s[:, 0:1], i[:, :], axis=AX), deps=a.rd() + [o_ss])
            a.did_read(o_sm)
            o_nm = P.emit("vector", lambda e, s=s.ap: e.tensor_scalar(s[:, 2:3], s[:, 0:1], -1.0 / D, None, OP.mult), deps=[o_sm])
            o_m2 = P.emit("vector", lambda e, s=s.ap: e.tensor_tensor(s[:, 3:4], s[:, 2:3], s[:, 2:3], OP.mult), deps=[o_nm])
            o_s4 = P.emit("vector", lambda e, s=s.ap: e.tensor_scalar(s[:, 4:5], s[:, 1:2], 1.0 / D, None, OP.mult), deps=[o_ss, o_m2])
            o_v = P.emit("vector", lambda e, s=s.ap: e.tensor_tensor(s[:, 4:5], s[:, 4:5], s[:, 3:4], OP.subtract), deps=[o_s4])
            var_col, pre = 4, [o_v]
        else:
            o_v = P.emit("vector", lambda e, s=s.ap: e.tensor_scalar(s[:, 4:5], s[:, 1:2], 1.0 / D, None, OP.mult), deps=[o_ss])
            var_col, pre = 4, [o_v]
        o_ep = P.emit("vector", lambda e, s=s.ap, vc=var_col, eps=eps: e.tensor_scalar(s[:, 7:8], s[:, vc:vc + 1], eps, None, OP.add), deps=pre)
        o_sd = P.emit("scalar", lambda e, s=s.ap: e.activation(s[:, 5:6], s[:, 7:8], AF.Sqrt), deps=[o_ep])
        o_r = P.emit("vector", lambda e, s=s.ap: e.reciprocal(s[:, 6:7], s[:, 5:6]), deps=[o_sd])
        s.did_write(o_r)
        if kind == "ln":
            o_c = P.emit("vector", lambda e, o=d2.ap, i=a.ap, s=s.ap: e.tensor_scalar(o[:, :], i[:, :], s[:, 2:3], None, OP.add), deps=a.rd() + [o_r] + d2.wr() + [o_ss])
            d2.did_write(o_c)
            a.did_read(o_c)
            o_n = P.emit("vector", lambda e, o=b.ap, i=d2.ap, s=s.ap: e.tensor_scalar(o[:, :], i[:, :], s[:, 6:7], None, OP.mult), deps=[o_c, o_r] + b.wr())
            b.did_write(o_n)
            d2.did_read(o_n)
            s.did_read(o_n)
        else:
            if gamma_bc is None:
                o_n = P.emit("vector", lambda e, o=b.ap, i=a.ap, s=s.ap: e.tensor_scalar(o[:, :], i[:, :], s[:, 6:7], None, OP.mult), deps=a.rd() + [o_r] + b.wr())
            else:
                o_n = P.emit("vector", lambda e, o=b.ap, i=a.ap, s=s.ap, g=gamma_bc.ap:
                             e.scalar_tensor_tensor(o[:, :], i[:, :], s[:, 6:7], g[:, :], OP.mult, OP.mult),
                             deps=a.rd() + [o_r] + b.wr() + gamma_bc.rd())
                gamma_bc.did_read(o_n)
            b.did_write(o_n)
            a.did_read(o_n)
            s.did_read(o_n)
        if xn_tok is not None:
            st = P.emit("scalar", lambda e, i=b.ap, o=xn_tok.ap, t=t: e.dma_start(out=o[t * 128:(t + 1) * 128, :], in_=i[:, :]), deps=b.rd() + [o_n] + xn_tok.r, dma=True)
            b.did_read(st)
            xn_tok.did_write(st, accum=True)
        pt = psT[t % 2]
        trs = [tr128(P, C, b, f * 128, pt, f * 128, extra=[o_n]) for f in range(NF)]
        ev = P.emit("vector", lambda e, o=c.ap, i=pt.ap: e.tensor_copy(o[:, :], i[:, :]), deps=[trs[-1]] + c.wr())
        c.did_write(ev)
        pt.did_read(ev)
        for f in range(NF):
            st = P.emit("scalar",
                        lambda e, i=c.ap, o=xnT.ap, t=t, f=f:
                        e.dma_start(out=o[f * 128:(f + 1) * 128, PAD + t * 128:PAD + (t + 1) * 128], in_=i[:, f * 128:(f + 1) * 128]),
                        deps=c.rd() + [ev] + xnT.r, dma=True)
            c.did_read(st)
            xnT.did_write(st, accum=True)


def emit_zero_pad(P, C, targets):
    z = C["stB"][0]
    zop = P.emit("gpsimd", lambda e, z=z.ap: e.memset(z[:, 0:PAD], 0.0), deps=z.wr())
    z.did_write(zop)
    for d, rows in targets:
        for f in range(rows // 128):
            op = P.emit("gpsimd", lambda e, z=z.ap, d=d.ap, f=f: e.dma_start(out=d[f * 128:(f + 1) * 128, 0:PAD], in_=z[:, 0:PAD]), deps=z.rd() + d.r, dma=True)
            z.did_read(op)
            d.did_write(op, accum=True)


def proj_fm(P, C, src, w_sb, dst, n_out, bias_vec=None, scale=1.0, act=None,
            cin=D, n_tok=T, src_pad=PAD, dst_pad=0, post_cb=None):
    ncin = cin // 128
    nout = n_out // 128
    ps = [C["ps0"], C["ps1"]]
    af = act or AF.Copy
    for tc in range(n_tok // 512):
        rh = C["rhsP"][tc % 2]
        for c in range(ncin):
            op = P.emit("sync", lambda e, o=rh[c].ap, i=src.ap, c=c, tc=tc, sp=src_pad:
                        e.dma_start(out=o[:, :], in_=i[c * 128:(c + 1) * 128, sp + tc * 512: sp + (tc + 1) * 512]),
                        deps=rh[c].wr() + src.rd(), dma=True)
            rh[c].did_write(op)
            src.did_read(op)
        for do in range(nout):
            p = ps[do % 2]
            mms = []
            for c in range(ncin):
                op = P.emit("tensor",
                            lambda e, p=p.ap, w=w_sb.ap, r=rh[c].ap, c=c, do=do, no=n_out, first=(c == 0), last=(c == ncin - 1):
                            e.matmul(p[:, :], w[:, c * no + do * 128: c * no + do * 128 + 128], r[:, :], start=first, stop=last),
                            deps=(p.wr() if c == 0 else []) + w_sb.rd() + rh[c].rd() + (mms[-1:] if mms else []))
                mms.append(op)
                rh[c].did_read(op)
                w_sb.did_read(op)
            p.did_write(mms[-1])
            eb = C["stC"][do % 2]
            deps = [mms[-1]] + eb.wr() + (bias_vec.rd() if bias_vec is not None else [])
            if bias_vec is not None and act is None:
                s2 = None if scale == 1.0 else scale
                o1 = None if s2 is None else OP.mult
                op = P.emit("vector", lambda e, o=eb.ap, i=p.ap, b=bias_vec.ap, do=do, s2=s2, o1=o1:
                            e.tensor_scalar(o[:, 0:512], i[:, :], b[:, do:do + 1], s2, OP.add, o1) if o1 is not None
                            else e.tensor_scalar(o[:, 0:512], i[:, :], b[:, do:do + 1], None, OP.add), deps=deps)
                bias_vec.did_read(op)
            elif bias_vec is not None:
                op = P.emit("scalar", lambda e, o=eb.ap, i=p.ap, b=bias_vec.ap, do=do, af=af, sc=scale:
                            e.activation(o[:, 0:512], i[:, :], af, bias=b[:, do:do + 1], scale=sc), deps=deps)
                bias_vec.did_read(op)
            else:
                op = P.emit("scalar", lambda e, o=eb.ap, i=p.ap, af=af, sc=scale:
                            e.activation(o[:, 0:512], i[:, :], af, scale=sc), deps=deps)
            eb.did_write(op)
            p.did_read(op)
            if post_cb is not None:
                eb, op = post_cb(do, tc, eb, op)
            st = P.emit("scalar", lambda e, i=eb.ap, o=dst.ap, do=do, tc=tc, dp=dst_pad:
                        e.dma_start(out=o[do * 128:(do + 1) * 128, dp + tc * 512: dp + (tc + 1) * 512], in_=i[:, 0:512]),
                        deps=eb.rd() + [op] + dst.r, dma=True)
            eb.did_read(st)
            dst.did_write(st, accum=True)


def proj_tm(P, C, srcT, w_sb, dst_cb, cin=D, cout=D, n_tok=T, src_pad=PAD):
    ncin = cin // 128
    ndc = cout // 512
    ps = [C["ps0"], C["ps1"]]
    for tt in range(n_tok // 128):
        lh = C["lhsP"][tt % 2]
        for c in range(ncin):
            op = P.emit("sync", lambda e, o=lh[c].ap, i=srcT.ap, c=c, tt=tt, sp=src_pad:
                        e.dma_start(out=o[:, :], in_=i[c * 128:(c + 1) * 128, sp + tt * 128: sp + (tt + 1) * 128]),
                        deps=lh[c].wr() + srcT.rd(), dma=True)
            lh[c].did_write(op)
            srcT.did_read(op)
        for dc in range(ndc):
            p = ps[dc % 2]
            mms = []
            for c in range(ncin):
                op = P.emit("tensor",
                            lambda e, p=p.ap, l=lh[c].ap, w=w_sb.ap, c=c, dc=dc, co=cout, first=(c == 0), last=(c == ncin - 1):
                            e.matmul(p[:, :], l[:, :], w[:, c * co + dc * 512: c * co + (dc + 1) * 512], start=first, stop=last),
                            deps=(p.wr() if c == 0 else []) + lh[c].rd() + w_sb.rd() + (mms[-1:] if mms else []))
                mms.append(op)
                lh[c].did_read(op)
                w_sb.did_read(op)
            p.did_write(mms[-1])
            dst_cb(tt, dc, p, mms[-1])


def emit_attention(P, C, nc, es, qT, kT, vaug, oT, masks, tag):
    NH = 16
    psS = C["psS"]
    psT = [C["psT1"], C["psT2"]]
    psAV = C["ps0"]
    psB = C["ps1"]
    qsb = sbuf(nc, es, f"{tag}q", [128, 4096], BF16)
    ksb = sbuf(nc, es, f"{tag}k", [128, 16384], BF16)
    vsb = [sbuf(nc, es, f"{tag}v0", [128, 1040], BF16)] * 2
    Psb = sbuf(nc, es, f"{tag}p", [128, 2048], BF16)
    PT = sbuf(nc, es, f"{tag}pt", [128, 8192], BF16)
    Orun = [sbuf(nc, es, f"{tag}or{h}", [65, 512], F32) for h in range(NH)]
    stats = [sbuf(nc, es, f"{tag}st{h}", [128, 16], F32) for h in range(NH)]
    a4c = sbuf(nc, es, f"{tag}a4c", [128, 4], BF16)
    a4r = sbuf(nc, es, f"{tag}a4r", [4, 128], BF16)
    abc = C["stB"][1].alias(C["stB"][1].ap[0:65, 0:512])
    tmpo = C["stE"][0].alias(C["stE"][0].ap[0:65, 0:512])
    lrec = C["stB"][0].alias(C["stB"][0].ap[0:65, 0:512])
    ofin = C["stC"][0].alias(C["stC"][0].ap[0:65, 0:512])
    sel4 = C["sel4"]
    sel64 = C["sel64"]
    mz = P.emit("gpsimd", lambda e, o=tmpo.ap: e.memset(o[:, :], 0.0), deps=tmpo.wr())
    tmpo.did_write(mz)
    mz2 = P.emit("gpsimd", lambda e, o=lrec.ap: e.memset(o[:, :], 0.0), deps=lrec.wr())
    lrec.did_write(mz2)
    for qb in range(8):
        kt = qb + 1
        nch = (kt + 3) // 4
        lq = P.emit("sync", lambda e, o=qsb.ap, i=qT.ap, qb=qb:
                    e.dma_start(out=o[:, :].rearrange("p (f w) -> p f w", w=512),
                                in_=i[:, qb * 512:(qb + 1) * 512].rearrange("(f p) w -> p f w", p=128)),
                    deps=qsb.wr() + qT.rd(), dma=True)
        qsb.did_write(lq)
        qT.did_read(lq)
        for ch in range(nch):
            tiles = min(4, kt - 4 * ch)
            cw = tiles * 512
            nks = tiles * 4
            lk = P.emit("sync", lambda e, o=ksb.ap, i=kT.ap, ch=ch, cw=cw:
                        e.dma_start(out=o[:, :].rearrange("p (f w) -> p f w", w=2048)[:, :, 0:cw],
                                    in_=i[:, ch * 2048: ch * 2048 + cw].rearrange("(f p) w -> p f w", p=128)),
                        deps=ksb.wr() + kT.rd(), dma=True)
            ksb.did_write(lk)
            kT.did_read(lk)
            for h in range(NH):
                fq = h // 2
                ro = (h % 2) * 64
                S = stats[h]
                for j in range(4):
                    mms = []
                    for ti in range(tiles):
                        op = P.emit("tensor",
                                    lambda e, p=psS.ap, q=qsb.ap, k=ksb.ap, fq=fq, ro=ro, j=j, ti=ti:
                                    e.matmul(p[:, ti * 512:(ti + 1) * 512],
                                             q[ro:ro + 64, fq * 512 + j * 128: fq * 512 + (j + 1) * 128],
                                             k[ro:ro + 64, fq * 2048 + ti * 512: fq * 2048 + (ti + 1) * 512],
                                             start=True, stop=True),
                                    deps=psS.wr() + qsb.rd() + ksb.rd() + (mms[-1:] if mms else []))
                        mms.append(op)
                        qsb.did_read(op)
                        ksb.did_read(op)
                    psS.did_write(mms[-1])
                    o1 = P.emit("vector", lambda e, s=S.ap, p=psS.ap, c=4 + j, w=cw:
                                e.reduce_max(s[:, c:c + 1], p[:, 0:w], axis=AX), deps=[mms[-1]] + S.wr())
                    psS.did_read(o1)
                    S.did_write(o1, accum=True)
                    if ch == 0:
                        o2 = P.emit("vector", lambda e, s=S.ap, j=j: e.tensor_scalar(s[:, j:j + 1], s[:, 4 + j:5 + j], -1.0, None, OP.mult), deps=[o1])
                        S.did_write(o2, accum=True)
                        bias_col = j
                        pdeps = [o2]
                    else:
                        o2 = P.emit("vector", lambda e, s=S.ap, j=j: e.tensor_scalar(s[:, 4 + j:5 + j], s[:, 4 + j:5 + j], -1.0, None, OP.mult), deps=[o1])
                        o3 = P.emit("vector", lambda e, s=S.ap, j=j: e.tensor_tensor(s[:, 8 + j:9 + j], s[:, j:j + 1], s[:, 4 + j:5 + j], OP.min), deps=[o2])
                        o4 = P.emit("vector", lambda e, s=S.ap, j=j: e.tensor_tensor(s[:, 12 + j:13 + j], s[:, 8 + j:9 + j], s[:, j:j + 1], OP.subtract), deps=[o3])
                        o5 = P.emit("scalar", lambda e, s=S.ap, j=j: e.activation(s[:, 12 + j:13 + j], s[:, 12 + j:13 + j], AF.Exp), deps=[o4])
                        S.did_write(o5, accum=True)
                        bias_col = 8 + j
                        pdeps = [o5]
                    o6 = P.emit("scalar", lambda e, o=Psb.ap, p=psS.ap, s=S.ap, c=bias_col, w=cw:
                                e.activation(o[:, 0:w], p[:, 0:w], AF.Exp, bias=s[:, c:c + 1]),
                                deps=[mms[-1]] + pdeps + Psb.wr())
                    Psb.did_write(o6)
                    psS.did_read(o6)
                    S.did_read(o6)
                    if ch == nch - 1:
                        mk = masks[j]
                        lo = (tiles - 1) * 512
                        o7 = P.emit("vector", lambda e, o=Psb.ap, m=mk.ap, lo=lo:
                                    e.tensor_tensor(o[:, lo:lo + 512], o[:, lo:lo + 512], m[:, :], OP.mult),
                                    deps=[o6] + mk.rd())
                        mk.did_read(o7)
                        Psb.did_write(o7)
                    trs = []
                    for bk in range(nks):
                        pt = psT[bk // 8]
                        trs.append(tr128(P, C, Psb, bk * 128, pt, (bk % 8) * 128))
                    for half in range((nks + 7) // 8):
                        wseg = min(8, nks - half * 8)
                        ev = P.emit("vector",
                                    lambda e, o=PT.ap, i=psT[half].ap, j=j, half=half, wseg=wseg:
                                    e.tensor_copy(o[:, 0:8192].rearrange("p (ks r) -> p ks r", r=512)[:, half * 8: half * 8 + wseg, j * 128:(j + 1) * 128],
                                                  i[:, 0:wseg * 128].rearrange("p (ks r) -> p ks r", r=128)),
                                    deps=[trs[-1]] + (PT.wr() if (j == 0 and half == 0) else []))
                        PT.did_write(ev, accum=True)
                        psT[half].did_read(ev)
                vb = vsb[ch % 2]
                lv = P.emit("sync", lambda e, o=vb.ap, i=vaug.ap, ch=ch, h=h, nks=nks:
                            e.dma_start(out=o[:, 0:nks * 65].rearrange("p (c w) -> p c w", w=65),
                                        in_=i[ch * 2048: ch * 2048 + nks * 128, 65 * h: 65 * h + 65].rearrange("(c p) w -> p c w", p=128)),
                            deps=vb.wr() + vaug.rd(), dma=True)
                vb.did_write(lv)
                vaug.did_read(lv)
                avs = []
                for ks in range(nks):
                    op = P.emit("tensor",
                                lambda e, p=psAV.ap, v=vb.ap, t=PT.ap, ks=ks, first=(ks == 0), last=(ks == nks - 1):
                                e.matmul(p[0:65, :], v[:, ks * 65:(ks + 1) * 65], t[:, ks * 512:(ks + 1) * 512], start=first, stop=last),
                                deps=(psAV.wr() if ks == 0 else []) + [lv] + PT.rd() + (avs[-1:] if avs else []))
                    avs.append(op)
                    vb.did_read(op)
                    PT.did_read(op)
                psAV.did_write(avs[-1])
                orun = Orun[h]
                if ch == 0:
                    ou = P.emit("vector", lambda e, o=orun.ap, i=psAV.ap: e.tensor_copy(o[:, :], i[0:65, :]), deps=[avs[-1]] + orun.wr())
                    orun.did_write(ou)
                    psAV.did_read(ou)
                else:
                    c1 = P.emit("vector", lambda e, o=a4c.ap, s=S.ap: e.tensor_copy(o[:, 0:4], s[:, 12:16]), deps=S.rd() + a4c.wr())
                    a4c.did_write(c1)
                    S.did_read(c1)
                    c2 = P.emit("tensor", lambda e, o=psT[0].ap, i=a4c.ap, idd=C["ident"].ap:
                                e.transpose(o[0:4, 0:128], i[:, 0:4], idd[:, :]), deps=[c1] + psT[0].wr() + C["ident"].rd())
                    psT[0].did_write(c2)
                    a4c.did_read(c2)
                    c3 = P.emit("vector", lambda e, o=a4r.ap, i=psT[0].ap: e.tensor_copy(o[0:4, :], i[0:4, 0:128]), deps=[c2] + a4r.wr())
                    a4r.did_write(c3)
                    psT[0].did_read(c3)
                    bms = []
                    for j in range(4):
                        op = P.emit("tensor", lambda e, p=psB.ap, l=sel4.ap, r=a4r.ap, j=j:
                                    e.matmul(p[0:65, j * 128:(j + 1) * 128], l[0:4, j * 65:(j + 1) * 65], r[0:4, :], start=True, stop=True),
                                    deps=(psB.wr() if j == 0 else [bms[-1]]) + [c3] + sel4.rd())
                        bms.append(op)
                        sel4.did_read(op)
                        a4r.did_read(op)
                    psB.did_write(bms[-1])
                    c4 = P.emit("vector", lambda e, o=abc.ap, i=psB.ap: e.tensor_copy(o[:, :], i[0:65, :]), deps=[bms[-1]] + abc.wr())
                    abc.did_write(c4)
                    psB.did_read(c4)
                    c5 = P.emit("vector", lambda e, o=tmpo.ap, a=orun.ap, b=abc.ap: e.tensor_tensor(o[:, :], a[:, :], b[:, :], OP.mult),
                                deps=[c4] + orun.rd() + tmpo.wr())
                    tmpo.did_write(c5)
                    orun.did_read(c5)
                    abc.did_read(c5)
                    c6 = P.emit("vector", lambda e, o=orun.ap, a=tmpo.ap, b=psAV.ap: e.tensor_tensor(o[:, :], a[:, :], b[0:65, :], OP.add),
                                deps=[c5, avs[-1]] + orun.wr())
                    orun.did_write(c6)
                    tmpo.did_read(c6)
                    psAV.did_read(c6)
        for h in range(NH):
            orun = Orun[h]
            f0 = P.emit("vector", lambda e, o=tmpo.ap, i=orun.ap: e.reciprocal(o[64:65, :], i[64:65, :]), deps=orun.rd() + tmpo.wr())
            tmpo.did_write(f0)
            orun.did_read(f0)
            f1 = P.emit("vector", lambda e, o=lrec.ap, i=tmpo.ap: e.tensor_copy(o[64:65, :], i[64:65, :]), deps=[f0] + lrec.wr())
            lrec.did_write(f1)
            tmpo.did_read(f1)
            f2 = P.emit("tensor", lambda e, p=psB.ap, l=sel64.ap, r=lrec.ap:
                        e.matmul(p[0:65, :], l[0:65, 0:65], r[0:65, :], start=True, stop=True),
                        deps=psB.wr() + [f1] + sel64.rd())
            sel64.did_read(f2)
            lrec.did_read(f2)
            psB.did_write(f2)
            f3 = P.emit("vector", lambda e, o=abc.ap, i=psB.ap: e.tensor_copy(o[:, :], i[0:65, :]), deps=[f2] + abc.wr())
            abc.did_write(f3)
            psB.did_read(f3)
            f4 = P.emit("vector", lambda e, o=ofin.ap, a=orun.ap, b=abc.ap: e.tensor_tensor(o[:, :], a[:, :], b[:, :], OP.mult),
                        deps=[f3] + orun.rd() + ofin.wr())
            ofin.did_write(f4)
            orun.did_read(f4)
            abc.did_read(f4)
            st = P.emit("scalar", lambda e, i=ofin.ap, o=oT.ap, h=h, qb=qb:
                        e.dma_start(out=o[h * 64:(h + 1) * 64, qb * 512:(qb + 1) * 512], in_=i[0:64, :]),
                        deps=[f4] + oT.r, dma=True)
            ofin.did_read(st)
            oT.did_write(st, accum=True)


def emit_vproj(P, C, srcT, w_sb, bias_bc, vaug, n_tok=T, src_pad=PAD):
    ones_ops = []
    for i in range(2):
        vs = C["vstg"][i]
        op = P.emit("gpsimd", lambda e, v=vs.ap: e.memset(v[:, :], 1.0), deps=vs.wr())
        vs.did_write(op)
        ones_ops.append(op)

    def cb(tt, dc, p, mm):
        vs = C["vstg"][tt % 2]
        w_ops = []
        for h8 in range(8):
            if bias_bc is not None:
                op = P.emit("vector",
                            lambda e, o=vs.ap, i=p.ap, b=bias_bc.ap, h8=h8, dc=dc:
                            e.tensor_tensor(o[:, h8 * 65: h8 * 65 + 64], i[:, h8 * 64:(h8 + 1) * 64], b[:, dc * 512 + h8 * 64: dc * 512 + (h8 + 1) * 64], OP.add),
                            deps=[mm] + vs.wr() + bias_bc.rd())
                bias_bc.did_read(op)
            else:
                op = P.emit("vector",
                            lambda e, o=vs.ap, i=p.ap, h8=h8:
                            e.tensor_copy(o[:, h8 * 65: h8 * 65 + 64], i[:, h8 * 64:(h8 + 1) * 64]),
                            deps=[mm] + vs.wr())
            vs.did_write(op, accum=True)
            w_ops.append(op)
        p.did_read(w_ops[-1])
        st = P.emit("scalar",
                    lambda e, i=vs.ap, o=vaug.ap, tt=tt, dc=dc:
                    e.dma_start(out=o[tt * 128:(tt + 1) * 128, dc * 520:(dc + 1) * 520], in_=i[:, :]),
                    deps=w_ops + vaug.r, dma=True)
        vs.did_read(st)
        vaug.did_write(st, accum=True)

    proj_tm(P, C, srcT, w_sb, cb, cin=D, cout=D, n_tok=n_tok, src_pad=src_pad)


def emit_epilogue(P, C, x_in, aout, fout, x_out, n_tok=T):
    for t in range(n_tok // 128):
        a = C["stA"][t % 2]
        u = C["stB"][t % 2]
        v = C["stC"][t % 2]
        d = C["stD"][t % 2]
        l1 = P.emit("sync", lambda e, o=a.ap, i=x_in.ap, t=t: e.dma_start(out=o[:, :], in_=i[t * 128:(t + 1) * 128, :]), deps=a.wr() + x_in.rd(), dma=True)
        a.did_write(l1)
        x_in.did_read(l1)
        l2 = P.emit("sync", lambda e, o=u.ap, i=aout.ap, t=t: e.dma_start(out=o[:, :], in_=i[t * 128:(t + 1) * 128, :]), deps=u.wr() + aout.rd(), dma=True)
        u.did_write(l2)
        aout.did_read(l2)
        l3 = P.emit("sync", lambda e, o=v.ap, i=fout.ap, t=t: e.dma_start(out=o[:, :], in_=i[t * 128:(t + 1) * 128, :]), deps=v.wr() + fout.rd(), dma=True)
        v.did_write(l3)
        fout.did_read(l3)
        f2 = C["stE"][t % 2]
        o1 = P.emit("vector", lambda e, o=d.ap, i0=u.ap, i1=v.ap: e.tensor_tensor(o[:, :], i0[:, :], i1[:, :], OP.add), deps=[l2, l3] + d.wr())
        d.did_write(o1)
        u.did_read(o1)
        v.did_read(o1)
        o2 = P.emit("vector", lambda e, o=f2.ap, i0=d.ap, i1=a.ap: e.tensor_tensor(o[:, :], i0[:, :], i1[:, :], OP.add), deps=[o1, l1] + f2.wr())
        f2.did_write(o2)
        d.did_read(o2)
        a.did_read(o2)
        st = P.emit("scalar", lambda e, i=f2.ap, o=x_out.ap, t=t: e.dma_start(out=o[t * 128:(t + 1) * 128, :], in_=i[:, :]), deps=f2.rd() + [o2] + x_out.r, dma=True)
        f2.did_read(st)
        x_out.did_write(st, accum=True)


def final_waits(P):
    dma_ops = [i for i, op in enumerate(P.ops) if op["dma"]]
    P.emit("sync", lambda e: e.nop(), deps=dma_ops)


_WPOOL_N = [0]


def alloc_wpool(nc, es, n=2, cols=8192):
    _WPOOL_N[0] += 1
    tag = _WPOOL_N[0]
    return [sbuf(nc, es, f"wpool{tag}_{i}", [128, cols], BF16) for i in range(n)]


def load_w_into(P, wbuf, w_dram, cin, cout):
    deps0 = wbuf.wr()
    ops = []
    for c in range(cin // 128):
        op = P.emit("gpsimd",
                    lambda e, o=wbuf.ap, i=w_dram, c=c, co=cout:
                    e.dma_start(out=o[:, c * co:(c + 1) * co], in_=i[c * 128:(c + 1) * 128, :], accum_op=OP.bypass),
                    deps=deps0, dma=True)
        ops.append(op)
    wbuf.w = ops
    wbuf.r = []
    return wbuf


def emit_shift_lerp(P, C, nc, es, xnT, xrT, xkT, mur, muk):
    om_r = sbuf(nc, es, "om_r", [128, 8], F32)
    om_k = sbuf(nc, es, "om_k", [128, 8], F32)
    for src, dst in ((mur, om_r), (muk, om_k)):
        o1 = P.emit("vector", lambda e, o=dst.ap, i=src.ap: e.tensor_scalar(o[:, :], i[:, :], -1.0, None, OP.mult), deps=src.rd() + dst.wr())
        src.did_read(o1)
        o2 = P.emit("vector", lambda e, o=dst.ap: e.tensor_scalar(o[:, :], o[:, :], 1.0, None, OP.add), deps=[o1])
        dst.did_write(o2)
    for f in range(NF):
        for tc in range(T // 512):
            b = C["stB"][tc % 2]
            d = C["stD"][tc % 2]
            c = C["stC"][tc % 2]
            ld = P.emit("sync", lambda e, o=b.ap, i=xnT.ap, f=f, tc=tc:
                        e.dma_start(out=o[:, 0:513], in_=i[f * 128:(f + 1) * 128, PAD + tc * 512 - 1: PAD + (tc + 1) * 512]),
                        deps=b.wr() + xnT.rd(), dma=True)
            b.did_write(ld)
            xnT.did_read(ld)
            for (mu, om, half, dstd) in ((mur, om_r, 0, xrT), (muk, om_k, 1, xkT)):
                o1 = P.emit("vector", lambda e, o=d.ap, i=b.ap, m=om.ap, f=f:
                            e.tensor_scalar(o[:, 0:512], i[:, 0:512], m[:, f:f + 1], None, OP.mult),
                            deps=b.rd() + om.rd() + d.wr())
                d.did_write(o1)
                b.did_read(o1)
                om.did_read(o1)
                o2 = P.emit("vector", lambda e, o=c.ap, i=b.ap, m=mu.ap, d2=d.ap, f=f, half=half:
                            e.scalar_tensor_tensor(o[:, half * 512:half * 512 + 512], i[:, 1:513], m[:, f:f + 1], d2[:, 0:512], OP.mult, OP.add),
                            deps=[o1] + b.rd() + mu.rd() + c.wr())
                c.did_write(o2, accum=True)
                b.did_read(o2)
                mu.did_read(o2)
                d.did_read(o2)
                st = P.emit("scalar", lambda e, i=c.ap, o=dstd.ap, f=f, tc=tc, half=half:
                            e.dma_start(out=o[f * 128:(f + 1) * 128, tc * 512:(tc + 1) * 512], in_=i[:, half * 512:half * 512 + 512]),
                            deps=[o2] + dstd.r, dma=True)
                c.did_read(st)
                dstd.did_write(st, accum=True)


def emit_layer0(nc, es, P, C, pfx, x_in, y, masks_d):
    di = lambda n, s: Buf(nc.dram_tensor(pfx + n, s, F32, kind="ExternalInput").ap())
    wq_d = di("wq", [D, D]); wk_d = di("wk", [D, D]); wv_d = di("wv", [D, D]); wo_d = di("wo", [D, D])
    bq_d = di("bq", [128, 8]); bk_d = di("bk", [128, 8])
    bv_d = di("bvb", [128, D]); bo_d = di("bob", [128, D])
    wrT_d = di("wrT", [D, D]); wkT_d = di("wkT", [D, D]); wvT_d = di("wvT", [D, D])
    mur_d = di("mur", [128, 8]); muk_d = di("muk", [128, 8])

    xnT = dbuf(nc, "xnT", [D, PAD + T], BF16)
    xn_tok = dbuf(nc, "xn_tok", [T, D], BF16)
    qT = dbuf(nc, "qT", [D, T], BF16)
    kT = dbuf(nc, "kT", [D, T], BF16)
    vaug = dbuf(nc, "vaug", [T, 1040], BF16)
    oT = dbuf(nc, "oT", [D, T], BF16)
    aout = dbuf(nc, "aout", [T, D], BF16)
    fout = dbuf(nc, "fout", [T, D], BF16)
    xrT = dbuf(nc, "xrT", [D, T], BF16)
    xkT = dbuf(nc, "xkT", [D, T], BF16)
    kact = dbuf(nc, "kact", [D, T], BF16)
    r_d = dbuf(nc, "r_d", [T, D], BF16)

    if True:
        W = alloc_wpool(nc, es, 2)
        masks = []
        for m in range(4):
            mb = sbuf(nc, es, f"mask{m}", [128, 512], BF16)
            op = P.emit("gpsimd", lambda e, o=mb.ap, i=masks_d.ap, m=m: e.dma_start(out=o[:, :], in_=i[m * 128:(m + 1) * 128, :], accum_op=OP.bypass), deps=mb.wr() + masks_d.rd(), dma=True)
            mb.did_write(op)
            masks_d.did_read(op)
            masks.append(mb)
        emit_zero_pad(P, C, [(xnT, D)])
        emit_norm(P, C, x_in, xnT, xn_tok, kind="ln")
        bq = load_f32(P, nc, es, "bq_s", bq_d.ap, [128, 8])
        bqs = sbuf(nc, es, "bqs", [128, 8], F32)
        o = P.emit("vector", lambda e, o=bqs.ap, i=bq.ap: e.tensor_scalar(o[:, :], i[:, :], 0.125, None, OP.mult), deps=bq.rd() + bqs.wr())
        bq.did_read(o)
        bqs.did_write(o)
        bk = load_f32(P, nc, es, "bk_s", bk_d.ap, [128, 8])
        load_w_into(P, W[0], wq_d.ap, D, D)
        proj_fm(P, C, xnT, W[0], qT, D, bias_vec=bqs, scale=0.125)
        load_w_into(P, W[1], wk_d.ap, D, D)
        proj_fm(P, C, xnT, W[1], kT, D, bias_vec=bk, scale=1.0)
        bvb = load_f32(P, nc, es, "bv_s", bv_d.ap, [128, D])
        load_w_into(P, W[0], wv_d.ap, D, D)
        emit_vproj(P, C, xnT, W[0], bvb, vaug)
        emit_attention(P, C, nc, es, qT, kT, vaug, oT, masks, tag="a0")
        bob = load_f32(P, nc, es, "bo_s", bo_d.ap, [128, D])
        load_w_into(P, W[1], wo_d.ap, D, D)

        def cb_o(tt, dc, p, mm):
            eb = C["stC"][dc % 2]
            o1 = P.emit("vector", lambda e, o=eb.ap, i=p.ap, b=bob.ap, dc=dc:
                        e.tensor_tensor(o[:, 0:512], i[:, :], b[:, dc * 512:(dc + 1) * 512], OP.add),
                        deps=[mm] + eb.wr() + bob.rd())
            bob.did_read(o1)
            eb.did_write(o1)
            p.did_read(o1)
            st = P.emit("scalar", lambda e, i=eb.ap, o=aout.ap, tt=tt, dc=dc:
                        e.dma_start(out=o[tt * 128:(tt + 1) * 128, dc * 512:(dc + 1) * 512], in_=i[:, 0:512]),
                        deps=[o1] + aout.r, dma=True)
            eb.did_read(st)
            aout.did_write(st, accum=True)

        proj_tm(P, C, oT, W[1], cb_o, src_pad=0)
        mur = load_f32(P, nc, es, "mur_s", mur_d.ap, [128, 8])
        muk = load_f32(P, nc, es, "muk_s", muk_d.ap, [128, 8])
        emit_shift_lerp(P, C, nc, es, xnT, xrT, xkT, mur, muk)
        load_w_into(P, W[0], wrT_d.ap, D, D)

        def cb_r(tt, dc, p, mm):
            eb = C["stC"][dc % 2]
            o1 = P.emit("scalar", lambda e, o=eb.ap, i=p.ap: e.activation(o[:, 0:512], i[:, :], AF.Sigmoid), deps=[mm] + eb.wr())
            eb.did_write(o1)
            p.did_read(o1)
            st = P.emit("scalar", lambda e, i=eb.ap, o=r_d.ap, tt=tt, dc=dc:
                        e.dma_start(out=o[tt * 128:(tt + 1) * 128, dc * 512:(dc + 1) * 512], in_=i[:, 0:512]),
                        deps=[o1] + r_d.r, dma=True)
            eb.did_read(st)
            r_d.did_write(st, accum=True)

        proj_tm(P, C, xrT, W[0], cb_r, src_pad=0)
        load_w_into(P, W[1], wkT_d.ap, D, D)

        def cb_sq(do, tc, eb, op):
            nb = C["stB"][do % 2]
            o1 = P.emit("scalar", lambda e, o=nb.ap, i=eb.ap: e.activation(o[:, 0:512], i[:, 0:512], AF.Square), deps=[op] + nb.wr())
            nb.did_write(o1)
            eb.did_read(o1)
            return nb, o1

        proj_fm(P, C, xrT if False else xkT, W[1], kact, D, act=AF.Relu, src_pad=0, post_cb=cb_sq)
        load_w_into(P, W[0], wvT_d.ap, D, D)
        rtile = [sbuf(nc, es, f"rt{i}", [128, 512], BF16) for i in range(2)]

        def cb_v(tt, dc, p, mm):
            rb = rtile[dc % 2]
            ld = P.emit("sync", lambda e, o=rb.ap, i=r_d.ap, tt=tt, dc=dc:
                        e.dma_start(out=o[:, :], in_=i[tt * 128:(tt + 1) * 128, dc * 512:(dc + 1) * 512]),
                        deps=rb.wr() + r_d.rd(), dma=True)
            rb.did_write(ld)
            r_d.did_read(ld)
            eb = C["stC"][dc % 2]
            o1 = P.emit("vector", lambda e, o=eb.ap, i=p.ap, r=rb.ap:
                        e.tensor_tensor(o[:, 0:512], i[:, :], r[:, :], OP.mult),
                        deps=[mm, ld] + eb.wr())
            eb.did_write(o1)
            rb.did_read(o1)
            p.did_read(o1)
            st = P.emit("scalar", lambda e, i=eb.ap, o=fout.ap, tt=tt, dc=dc:
                        e.dma_start(out=o[tt * 128:(tt + 1) * 128, dc * 512:(dc + 1) * 512], in_=i[:, 0:512]),
                        deps=[o1] + fout.r, dma=True)
            eb.did_read(st)
            fout.did_write(st, accum=True)

        proj_tm(P, C, kact, W[0], cb_v, src_pad=0)
        emit_epilogue(P, C, x_in, aout, fout, y)


def _build_single(emitter, proj_pools=True):
    _reset_build_state()
    nc = bass.Bass("TRN2", target_bir_lowering=False)
    x_in = Buf(nc.dram_tensor("x", [T, D], F32, kind="ExternalInput").ap())
    ident_d = Buf(nc.dram_tensor("ident", [128, 128], F32, kind="ExternalInput").ap())
    masks_d = Buf(nc.dram_tensor("masks", [512, 512], F32, kind="ExternalInput").ap())
    y = Buf(nc.dram_tensor("y", [T, D], F32, kind="ExternalOutput").ap())
    sel4_d = Buf(nc.dram_tensor("sel4", [4, 260], F32, kind="ExternalInput").ap())
    sel64_d = Buf(nc.dram_tensor("sel64", [65, 65], F32, kind="ExternalInput").ap())
    es = ExitStack()
    with es:
        P = Prog(nc)
        C = alloc_common(nc, es, proj_pools=proj_pools)
        load_ident(P, C, ident_d.ap)
        C["sel4"] = load_bf16(P, nc, es, "sel4_s", sel4_d.ap, [4, 260])
        C["sel64"] = load_bf16(P, nc, es, "sel64_s", sel64_d.ap, [65, 65])
        eL = ExitStack()
        with eL:
            emitter(nc, eL, P, C, "", x_in, y, masks_d)
        final_waits(P)
        P.finalize(es)
    return nc


def build_layer0():
    return _build_single(emit_layer0)


def _l0_consts():
    ident = np.eye(128, dtype=np.float32)
    i512 = np.arange(512, dtype=np.int64)[None, :]
    i128 = np.arange(128, dtype=np.int64)[:, None]
    masks = np.concatenate([(i512 <= 128 * m + i128).astype(np.float32) for m in range(4)], axis=0)
    return ident, masks


def _sel_consts():
    sel4 = np.zeros((4, 260), np.float32)
    for j in range(4):
        sel4[j, j * 65:(j + 1) * 65] = 1.0
    sel64 = np.zeros((65, 65), np.float32)
    sel64[64, :] = 1.0
    return sel4, sel64


_SEL4, _SEL64 = _sel_consts()


def prep_l0_inputs(ins, xb):
    ident, masks = _l0_consts()
    return {
        "x": np.ascontiguousarray(xb, dtype=np.float32),
        "ident": ident, "masks": masks,
        "sel4": _SEL4, "sel64": _SEL64,
        "wq": np.asarray(ins["L0_attn_wq"], np.float32),
        "wk": np.asarray(ins["L0_attn_wk"], np.float32),
        "wv": np.asarray(ins["L0_attn_wv"], np.float32),
        "wo": np.asarray(ins["L0_attn_wo"], np.float32),
        "bq": vec8(np.asarray(ins["L0_attn_bq"], np.float32)),
        "bk": vec8(np.asarray(ins["L0_attn_bk"], np.float32)),
        "bvb": brow(np.asarray(ins["L0_attn_bv"], np.float32)),
        "bob": brow(np.asarray(ins["L0_attn_bo"], np.float32)),
        "wrT": t_(np.asarray(ins["L0_ffn_W_r"], np.float32)),
        "wkT": t_(np.asarray(ins["L0_ffn_W_k"], np.float32)),
        "wvT": t_(np.asarray(ins["L0_ffn_W_v"], np.float32)),
        "mur": vec8(np.asarray(ins["L0_ffn_mu_r"], np.float32)),
        "muk": vec8(np.asarray(ins["L0_ffn_mu_k"], np.float32)),
    }





def make_out_cb(P, C, dst, bias_bc):
    def cb(tt, dc, p, mm):
        eb = C["stC"][dc % 2]
        o1 = P.emit("vector", lambda e, o=eb.ap, i=p.ap, b=bias_bc.ap, dc=dc:
                    e.tensor_tensor(o[:, 0:512], i[:, :], b[:, dc * 512:(dc + 1) * 512], OP.add),
                    deps=[mm] + eb.wr() + bias_bc.rd())
        bias_bc.did_read(o1)
        eb.did_write(o1)
        p.did_read(o1)
        st = P.emit("scalar", lambda e, i=eb.ap, o=dst.ap, tt=tt, dc=dc:
                    e.dma_start(out=o[tt * 128:(tt + 1) * 128, dc * 512:(dc + 1) * 512], in_=i[:, 0:512]),
                    deps=[o1] + dst.r, dma=True)
        eb.did_read(st)
        dst.did_write(st, accum=True)
    return cb


def emit_elu1(P, C, buf, op, width=512):
    d = C["stD"][0]
    ee = C["stE"][0]
    o1 = P.emit("vector", lambda e, o=d.ap, i=buf.ap, w=width: e.tensor_scalar(o[:, 0:w], i[:, 0:w], 0.0, None, OP.min), deps=buf.rd() + [op] + d.wr())
    d.did_write(o1)
    buf.did_read(o1)
    o2 = P.emit("scalar", lambda e, o=ee.ap, i=d.ap, w=width: e.activation(o[:, 0:w], i[:, 0:w], AF.Exp), deps=[o1] + ee.wr())
    ee.did_write(o2)
    d.did_read(o2)
    o3 = P.emit("vector", lambda e, o=buf.ap, i=buf.ap, x=ee.ap, w=width:
                e.scalar_tensor_tensor(o[:, 0:w], i[:, 0:w], 0.0, x[:, 0:w], OP.max, OP.add),
                deps=[o2, op] + buf.wr())
    buf.did_write(o3)
    ee.did_read(o3)
    return buf, o3


def emit_layer1(nc, es, P, C, pfx, x_in, y, masks_d):
    di = lambda n, s: Buf(nc.dram_tensor(pfx + n, s, F32, kind="ExternalInput").ap())
    wq_d = di("wq", [D, D]); wk_d = di("wk", [D, D]); wv_d = di("wv", [D, D]); wp_d = di("wp", [D, D])
    bq_d = di("bq", [128, 8])
    bkb_d = di("bkb", [128, D]); bvb_d = di("bvb", [128, D]); bpb_d = di("bpb", [128, D])
    wg_d = di("wg", [D, 8])
    A_d = di("Aall", [D, 128])
    Bm_d = di("Ball", [128, D])
    rep_d = di("rep", [8, 128])

    xnT = dbuf(nc, "xnT", [D, PAD + T], BF16)
    qT = dbuf(nc, "qT", [D, T], BF16)
    fk_tok = dbuf(nc, "fk_tok", [T, D], BF16)
    v_tok = dbuf(nc, "v_tok", [T, D], BF16)
    oT = dbuf(nc, "oT", [D, T], BF16)
    aout = dbuf(nc, "aout", [T, D], BF16)
    fout = dbuf(nc, "fout", [T, D], BF16)

    if True:
        W = alloc_wpool(nc, es, 2)
        emit_zero_pad(P, C, [(xnT, D)])
        emit_norm(P, C, x_in, xnT, None, kind="ln")
        bq = load_f32(P, nc, es, "bq_s", bq_d.ap, [128, 8])
        load_w_into(P, W[0], wq_d.ap, D, D)

        def cb_fq(do, tc, eb, op):
            return emit_elu1(P, C, eb, op)

        proj_fm(P, C, xnT, W[0], qT, D, bias_vec=bq, post_cb=cb_fq)
        bkb = load_f32(P, nc, es, "bkb_s", bkb_d.ap, [128, D])
        bvb = load_f32(P, nc, es, "bvb_s", bvb_d.ap, [128, D])
        load_w_into(P, W[1], wk_d.ap, D, D)

        def cb_fk(tt, dc, p, mm):
            eb = C["stC"][dc % 2]
            o1 = P.emit("vector", lambda e, o=eb.ap, i=p.ap, b=bkb.ap, dc=dc:
                        e.tensor_tensor(o[:, 0:512], i[:, :], b[:, dc * 512:(dc + 1) * 512], OP.add),
                        deps=[mm] + eb.wr() + bkb.rd())
            bkb.did_read(o1)
            eb.did_write(o1)
            p.did_read(o1)
            eb2, o2 = emit_elu1(P, C, eb, o1)
            st = P.emit("scalar", lambda e, i=eb2.ap, o=fk_tok.ap, tt=tt, dc=dc:
                        e.dma_start(out=o[tt * 128:(tt + 1) * 128, dc * 512:(dc + 1) * 512], in_=i[:, 0:512]),
                        deps=[o2] + fk_tok.r, dma=True)
            eb2.did_read(st)
            fk_tok.did_write(st, accum=True)

        proj_tm(P, C, xnT, W[1], cb_fk)
        load_w_into(P, W[0], wv_d.ap, D, D)
        proj_tm(P, C, xnT, W[0], make_out_cb(P, C, v_tok, bvb))
        kv_sb = sbuf(nc, es, "kv_sb", [64, 1024], BF16)
        kv_dram = dbuf(nc, "kv_dram", [64, 1024], BF16)
        kv2_sb = sbuf(nc, es, "kv2_sb", [128, 512], BF16)
        fk_sb = [sbuf(nc, es, f"fk{i}", [128, 1024], BF16) for i in range(2)]
        v_sb = [sbuf(nc, es, f"vv{i}", [128, 1024], BF16) for i in range(2)]
        slots = [(C["psS1"], 0), (C["psS1"], 512), (C["psS2"], 0), (C["psS2"], 512), (C["ps0"], 0), (C["ps1"], 0)]
        for wave in range(3):
            heads = list(range(wave * 6, min(16, wave * 6 + 6)))
            mm_last = {}
            for tc in range(NT):
                fk = fk_sb[tc % 2]
                vv = v_sb[tc % 2]
                l1 = P.emit("sync", lambda e, o=fk.ap, i=fk_tok.ap, tc=tc: e.dma_start(out=o[:, :], in_=i[tc * 128:(tc + 1) * 128, :]), deps=fk.wr() + fk_tok.rd(), dma=True)
                fk.did_write(l1)
                fk_tok.did_read(l1)
                l2 = P.emit("sync", lambda e, o=vv.ap, i=v_tok.ap, tc=tc: e.dma_start(out=o[:, :], in_=i[tc * 128:(tc + 1) * 128, :]), deps=vv.wr() + v_tok.rd(), dma=True)
                vv.did_write(l2)
                v_tok.did_read(l2)
                for hi, h in enumerate(heads):
                    pb, off = slots[hi]
                    op = P.emit("tensor",
                                lambda e, p=pb.ap, l=fk.ap, r=vv.ap, h=h, off=off, first=(tc == 0), last=(tc == NT - 1):
                                e.matmul(p[0:64, off:off + 64], l[:, h * 64:(h + 1) * 64], r[:, h * 64:(h + 1) * 64], start=first, stop=last),
                                deps=(pb.wr() if tc == 0 else []) + fk.rd() + vv.rd() + ([mm_last[hi]] if hi in mm_last else []))
                    mm_last[hi] = op
                    fk.did_read(op)
                    vv.did_read(op)
                    pb.did_write(op, accum=True)
            for hi, h in enumerate(heads):
                pb, off = slots[hi]
                ev = P.emit("vector", lambda e, o=kv_sb.ap, i=pb.ap, h=h, off=off:
                            e.tensor_copy(o[0:64, h * 64:(h + 1) * 64], i[0:64, off:off + 64]),
                            deps=[mm_last[hi]] + kv_sb.r)
                kv_sb.did_write(ev, accum=True)
                pb.did_read(ev)
        stkv = P.emit("scalar", lambda e, i=kv_sb.ap, o=kv_dram.ap: e.dma_start(out=o[0:64, :], in_=i[0:64, :]), deps=kv_sb.rd() + kv_dram.wr(), dma=True)
        kv_sb.did_read(stkv)
        kv_dram.did_write(stkv)
        kvv = kv_dram.ap.rearrange("d (f par c) -> d f par c", par=2, c=64)
        for par in range(2):
            op = P.emit("sync", lambda e, o=kv2_sb.ap, i=kvv, par=par:
                        e.dma_start(out=o[par * 64:(par + 1) * 64, :].rearrange("d (f c) -> d f c", c=64), in_=i[:, :, par, :]),
                        deps=kv_dram.rd() + kv2_sb.wr() + kv2_sb.r, dma=True)
            kv_dram.did_read(op)
            kv2_sb.did_write(op, accum=True)
        fq_sb = [sbuf(nc, es, f"fq{i}", [128, 512], BF16) for i in range(2)]
        o_ev = [sbuf(nc, es, f"oev{i}", [64, 512], BF16) for i in range(2)]
        for tc in range(T // 512):
            for fp in range(NF):
                fq = fq_sb[fp % 2]
                ld = P.emit("sync", lambda e, o=fq.ap, i=qT.ap, fp=fp, tc=tc:
                            e.dma_start(out=o[:, :], in_=i[fp * 128:(fp + 1) * 128, tc * 512:(tc + 1) * 512]),
                            deps=fq.wr() + qT.rd(), dma=True)
                fq.did_write(ld)
                qT.did_read(ld)
                for sub in range(2):
                    h = fp * 2 + sub
                    pb = [C["ps0"], C["ps1"]][sub]
                    ro = sub * 64
                    mm = P.emit("tensor",
                                lambda e, p=pb.ap, l=kv2_sb.ap, r=fq.ap, fp=fp, ro=ro:
                                e.matmul(p[0:64, :], l[ro:ro + 64, fp * 64:(fp + 1) * 64], r[ro:ro + 64, :], start=True, stop=True),
                                deps=pb.wr() + kv2_sb.rd() + fq.rd())
                    kv2_sb.did_read(mm)
                    fq.did_read(mm)
                    pb.did_write(mm)
                    oe = o_ev[sub]
                    ev = P.emit("vector", lambda e, o=oe.ap, i=pb.ap: e.tensor_copy(o[0:64, :], i[0:64, :]), deps=[mm] + oe.wr())
                    oe.did_write(ev)
                    pb.did_read(ev)
                    st = P.emit("scalar", lambda e, i=oe.ap, o=oT.ap, h=h, tc=tc:
                                e.dma_start(out=o[h * 64:(h + 1) * 64, tc * 512:(tc + 1) * 512], in_=i[0:64, :]),
                                deps=[ev] + oT.r, dma=True)
                    oe.did_read(st)
                    oT.did_write(st, accum=True)
        bpb = load_f32(P, nc, es, "bpb_s", bpb_d.ap, [128, D])
        load_w_into(P, W[1], wp_d.ap, D, D)
        proj_tm(P, C, oT, W[1], make_out_cb(P, C, aout, bpb), src_pad=0)

        wg_sb = load_w(P, nc, es, "wg_sb", wg_d.ap, D, 8)
        A_sb = load_w(P, nc, es, "A_sb", A_d.ap, D, 128)
        B_sb = load_w(P, nc, es, "B_sb", Bm_d.ap, 128, D)
        rep_sb = load_bf16(P, nc, es, "rep_sb", rep_d.ap, [8, 128])
        lg_sb = sbuf(nc, es, "lg_sb", [128, 8], F32)
        m8_sb = sbuf(nc, es, "m8_sb", [128, 8], F32)
        w12 = sbuf(nc, es, "w12", [128, 2], F32)
        mk1 = sbuf(nc, es, "mk1", [128, 8], F32)
        mk2 = sbuf(nc, es, "mk2", [128, 8], F32)
        cwb = sbuf(nc, es, "cwb", [128, 128], BF16)
        cwT = sbuf(nc, es, "cwT", [8, T], BF16)
        cwrep = [sbuf(nc, es, f"cwrep{i}", [128, 512], BF16) for i in range(2)]
        hwT = [sbuf(nc, es, f"hwT{i}", [128, 512], BF16) for i in range(2)]
        for tt in range(NT):
            lh = C["lhsP"][tt % 2]
            for c in range(NF):
                op = P.emit("sync", lambda e, o=lh[c].ap, i=xnT.ap, c=c, tt=tt:
                            e.dma_start(out=o[:, :], in_=i[c * 128:(c + 1) * 128, PAD + tt * 128: PAD + (tt + 1) * 128]),
                            deps=lh[c].wr() + xnT.rd(), dma=True)
                lh[c].did_write(op)
                xnT.did_read(op)
            p = C["ps0"]
            mms = []
            for c in range(NF):
                op = P.emit("tensor", lambda e, p=p.ap, l=lh[c].ap, w=wg_sb.ap, c=c, first=(c == 0), last=(c == NF - 1):
                            e.matmul(p[:, 0:8], l[:, :], w[:, c * 8:(c + 1) * 8], start=first, stop=last),
                            deps=(p.wr() if c == 0 else []) + lh[c].rd() + wg_sb.rd() + (mms[-1:] if mms else []))
                mms.append(op)
                lh[c].did_read(op)
                wg_sb.did_read(op)
            p.did_write(mms[-1])
            o1 = P.emit("vector", lambda e, o=lg_sb.ap, i=p.ap: e.tensor_copy(o[:, :], i[:, 0:8]), deps=[mms[-1]] + lg_sb.wr())
            lg_sb.did_write(o1)
            p.did_read(o1)
            o2 = P.emit("vector", lambda e, o=m8_sb.ap, i=lg_sb.ap: e.max(out=o[:, :], in_=i[:, :]), deps=[o1] + m8_sb.wr())
            m8_sb.did_write(o2)
            lg_sb.did_read(o2)
            o3 = P.emit("vector", lambda e, o=w12.ap, i=m8_sb.ap: e.tensor_tensor(o[:, 0:1], i[:, 0:1], i[:, 1:2], OP.subtract), deps=[o2] + w12.wr())
            o4 = P.emit("vector", lambda e, o=w12.ap, i=m8_sb.ap: e.tensor_tensor(o[:, 1:2], i[:, 1:2], i[:, 0:1], OP.subtract), deps=[o2, o3])
            w12.did_write(o4)
            m8_sb.did_read(o4)
            o5 = P.emit("scalar", lambda e, o=w12.ap: e.activation(o[:, 0:1], o[:, 0:1], AF.Sigmoid), deps=[o3, o4])
            o6 = P.emit("scalar", lambda e, o=w12.ap: e.activation(o[:, 1:2], o[:, 1:2], AF.Sigmoid), deps=[o4, o5])
            w12.did_write(o6)
            o7 = P.emit("vector", lambda e, o=mk1.ap, i=lg_sb.ap, m=m8_sb.ap: e.tensor_scalar(o[:, :], i[:, :], m[:, 0:1], None, OP.is_ge), deps=[o2] + mk1.wr())
            mk1.did_write(o7)
            o8 = P.emit("vector", lambda e, o=mk2.ap, i=lg_sb.ap, m=m8_sb.ap: e.tensor_scalar(o[:, :], i[:, :], m[:, 1:2], None, OP.is_ge), deps=[o2, o7] + mk2.wr())
            mk2.did_write(o8)
            lg_sb.did_read(o8)
            m8_sb.did_read(o8)
            o9 = P.emit("vector", lambda e, o=mk2.ap, i=mk1.ap: e.tensor_tensor(o[:, :], o[:, :], i[:, :], OP.subtract), deps=[o7, o8])
            mk2.did_write(o9)
            mk1.did_read(o9)
            oa = P.emit("vector", lambda e, o=mk1.ap, w=w12.ap: e.tensor_scalar(o[:, :], o[:, :], w[:, 0:1], None, OP.mult), deps=[o9, o6])
            mk1.did_write(oa)
            w12.did_read(oa)
            ob = P.emit("vector", lambda e, o=mk2.ap, w=w12.ap: e.tensor_scalar(o[:, :], o[:, :], w[:, 1:2], None, OP.mult), deps=[o9, o6, oa])
            mk2.did_write(ob)
            w12.did_read(ob)
            oc = P.emit("vector", lambda e, o=cwb.ap, a=mk1.ap, b=mk2.ap: e.tensor_tensor(o[:, 0:8], a[:, :], b[:, :], OP.add), deps=[oa, ob] + cwb.wr())
            cwb.did_write(oc)
            mk1.did_read(oc)
            mk2.did_read(oc)
            pt = C["psT1"]
            tr = tr128(P, C, cwb, 0, pt, 0, extra=[oc])
            ev = P.emit("vector", lambda e, o=cwT.ap, i=pt.ap, tt=tt: e.tensor_copy(o[0:8, tt * 128:(tt + 1) * 128], i[0:8, 0:128]), deps=[tr] + cwT.r)
            cwT.did_write(ev, accum=True)
            pt.did_read(ev)
        for tc in range(T // 512):
            rh = C["rhsP"][tc % 2]
            for c in range(NF):
                op = P.emit("sync", lambda e, o=rh[c].ap, i=xnT.ap, c=c, tc=tc:
                            e.dma_start(out=o[:, :], in_=i[c * 128:(c + 1) * 128, PAD + tc * 512: PAD + (tc + 1) * 512]),
                            deps=rh[c].wr() + xnT.rd(), dma=True)
                rh[c].did_write(op)
                xnT.did_read(op)
            p0 = C["ps0"]
            m1 = P.emit("tensor", lambda e, p=p0.ap, l=rep_sb.ap, r=cwT.ap, tc=tc:
                        e.matmul(p[:, :], l[0:8, :], r[0:8, tc * 512:(tc + 1) * 512], start=True, stop=True),
                        deps=p0.wr() + rep_sb.rd() + cwT.rd())
            rep_sb.did_read(m1)
            cwT.did_read(m1)
            p0.did_write(m1)
            cw = cwrep[tc % 2]
            e1 = P.emit("vector", lambda e, o=cw.ap, i=p0.ap: e.tensor_copy(o[:, :], i[:, :]), deps=[m1] + cw.wr())
            cw.did_write(e1)
            p0.did_read(e1)
            pS = C["psS1"]
            mms = []
            for c in range(NF):
                op = P.emit("tensor", lambda e, p=pS.ap, w=A_sb.ap, r=rh[c].ap, c=c, first=(c == 0), last=(c == NF - 1):
                            e.matmul(p[:, 0:512], w[:, c * 128:(c + 1) * 128], r[:, :], start=first, stop=last),
                            deps=(pS.wr() if c == 0 else []) + A_sb.rd() + rh[c].rd() + (mms[-1:] if mms else []))
                mms.append(op)
                A_sb.did_read(op)
                rh[c].did_read(op)
            pS.did_write(mms[-1])
            hw = hwT[tc % 2]
            e2 = P.emit("vector", lambda e, o=hw.ap, i=pS.ap, c=cw.ap: e.tensor_tensor(o[:, :], i[:, 0:512], c[:, :], OP.mult),
                        deps=[mms[-1], e1] + hw.wr())
            hw.did_write(e2)
            pS.did_read(e2)
            cw.did_read(e2)
            for sub in range(4):
                for dc in range(2):
                    p1 = C["ps1"]
                    m2 = P.emit("tensor", lambda e, p=p1.ap, l=hw.ap, w=B_sb.ap, sub=sub, dc=dc:
                                e.matmul(p[:, :], l[:, sub * 128:(sub + 1) * 128], w[:, dc * 512:(dc + 1) * 512], start=True, stop=True),
                                deps=p1.wr() + [e2] + B_sb.rd())
                    hw.did_read(m2)
                    B_sb.did_read(m2)
                    p1.did_write(m2)
                    eb = C["stC"][sub % 2]
                    e3 = P.emit("vector", lambda e, o=eb.ap, i=p1.ap: e.tensor_copy(o[:, 0:512], i[:, :]), deps=[m2] + eb.wr())
                    eb.did_write(e3)
                    p1.did_read(e3)
                    st = P.emit("scalar", lambda e, i=eb.ap, o=fout.ap, tc=tc, sub=sub, dc=dc:
                                e.dma_start(out=o[tc * 512 + sub * 128: tc * 512 + (sub + 1) * 128, dc * 512:(dc + 1) * 512], in_=i[:, 0:512]),
                                deps=[e3] + fout.r, dma=True)
                    eb.did_read(st)
                    fout.did_write(st, accum=True)
        emit_epilogue(P, C, x_in, aout, fout, y)


def build_layer1():
    return _build_single(emit_layer1)


def prep_l1_inputs(ins, xb):
    ident, _ = _l0_consts()
    A = np.asarray(ins["L1_ffn_A"], np.float32)
    Bm = np.asarray(ins["L1_ffn_B"], np.float32)
    rep = np.zeros((8, 128), np.float32)
    for e in range(8):
        rep[e, e * 16:(e + 1) * 16] = 1.0
    return {
        "x": np.ascontiguousarray(xb, dtype=np.float32),
        "ident": ident,
        "sel4": _SEL4, "sel64": _SEL64,
        "wq": np.asarray(ins["L1_attn_Wq"], np.float32),
        "wk": np.asarray(ins["L1_attn_Wk"], np.float32),
        "wv": np.asarray(ins["L1_attn_Wv"], np.float32),
        "wp": np.asarray(ins["L1_attn_Wp"], np.float32),
        "bq": vec8(np.asarray(ins["L1_attn_bq"], np.float32)),
        "bkb": brow(np.asarray(ins["L1_attn_bk"], np.float32)),
        "bvb": brow(np.asarray(ins["L1_attn_bv"], np.float32)),
        "bpb": brow(np.asarray(ins["L1_attn_bp"], np.float32)),
        "wg": np.asarray(ins["L1_ffn_Wg"], np.float32),
        "Aall": np.ascontiguousarray(np.transpose(A, (1, 0, 2)).reshape(D, 128)),
        "Ball": np.ascontiguousarray(Bm.reshape(128, D)),
        "rep": rep,
    }


def emit_precast(P, C, src_d, dst_d, rows, cols):
    stg = C["stB"]
    nco = (cols + 1023) // 1024
    k = 0
    for r in range(rows // 128):
        for co in range(nco):
            cw = min(1024, cols - co * 1024)
            s = stg[k % 2]
            k += 1
            ld = P.emit("gpsimd", lambda e, o=s.ap, i=src_d.ap, r=r, co=co, cw=cw:
                        e.dma_start(out=o[:, 0:cw], in_=i[r * 128:(r + 1) * 128, co * 1024:co * 1024 + cw], accum_op=OP.bypass),
                        deps=s.wr() + src_d.rd(), dma=True)
            s.did_write(ld)
            src_d.did_read(ld)
            st = P.emit("scalar", lambda e, i=s.ap, o=dst_d.ap, r=r, co=co, cw=cw:
                        e.dma_start(out=o[r * 128:(r + 1) * 128, co * 1024:co * 1024 + cw], in_=i[:, 0:cw]),
                        deps=[ld] + dst_d.r, dma=True)
            s.did_read(st)
            dst_d.did_write(st, accum=True)


def emit_rownorm_scale(P, C, v_d, g_sb, out_sb, n_rows, n_cols, eps=1e-12):
    nco = n_cols // 1024
    for r in range(n_rows // 128):
        parts = []
        for co in range(nco):
            s = C["stB"][co % 2]
            q = C["stD"][co % 2]
            ld = P.emit("gpsimd", lambda e, o=s.ap, i=v_d.ap, r=r, co=co:
                        e.dma_start(out=o[:, :], in_=i[r * 128:(r + 1) * 128, co * 1024:(co + 1) * 1024], accum_op=OP.bypass),
                        deps=s.wr() + v_d.rd(), dma=True)
            s.did_write(ld)
            v_d.did_read(ld)
            o1 = P.emit("gpsimd", lambda e, o=q.ap, i=s.ap: e.tensor_tensor(o[:, :], i[:, :], i[:, :], OP.mult), deps=[ld] + q.wr())
            q.did_write(o1)
            s.did_read(o1)
            sm = C["sm"][r % 4]
            o2 = P.emit("vector", lambda e, o=sm.ap, i=q.ap, co=co: e.tensor_reduce(o[:, co:co + 1], i[:, :], axis=AX, op=OP.add), deps=[o1] + sm.r + (sm.w if co == 0 else [parts[-1]]))
            sm.did_write(o2, accum=(co > 0))
            q.did_read(o2)
            parts.append(o2)
        sm = C["sm"][r % 4]
        o3 = P.emit("vector", lambda e, o=out_sb.ap, i=sm.ap, r=r, nco=nco: e.reduce_sum(o[:, r:r + 1], i[:, 0:nco], axis=AX), deps=parts + out_sb.r)
        out_sb.did_write(o3, accum=True)
        sm.did_read(o3)
    o4 = P.emit("scalar", lambda e, o=out_sb.ap: e.activation(o[:, :], o[:, :], AF.Sqrt), deps=out_sb.wr() + out_sb.rd())
    out_sb.did_write(o4)
    o5 = P.emit("vector", lambda e, o=out_sb.ap, eps=eps: e.tensor_scalar(o[:, :], o[:, :], eps, None, OP.max), deps=[o4])
    out_sb.did_write(o5)
    o6 = P.emit("vector", lambda e, o=out_sb.ap: e.reciprocal(o[:, :], o[:, :]), deps=[o5])
    out_sb.did_write(o6)
    o7 = P.emit("vector", lambda e, o=out_sb.ap, g=g_sb.ap: e.tensor_tensor(o[:, :], o[:, :], g[:, :], OP.mult), deps=[o6] + g_sb.rd())
    g_sb.did_read(o7)
    out_sb.did_write(o7)


def emit_conv_taps(P, C, src_sb, dst_sb, w_sb, b_sb, wcol0, bcol0, ncols, ntaps, src_off, dst_off, np_tiles, src_ops):
    ops_out = []
    for p in range(np_tiles):
        s = src_sb[p]
        d = dst_sb[p]
        prev = None
        for k in range(ntaps):
            off = src_off - (ntaps - 1) + k
            if k == 0:
                op = P.emit("vector", lambda e, o=d.ap, i=s.ap, w=w_sb.ap, p=p, k=k, off=off, n=ncols, w0=wcol0, do=dst_off:
                            e.tensor_scalar(o[:, do:do + n], i[:, off:off + n], w[:, w0 + p * ntaps + k: w0 + p * ntaps + k + 1], None, OP.mult),
                            deps=src_ops + s.rd() + w_sb.rd() + d.wr())
            else:
                op = P.emit("vector", lambda e, o=d.ap, i=s.ap, w=w_sb.ap, p=p, k=k, off=off, n=ncols, w0=wcol0, do=dst_off:
                            e.scalar_tensor_tensor(o[:, do:do + n], i[:, off:off + n], w[:, w0 + p * ntaps + k: w0 + p * ntaps + k + 1], o[:, do:do + n], OP.mult, OP.add),
                            deps=[prev] + s.rd() + w_sb.rd())
            prev = op
            s.did_read(op)
            w_sb.did_read(op)
            d.did_write(op)
        op = P.emit("vector", lambda e, o=d.ap, b=b_sb.ap, n=ncols, bp=bcol0 + p, do=dst_off:
                    e.tensor_scalar(o[:, do:do + n], o[:, do:do + n], b[:, bp:bp + 1], None, OP.add),
                    deps=[prev] + b_sb.rd())
        b_sb.did_read(op)
        d.did_write(op)
        ops_out.append(op)
    return ops_out




def emit_layer3(nc, es, P, C, pfx, x_in, y, masks_d):
    TL = TL3
    di = lambda n, s: Buf(nc.dram_tensor(pfx + n, s, F32, kind="ExternalInput").ap())
    win_d = di("win", [D, 3 * D])
    binv_d = di("binv", [128, 24])
    shw_d = di("shw", [128, 72])
    shb_d = di("shb", [128, 24])
    inw_d = di("inw", [128, 32])
    inb_d = di("inb", [128, 8])
    wout_d = di("wout", [D, D])
    boutb_d = di("boutb", [128, D])
    gv_d = di("gv", [8, D])
    gg_d = di("gg", [8, 1])
    gbb_d = di("gbb", [128, 8])
    sel_d = di("sel", [8, 1024])
    db16 = lambda n, s: Buf(nc.dram_tensor(pfx + n, s, BF16, kind="ExternalInput").ap())
    w1T_d = db16("w1T", [8 * D, 4096])
    w1g_d = di("w1g", [128, 256])
    b1_d = di("b1", [128, 256])
    w2T_d = db16("w2T", [8 * 4096, D])
    w2g_d = di("w2g", [128, 64])
    b2r_d = di("b2r", [8, D])
    oner_d = di("oner", [128, 1])

    xnT = dbuf(nc, "xnT", [D, PAD + TL], BF16)
    aout = dbuf(nc, "aout", [TL, D], BF16)
    ucq_d = dbuf(nc, "ucq", [D, 8 + TL], BF16)
    uck_d = dbuf(nc, "uck", [D, 8 + TL], BF16)
    ucv_d = dbuf(nc, "ucv", [D, 8 + TL], BF16)
    nrm1_d = dbuf(nc, "nrm1", [8, 4096], F32)
    nrm2_d = dbuf(nc, "nrm2", [8, D], F32)
    yT_d = dbuf(nc, "yT", [D, TL], F32)
    CH = 512
    import os as _os
    dbg = bool(_os.environ.get("L3_DEBUG"))
    if dbg:
        dbg_a = Buf(nc.dram_tensor("dbg_a", [T, D], BF16, kind="ExternalOutput").ap())
        dbg_y = Buf(nc.dram_tensor("dbg_y", [D, T], F32, kind="ExternalOutput").ap())
        dbg_uq = Buf(nc.dram_tensor("dbg_uq", [D, 8 + T], BF16, kind="ExternalOutput").ap())
        dbg_xn = Buf(nc.dram_tensor("dbg_xn", [D, PAD + T], BF16, kind="ExternalOutput").ap())

    if True:
        emit_zero_pad(P, C, [(xnT, D), (ucq_d, D), (uck_d, D), (ucv_d, D)])
        emit_norm(P, C, x_in, xnT, None, kind="ln", n_tok=TL)
        xnf = [sbuf(nc, es, f"xnf{f}", [128, PAD + TL], BF16) for f in range(NF)]
        for f in range(NF):
            op = P.emit("sync", lambda e, o=xnf[f].ap, i=xnT.ap, f=f: e.dma_start(out=o[:, :], in_=i[f * 128:(f + 1) * 128, :]), deps=xnf[f].wr() + xnT.rd(), dma=True)
            xnf[f].did_write(op)
            xnT.did_read(op)

        esA = ExitStack()
        with esA:
            W = alloc_wpool(nc, esA, 2)
            binv = load_f32(P, nc, esA, "binv_s", binv_d.ap, [128, 24])
            shw = load_f32(P, nc, esA, "shw_s", shw_d.ap, [128, 72])
            shb = load_f32(P, nc, esA, "shb_s", shb_d.ap, [128, 24])
            inw = load_f32(P, nc, esA, "inw_s", inw_d.ap, [128, 32])
            inb = load_f32(P, nc, esA, "inb_s", inb_d.ap, [128, 8])
            boutb = load_f32(P, nc, esA, "boutb_s", boutb_d.ap, [128, D])
            x1g = [sbuf(nc, esA, f"x1g{p}", [128, 528], BF16) for p in range(NF)]
            ucg = [sbuf(nc, esA, f"ucg{p}", [128, 512], BF16) for p in range(NF)]

            def load_w_group(wbuf, grp):
                deps0 = wbuf.wr()
                ops = []
                for c in range(NF):
                    op = P.emit("gpsimd",
                                lambda e, o=wbuf.ap, i=win_d.ap, c=c, grp=grp:
                                e.dma_start(out=o[:, c * D:(c + 1) * D], in_=i[c * 128:(c + 1) * 128, grp * D:(grp + 1) * D], accum_op=OP.bypass),
                                deps=deps0, dma=True)
                    ops.append(op)
                wbuf.w = ops
                wbuf.r = []
                wbuf.open = False

            for gi, (grp, dst) in enumerate(((0, ucq_d), (1, uck_d), (2, ucv_d))):
                wb = W[gi % 2]
                load_w_group(wb, grp)
                for ch in range(TL // CH):
                    pops = []
                    for eo in range(NF):
                        pps = C["ps0"] if eo % 2 == 0 else C["ps1"]
                        for part, (c0, cw) in enumerate(((0, 512), (512, 16))):
                            mms = []
                            for c in range(NF):
                                op = P.emit("tensor",
                                            lambda e, p=pps.ap, w=wb.ap, r=xnf[c].ap, c=c, eo=eo, ch=ch, c0=c0, cw=cw, first=(c == 0), last=(c == NF - 1):
                                            e.matmul(p[:, 0:cw], w[:, c * D + eo * 128: c * D + eo * 128 + 128],
                                                     r[:, PAD + ch * CH - 16 + c0: PAD + ch * CH - 16 + c0 + cw], start=first, stop=last),
                                            deps=(pps.wr() if c == 0 else []) + wb.rd() + xnf[c].rd() + (mms[-1:] if mms else []))
                                mms.append(op)
                                wb.did_read(op)
                                xnf[c].did_read(op)
                            pps.did_write(mms[-1])
                            ev = P.emit("vector", lambda e, o=x1g[eo].ap, i=pps.ap, b=binv.ap, grp=grp, eo=eo, c0=c0, cw=cw:
                                        e.tensor_scalar(o[:, c0:c0 + cw], i[:, 0:cw], b[:, grp * 8 + eo: grp * 8 + eo + 1], None, OP.add),
                                        deps=[mms[-1]] + binv.rd() + (x1g[eo].wr() if part == 0 else []))
                            binv.did_read(ev)
                            x1g[eo].did_write(ev, accum=True)
                            pps.did_read(ev)
                            pops.append(ev)
                    cops = emit_conv_taps(P, C, x1g, ucg, shw, shb, grp * 24, grp * 8, 512, 3, 16, 0, NF, pops)
                    for p in range(NF):
                        st = P.emit("scalar", lambda e, i=ucg[p].ap, o=dst.ap, p=p, ch=ch:
                                    e.dma_start(out=o[p * 128:(p + 1) * 128, 8 + ch * CH: 8 + ch * CH + 512], in_=i[:, :]),
                                    deps=[cops[p]] + dst.r, dma=True)
                        ucg[p].did_read(st)
                        dst.did_write(st, accum=True)
            load_w_into(P, W[0], wout_d.ap, D, D)
            ka = [sbuf(nc, esA, f"ka{p}", [128, 515], BF16) for p in range(NF)]
            vv2 = [sbuf(nc, esA, f"vv2{p}", [128, 512], BF16) for p in range(NF)]
            yb = [sbuf(nc, esA, f"yb{p}", [128, 512], BF16) for p in range(NF)]
            qb = [sbuf(nc, esA, f"qb{i}", [128, 512], BF16) for i in range(2)]
            for ch in range(TL // CH):
                for p in range(NF):
                    a = ka[p]
                    l1 = P.emit("sync", lambda e, o=a.ap, i=ucv_d.ap, p=p, ch=ch:
                                e.dma_start(out=o[:, :], in_=i[p * 128:(p + 1) * 128, 5 + ch * CH: 5 + ch * CH + 515]),
                                deps=a.wr() + ucv_d.rd(), dma=True)
                    a.did_write(l1)
                    ucv_d.did_read(l1)
                    d2 = C["stB"][p % 2]
                    l2 = P.emit("sync", lambda e, o=d2.ap, i=uck_d.ap, p=p, ch=ch:
                                e.dma_start(out=o[:, 0:515], in_=i[p * 128:(p + 1) * 128, 5 + ch * CH: 5 + ch * CH + 515]),
                                deps=d2.wr() + uck_d.rd(), dma=True)
                    d2.did_write(l2)
                    uck_d.did_read(l2)
                    o1 = P.emit("vector", lambda e, o=a.ap, i=d2.ap: e.tensor_tensor(o[:, :], o[:, :], i[:, 0:515], OP.mult), deps=[l1, l2])
                    a.did_write(o1)
                    d2.did_read(o1)
                    cops = emit_conv_taps(P, C, [a], [vv2[p]], inw, inb, p * 4, p, 512, 4, 3, 0, 1, [o1])
                    q = qb[p % 2]
                    l3 = P.emit("sync", lambda e, o=q.ap, i=ucq_d.ap, p=p, ch=ch:
                                e.dma_start(out=o[:, :], in_=i[p * 128:(p + 1) * 128, 8 + ch * CH: 8 + ch * CH + 512]),
                                deps=q.wr() + ucq_d.rd(), dma=True)
                    q.did_write(l3)
                    ucq_d.did_read(l3)
                    o2 = P.emit("vector", lambda e, o=yb[p].ap, i=q.ap, v=vv2[p].ap: e.tensor_tensor(o[:, :], i[:, :], v[:, :], OP.mult),
                                deps=[l3, cops[0]] + yb[p].wr())
                    yb[p].did_write(o2)
                    q.did_read(o2)
                    vv2[p].did_read(o2)
                for tsub in range(4):
                    for dc in range(2):
                        pps = C["ps0"] if dc == 0 else C["ps1"]
                        mms = []
                        for p in range(NF):
                            op = P.emit("tensor",
                                        lambda e, ps_=pps.ap, l=yb[p].ap, w=W[0].ap, p=p, tsub=tsub, dc=dc, first=(p == 0), last=(p == NF - 1):
                                        e.matmul(ps_[:, :], l[:, tsub * 128:(tsub + 1) * 128], w[:, p * D + dc * 512: p * D + (dc + 1) * 512], start=first, stop=last),
                                        deps=(pps.wr() if p == 0 else []) + yb[p].rd() + W[0].rd() + (mms[-1:] if mms else []))
                            mms.append(op)
                            yb[p].did_read(op)
                            W[0].did_read(op)
                        pps.did_write(mms[-1])
                        eb = C["stC"][dc % 2]
                        o1 = P.emit("vector", lambda e, o=eb.ap, i=pps.ap, b=boutb.ap, dc=dc:
                                    e.tensor_tensor(o[:, 0:512], i[:, :], b[:, dc * 512:(dc + 1) * 512], OP.add),
                                    deps=[mms[-1]] + eb.wr() + boutb.rd())
                        boutb.did_read(o1)
                        eb.did_write(o1)
                        pps.did_read(o1)
                        st = P.emit("scalar", lambda e, i=eb.ap, o=aout.ap, ch=ch, tsub=tsub, dc=dc:
                                    e.dma_start(out=o[ch * CH + tsub * 128: ch * CH + (tsub + 1) * 128, dc * 512:(dc + 1) * 512], in_=i[:, 0:512]),
                                    deps=[o1] + aout.r, dma=True)
                        eb.did_read(st)
                        aout.did_write(st, accum=True)
            fA = P.fence()

        esB = ExitStack()
        with esB:
            maskT = sbuf(nc, esB, "maskT", [8, TL], BF16, init=fA)
            b2s = sbuf(nc, esB, "b2s", [8, D], BF16, init=fA)
            s1 = sbuf(nc, esB, "s1", [128, 256], F32, init=fA)
            s2 = sbuf(nc, esB, "s2", [128, 64], F32, init=fA)
            gbb = load_f32(P, nc, esB, "gbb_s", gbb_d.ap, [128, 8])
            gbb.w += fA
            esB0 = ExitStack()
            esB0.__enter__()
            sb2 = lambda n, s, d: sbuf(nc, esB0, n, s, d, init=fA)
            gvf = sb2("gvf", [8, D], F32)
            op = P.emit("sync", lambda e, o=gvf.ap, i=gv_d.ap: e.dma_start(out=o[:, :], in_=i), deps=gvf.wr(), dma=True)
            gvf.did_write(op)
            ggs = sb2("ggs", [8, 1], F32)
            op = P.emit("sync", lambda e, o=ggs.ap, i=gg_d.ap: e.dma_start(out=o[:, :], in_=i), deps=ggs.wr(), dma=True)
            ggs.did_write(op)
            gs = sb2("gs", [8, 8], F32)
            q1 = C["stD"][0]
            o1 = P.emit("vector", lambda e, o=q1.ap, i=gvf.ap: e.tensor_tensor(o[0:8, :], i[:, :], i[:, :], OP.mult), deps=gvf.rd() + q1.wr() + fA)
            q1.did_write(o1)
            gvf.did_read(o1)
            o2 = P.emit("vector", lambda e, o=gs.ap, i=q1.ap: e.reduce_sum(o[0:8, 0:1], i[0:8, :], axis=AX), deps=[o1] + gs.wr())
            q1.did_read(o2)
            o3 = P.emit("scalar", lambda e, o=gs.ap: e.activation(o[0:8, 1:2], o[0:8, 0:1], AF.Sqrt), deps=[o2])
            o4 = P.emit("vector", lambda e, o=gs.ap: e.tensor_scalar(o[0:8, 2:3], o[0:8, 1:2], 1e-12, None, OP.max), deps=[o3])
            o5 = P.emit("vector", lambda e, o=gs.ap: e.reciprocal(o[0:8, 3:4], o[0:8, 2:3]), deps=[o4])
            o6 = P.emit("vector", lambda e, o=gs.ap, g=ggs.ap: e.tensor_tensor(o[0:8, 4:5], o[0:8, 3:4], g[0:8, 0:1], OP.mult), deps=[o5] + ggs.rd())
            ggs.did_read(o6)
            gs.did_write(o6)
            gwn = sb2("gwn", [128, D], BF16)
            oz = P.emit("gpsimd", lambda e, o=gwn.ap: e.memset(o[:, :], 0.0), deps=gwn.wr() + fA)
            gwn.did_write(oz)
            o7 = P.emit("vector", lambda e, o=gwn.ap, i=gvf.ap, s=gs.ap: e.tensor_scalar(o[0:8, :], i[:, :], s[0:8, 4:5], None, OP.mult), deps=[oz, o6] + gvf.rd())
            gwn.did_write(o7, accum=False)
            gwn.w.append(oz)
            gvf.did_read(o7)
            gs.did_read(o7)
            gwT = sb2("gwT", [128, 64], BF16)
            for f in range(NF):
                pt = C["psT1"]
                tr = tr128(P, C, gwn, f * 128, pt, 0)
                ev = P.emit("vector", lambda e, o=gwT.ap, i=pt.ap, f=f: e.tensor_copy(o[:, f * 8:(f + 1) * 8], i[:, 0:8]), deps=[tr] + gwT.r)
                gwT.did_write(ev, accum=True)
                pt.did_read(ev)
            lg = sb2("lg3", [128, 8], F32)
            m8 = sb2("m83", [128, 8], F32)
            mtok = sb2("mtok", [128, 128], BF16)
            for tt in range(TL // 128):
                p = C["ps0"]
                mms = []
                for c in range(NF):
                    op = P.emit("tensor", lambda e, p_=p.ap, l=xnf[c].ap, w=gwT.ap, c=c, tt=tt, first=(c == 0), last=(c == NF - 1):
                                e.matmul(p_[:, 0:8], l[:, PAD + tt * 128: PAD + (tt + 1) * 128], w[:, c * 8:(c + 1) * 8], start=first, stop=last),
                                deps=(p.wr() if c == 0 else []) + xnf[c].rd() + gwT.rd() + (mms[-1:] if mms else []))
                    mms.append(op)
                    xnf[c].did_read(op)
                    gwT.did_read(op)
                p.did_write(mms[-1])
                o1 = P.emit("vector", lambda e, o=lg.ap, i=p.ap, b=gbb.ap: e.tensor_tensor(o[:, :], i[:, 0:8], b[:, :], OP.add), deps=[mms[-1]] + lg.wr() + gbb.rd())
                gbb.did_read(o1)
                lg.did_write(o1)
                p.did_read(o1)
                o2 = P.emit("vector", lambda e, o=m8.ap, i=lg.ap: e.max(out=o[:, :], in_=i[:, :]), deps=[o1] + m8.wr())
                m8.did_write(o2)
                lg.did_read(o2)
                o3 = P.emit("vector", lambda e, o=mtok.ap, i=lg.ap, m=m8.ap: e.tensor_scalar(o[:, 0:8], i[:, :], m[:, 1:2], None, OP.is_ge), deps=[o2] + mtok.wr())
                mtok.did_write(o3)
                lg.did_read(o3)
                m8.did_read(o3)
                pt = C["psT2"]
                tr = tr128(P, C, mtok, 0, pt, 0, extra=[o3])
                ev = P.emit("vector", lambda e, o=maskT.ap, i=pt.ap, tt=tt: e.tensor_copy(o[0:8, tt * 128:(tt + 1) * 128], i[0:8, 0:128]), deps=[tr] + maskT.r)
                maskT.did_write(ev, accum=True)
                pt.did_read(ev)
            esB0.__exit__(None, None, None)
            fB0 = P.fence()
            op = P.emit("gpsimd", lambda e, o=b2s.ap, i=b2r_d.ap: e.dma_start(out=o[:, :], in_=i, accum_op=OP.bypass), deps=b2s.wr() + fB0, dma=True)
            b2s.did_write(op)
            for do in range(NF):
                for tc in range(TL // 512):
                    p = C["psS2"]
                    mm = P.emit("tensor", lambda e, p_=p.ap, l=b2s.ap, r=maskT.ap, do=do, tc=tc:
                                e.matmul(p_[:, 0:512], l[0:8, do * 128:(do + 1) * 128], r[0:8, tc * 512:(tc + 1) * 512], start=True, stop=True),
                                deps=p.wr() + b2s.rd() + maskT.rd())
                    b2s.did_read(mm)
                    maskT.did_read(mm)
                    p.did_write(mm)
                    eb = C["stE"][tc % 2]
                    ev = P.emit("vector", lambda e, o=eb.ap, i=p.ap: e.tensor_copy(o[:, 0:512], i[:, 0:512]), deps=[mm] + eb.wr())
                    eb.did_write(ev)
                    p.did_read(ev)
                    st = P.emit("scalar", lambda e, i=eb.ap, o=yT_d.ap, do=do, tc=tc:
                                e.dma_start(out=o[do * 128:(do + 1) * 128, tc * 512:(tc + 1) * 512], in_=i[:, 0:512]),
                                deps=[ev] + yT_d.r, dma=True)
                    eb.did_read(st)
                    yT_d.did_write(st, accum=True)
            w1g = load_f32(P, nc, esB, "w1g_s", w1g_d.ap, [128, 256])
            w1g.w += fB0
            b1s = load_f32(P, nc, esB, "b1_s", b1_d.ap, [128, 256])
            b1s.w += fB0
            w2g = load_f32(P, nc, esB, "w2g_s", w2g_d.ap, [128, 64])
            w2g.w += fB0
            oner = load_bf16(P, nc, esB, "oner_s", oner_d.ap, [128, 1])
            oner.w += fB0
            esB1 = ExitStack()
            esB1.__enter__()
            import os as _os
            if _os.environ.get("L3_SKIP_NORMS"):
                op1 = P.emit("gpsimd", lambda e, o=s1.ap: e.memset(o[:, :], 1.0), deps=s1.wr() + fA)
                s1.did_write(op1)
                op2 = P.emit("gpsimd", lambda e, o=s2.ap: e.memset(o[:, :], 1.0), deps=s2.wr() + fA)
                s2.did_write(op2)
            else:
                emit_norms_from_T(P, C, nc, esB1, w1T_d, w1g, s1, oner, nrm1_d, D, 4096, 8, init=fA)
                emit_norms_from_T(P, C, nc, esB1, w2T_d, w2g, s2, oner, nrm2_d, 4096, D, 8, init=fA)
            esB1.__exit__(None, None, None)
            fB1 = P.fence()
            sb3 = lambda n, s, d: sbuf(nc, esB, n, s, d, init=fB1)
            sel_s = load_bf16(P, nc, esB, "sel_s", sel_d.ap, [8, 1024])
            sel_s.w += fB1
            sT = [sb3(f"sT{i}", [128, 1024], BF16) for i in range(32)]
            w1t = C["stC"][0]
            w2t = sb3("w2t", [128, 4096], BF16)
            mkb = C["stB"][0]
            ypv = [C["stD"][0]] * 2
            for tb_off, tb_w in ((0, 1024), (1024, 1024), (2048, 512)):
                for e_ in range(8):
                    for ho in range(32):
                        lw = P.emit("sync", lambda e, o=w1t.ap, i=w1T_d.ap, e_=e_, ho=ho:
                                    e.dma_start(out=o[:, :].rearrange("p (c w) -> p c w", w=128),
                                                in_=i[e_ * 1024:(e_ + 1) * 1024, ho * 128:(ho + 1) * 128].rearrange("(c p) w -> p c w", p=128)),
                                    deps=w1t.wr() + w1T_d.rd(), dma=True)
                        w1t.did_write(lw)
                        w1T_d.did_read(lw)
                        for tc2 in range(tb_w // 512):
                            p = [C["ps0"], C["ps1"]][tc2]
                            mms = []
                            for c in range(NF):
                                op = P.emit("tensor",
                                            lambda e, p_=p.ap, l=w1t.ap, r=xnf[c].ap, c=c, tb_off=tb_off, tc2=tc2, first=(c == 0), last=(c == NF - 1):
                                            e.matmul(p_[:, :], l[:, c * 128:(c + 1) * 128],
                                                     r[:, PAD + tb_off + tc2 * 512: PAD + tb_off + (tc2 + 1) * 512], start=first, stop=last),
                                            deps=(p.wr() if c == 0 else []) + [lw] + xnf[c].rd() + (mms[-1:] if mms else []))
                                mms.append(op)
                                w1t.did_read(op)
                                xnf[c].did_read(op)
                            p.did_write(mms[-1])
                            ev = P.emit("scalar", lambda e, o=sT[ho].ap, i=p.ap, s=s1.ap, b=b1s.ap, e_=e_, ho=ho, tc2=tc2:
                                        e.activation(o[:, tc2 * 512:(tc2 + 1) * 512], i[:, :], AF.Silu,
                                                     bias=b[:, e_ * 32 + ho: e_ * 32 + ho + 1], scale=s[:, e_ * 32 + ho: e_ * 32 + ho + 1]),
                                        deps=[mms[-1]] + sT[ho].wr() + s1.rd() + b1s.rd())
                            sT[ho].did_write(ev, accum=True)
                            s1.did_read(ev)
                            b1s.did_read(ev)
                            p.did_read(ev)
                    for tc2 in range(tb_w // 512):
                        p = C["psS2"]
                        mm = P.emit("tensor", lambda e, p_=p.ap, l=sel_s.ap, r=maskT.ap, e_=e_, tb_off=tb_off, tc2=tc2:
                                    e.matmul(p_[:, 0:512], l[0:8, e_ * 128:(e_ + 1) * 128], r[0:8, tb_off + tc2 * 512: tb_off + (tc2 + 1) * 512], start=True, stop=True),
                                    deps=p.wr() + sel_s.rd() + maskT.rd())
                        sel_s.did_read(mm)
                        maskT.did_read(mm)
                        p.did_write(mm)
                        ev = P.emit("vector", lambda e, o=mkb.ap, i=p.ap, tc2=tc2: e.tensor_copy(o[:, tc2 * 512:(tc2 + 1) * 512], i[:, 0:512]),
                                    deps=[mm] + (mkb.wr() if tc2 == 0 else []))
                        mkb.did_write(ev, accum=True)
                        p.did_read(ev)
                    for do in range(NF):
                        lw = P.emit("sync", lambda e, o=w2t.ap, i=w2T_d.ap, e_=e_, do=do:
                                    e.dma_start(out=o[:, :].rearrange("p (c w) -> p c w", w=128),
                                                in_=i[e_ * 4096:(e_ + 1) * 4096, do * 128:(do + 1) * 128].rearrange("(c p) w -> p c w", p=128)),
                                    deps=w2t.wr() + w2T_d.rd(), dma=True)
                        w2t.did_write(lw)
                        w2T_d.did_read(lw)
                        for tc2 in range(tb_w // 512):
                            p = C["psS1"]
                            mms = []
                            for ho in range(32):
                                op = P.emit("tensor",
                                            lambda e, p_=p.ap, l=w2t.ap, r=sT[ho].ap, ho=ho, tc2=tc2, first=(ho == 0), last=(ho == 31):
                                            e.matmul(p_[:, 0:512], l[:, ho * 128:(ho + 1) * 128], r[:, tc2 * 512:(tc2 + 1) * 512], start=first, stop=last),
                                            deps=(p.wr() if ho == 0 else []) + [lw] + sT[ho].rd() + (mms[-1:] if mms else []))
                                mms.append(op)
                                w2t.did_read(op)
                                sT[ho].did_read(op)
                            p.did_write(mms[-1])
                            yp = ypv[tc2]
                            lp = P.emit("sync", lambda e, o=yp.ap, i=yT_d.ap, do=do, tb_off=tb_off, tc2=tc2:
                                        e.dma_start(out=o[:, 0:512], in_=i[do * 128:(do + 1) * 128, tb_off + tc2 * 512: tb_off + (tc2 + 1) * 512]),
                                        deps=yp.wr() + yT_d.rd(), dma=True)
                            yp.did_write(lp)
                            yT_d.did_read(lp)
                            eb = C["stE"][tc2]
                            o1 = P.emit("vector", lambda e, o=eb.ap, i=p.ap, s=s2.ap, m=mkb.ap, e_=e_, do=do, tc2=tc2:
                                        e.scalar_tensor_tensor(o[:, 0:512], i[:, 0:512], s[:, e_ * 8 + do: e_ * 8 + do + 1], m[:, tc2 * 512:(tc2 + 1) * 512], OP.mult, OP.mult),
                                        deps=[mms[-1]] + eb.wr() + s2.rd() + mkb.rd())
                            eb.did_write(o1)
                            p.did_read(o1)
                            s2.did_read(o1)
                            mkb.did_read(o1)
                            o2 = P.emit("vector", lambda e, o=eb.ap, i=yp.ap: e.tensor_tensor(o[:, 0:512], o[:, 0:512], i[:, 0:512], OP.add), deps=[o1, lp])
                            eb.did_write(o2)
                            yp.did_read(o2)
                            st = P.emit("scalar", lambda e, i=eb.ap, o=yT_d.ap, do=do, tb_off=tb_off, tc2=tc2:
                                        e.dma_start(out=o[do * 128:(do + 1) * 128, tb_off + tc2 * 512: tb_off + (tc2 + 1) * 512], in_=i[:, 0:512]),
                                        deps=[o2] + yT_d.r + yT_d.w, dma=True)
                            eb.did_read(st)
                            yT_d.w.append(st)
            fB = P.fence()

        for ti in range(16):
            t = TOFF // 128 + ti
            a = C["stA"][t % 2]
            u = C["stB"][t % 2]
            ft = C["stD"][t % 2]
            res = C["stE"][t % 2]
            l1 = P.emit("sync", lambda e, o=a.ap, i=x_in.ap, t=t: e.dma_start(out=o[:, :], in_=i[t * 128:(t + 1) * 128, :]), deps=a.wr() + x_in.rd() + fB, dma=True)
            a.did_write(l1)
            x_in.did_read(l1)
            l2 = P.emit("sync", lambda e, o=u.ap, i=aout.ap, t=t: e.dma_start(out=o[:, :], in_=i[t * 128:(t + 1) * 128, :]), deps=u.wr() + aout.rd() + fB, dma=True)
            u.did_write(l2)
            aout.did_read(l2)
            bstage = C["stC"][t % 2]
            pt = [C["psT1"], C["psT2"]][t % 2]
            trs = []
            for f in range(NF):
                l3 = P.emit("sync", lambda e, o=C["stD"][(t + 1) % 2].ap, i=yT_d.ap, f=f, t=t:
                            e.dma_start(out=o[:, 0:128], in_=i[f * 128:(f + 1) * 128, t * 128:(t + 1) * 128]),
                            deps=C["stD"][(t + 1) % 2].wr() + yT_d.rd() + fB, dma=True)
                C["stD"][(t + 1) % 2].did_write(l3)
                yT_d.did_read(l3)
                cv = P.emit("vector", lambda e, o=bstage.ap, i=C["stD"][(t + 1) % 2].ap, f=f: e.tensor_copy(o[:, f * 128:(f + 1) * 128], i[:, 0:128]),
                            deps=[l3] + (bstage.wr() if f == 0 else []))
                bstage.did_write(cv, accum=True)
                C["stD"][(t + 1) % 2].did_read(cv)
                trs.append(tr128(P, C, bstage, f * 128, pt, f * 128, extra=[cv]))
            ev = P.emit("vector", lambda e, o=ft.ap, i=pt.ap: e.tensor_copy(o[:, :], i[:, :]), deps=[trs[-1]] + ft.wr())
            ft.did_write(ev)
            pt.did_read(ev)
            o1 = P.emit("vector", lambda e, o=res.ap, i0=a.ap, i1=u.ap: e.tensor_tensor(o[:, :], i0[:, :], i1[:, :], OP.add), deps=[l1, l2] + res.wr())
            res.did_write(o1)
            a.did_read(o1)
            u.did_read(o1)
            o2 = P.emit("vector", lambda e, o=res.ap, i1=ft.ap: e.tensor_tensor(o[:, :], o[:, :], i1[:, :], OP.add), deps=[o1, ev])
            res.did_write(o2)
            ft.did_read(o2)
            st = P.emit("scalar", lambda e, i=res.ap, o=y.ap, ti=ti: e.dma_start(out=o[ti * 128:(ti + 1) * 128, :], in_=i[:, :]), deps=[o2] + y.r, dma=True)
            res.did_read(st)
            y.did_write(st, accum=True)
        if dbg:
            bd = P.barrier_deps()
            for (dsrc, ddst) in ((aout, dbg_a), (yT_d, dbg_y), (ucq_d, dbg_uq), (xnT, dbg_xn)):
                op = P.emit("gpsimd", lambda e, o=ddst.ap, i=dsrc.ap: e.dma_start(out=o, in_=i), deps=bd, dma=True)


def build_layer3():
    return _build_single(emit_layer3, proj_pools=False)


def prep_l3_inputs(ins, xb):
    ident, _ = _l0_consts()
    shw = np.asarray(ins["L3_attn_short_w"], np.float32)
    inw = np.asarray(ins["L3_attn_inner_w"], np.float32)
    w1v = np.asarray(ins["L3_ffn_w1_v"], np.float32)
    w2v = np.asarray(ins["L3_ffn_w2_v"], np.float32)
    sel = np.zeros((8, 1024), np.float32)
    for e in range(8):
        sel[e, e * 128:(e + 1) * 128] = 1.0
    return {
        "x": np.ascontiguousarray(xb, dtype=np.float32),
        "ident": ident,
        "sel4": _SEL4, "sel64": _SEL64,
        "win": np.asarray(ins["L3_attn_W_in"], np.float32),
        "binv": vec8(np.asarray(ins["L3_attn_b_in"], np.float32)),
        "shw": np.ascontiguousarray(shw.reshape(24, 128, 3).transpose(1, 0, 2).reshape(128, 72)),
        "shb": vec8(np.asarray(ins["L3_attn_short_b"], np.float32)),
        "inw": np.ascontiguousarray(inw.reshape(8, 128, 4).transpose(1, 0, 2).reshape(128, 32)),
        "inb": vec8(np.asarray(ins["L3_attn_inner_b"], np.float32)),
        "wout": np.asarray(ins["L3_attn_W_out"], np.float32),
        "boutb": brow(np.asarray(ins["L3_attn_b_out"], np.float32)),
        "gv": np.asarray(ins["L3_ffn_gate_v"], np.float32),
        "gg": np.asarray(ins["L3_ffn_gate_g"], np.float32).reshape(8, 1),
        "gbb": brow(np.asarray(ins["L3_ffn_gate_b"], np.float32)),
        "sel": sel,
        "w1T": np.ascontiguousarray(np.transpose(w1v, (0, 2, 1)).reshape(8 * 1024, 4096)).astype(_bf16()),
        "w1g": np.concatenate([vec8(np.asarray(ins["L3_ffn_w1_g"], np.float32)[e]) for e in range(8)], axis=1),
        "b1": np.concatenate([vec8(np.asarray(ins["L3_ffn_b1"], np.float32)[e]) for e in range(8)], axis=1),
        "w2T": np.ascontiguousarray(np.transpose(w2v, (0, 2, 1)).reshape(8 * 4096, 1024)).astype(_bf16()),
        "oner": np.ones((128, 1), np.float32),
        "w2g": np.concatenate([vec8(np.asarray(ins["L3_ffn_w2_g"], np.float32)[e]) for e in range(8)], axis=1),
        "b2r": np.asarray(ins["L3_ffn_b2"], np.float32),
    }


def make_store_cb(P, C, dst):
    def cb(tt, dc, p, mm):
        eb = C["stC"][dc % 2]
        o1 = P.emit("vector", lambda e, o=eb.ap, i=p.ap: e.tensor_copy(o[:, 0:512], i[:, :]), deps=[mm] + eb.wr())
        eb.did_write(o1)
        p.did_read(o1)
        st = P.emit("scalar", lambda e, i=eb.ap, o=dst.ap, tt=tt, dc=dc:
                    e.dma_start(out=o[tt * 128:(tt + 1) * 128, dc * 512:(dc + 1) * 512], in_=i[:, 0:512]),
                    deps=[o1] + dst.r, dma=True)
        eb.did_read(st)
        dst.did_write(st, accum=True)
    return cb


def emit_norms_from_T(P, C, nc, es, wT_d, g_sb, out_sb, onerow, nrm_d, n_in, n_out, n_e, eps=1e-12, init=()):
    stg = C["stB"]
    sq = C["stC"]
    row = sbuf(nc, es, f"nrmrow{id(wT_d) % 99999}", [1, n_out], F32, init=list(init))
    ncc = n_in // 128
    for e_ in range(n_e):
        for co in range(n_out // 512):
            mm = None
            for c in range(ncc):
                s = stg[c % 2]
                q = sq[c % 2]
                ld = P.emit("sync", lambda e, o=s.ap, i=wT_d.ap, e_=e_, c=c, co=co, ni=n_in:
                            e.dma_start(out=o[:, 0:512], in_=i[e_ * ni + c * 128: e_ * ni + (c + 1) * 128, co * 512:(co + 1) * 512]),
                            deps=s.wr() + wT_d.rd(), dma=True)
                s.did_write(ld)
                wT_d.did_read(ld)
                o1 = P.emit("scalar", lambda e, o=q.ap, i=s.ap: e.activation(o[:, 0:512], i[:, 0:512], AF.Square), deps=[ld] + q.wr())
                q.did_write(o1)
                s.did_read(o1)
                first, last = (c == 0), (c == ncc - 1)
                mm = P.emit("tensor", lambda e, p=C["ps0"].ap, l=onerow.ap, r=q.ap, first=first, last=last:
                            e.matmul(p[0:1, 0:512], l[:, 0:1], r[:, 0:512], start=first, stop=last),
                            deps=(C["ps0"].wr() if first else [mm]) + [o1] + onerow.rd())
                onerow.did_read(mm)
                q.did_read(mm)
                C["ps0"].did_write(mm, accum=not first)
            ev = P.emit("vector", lambda e, o=row.ap, i=C["ps0"].ap, co=co: e.tensor_copy(o[0:1, co * 512:(co + 1) * 512], i[0:1, 0:512]),
                        deps=[mm] + (row.wr() if co == 0 else []))
            row.did_write(ev, accum=True)
            C["ps0"].did_read(ev)
        st = P.emit("scalar", lambda e, i=row.ap, o=nrm_d.ap, e_=e_, no=n_out:
                    e.dma_start(out=o[e_:e_ + 1, :], in_=i[0:1, 0:no]), deps=row.rd() + row.w + nrm_d.r, dma=True)
        row.did_read(st)
        nrm_d.did_write(st, accum=True)
    nv = nrm_d.ap.rearrange("e (c p) -> p e c", p=128)
    ld = P.emit("sync", lambda e, o=out_sb.ap, i=nv, ne=n_e:
                e.dma_start(out=out_sb.ap[:, :].rearrange("p (e c) -> p e c", e=ne), in_=i, allow_slow_non_contiguous=True),
                deps=nrm_d.rd() + out_sb.wr(), dma=True)
    nrm_d.did_read(ld)
    out_sb.did_write(ld)
    o4 = P.emit("scalar", lambda e, o=out_sb.ap: e.activation(o[:, :], o[:, :], AF.Sqrt), deps=out_sb.rd() + out_sb.w)
    out_sb.did_write(o4)
    o5 = P.emit("vector", lambda e, o=out_sb.ap, eps=eps: e.tensor_scalar(o[:, :], o[:, :], eps, None, OP.max), deps=[o4])
    out_sb.did_write(o5)
    o6 = P.emit("vector", lambda e, o=out_sb.ap: e.reciprocal(o[:, :], o[:, :]), deps=[o5])
    out_sb.did_write(o6)
    o7 = P.emit("vector", lambda e, o=out_sb.ap, g=g_sb.ap: e.tensor_tensor(o[:, :], o[:, :], g[:, :], OP.mult), deps=[o6] + g_sb.rd())
    g_sb.did_read(o7)
    out_sb.did_write(o7)


def emit_layer2(nc, es, P, C, pfx, x_in, y, masks_d):
    TL = TL3
    di = lambda n, s: Buf(nc.dram_tensor(pfx + n, s, F32, kind="ExternalInput").ap())
    gidx_d = Buf(nc.dram_tensor(pfx + "gidx", [128, TL // 512], I32, kind="ExternalInput").ap())
    db16 = lambda n, s: Buf(nc.dram_tensor(pfx + n, s, BF16, kind="ExternalInput").ap())
    triu_d = di("triu", [64, 64])
    wq_d = di("wq", [D, 512]); wk_d = di("wk", [D, 512])
    wv_d = di("wv", [D, D]); wg_d = di("wg", [D, D])
    wgk1_d = di("wgk1", [D, 16]); wgk2_d = di("wgk2", [16, 512])
    bgk2_d = di("bgk2", [128, 4])
    wo_d = di("wo", [D, D])
    gnwb_d = di("gnwb", [128, D])
    n1b_d = di("n1b", [128, D]); n2b_d = di("n2b", [128, D])
    wqiT_d = di("wqiT", [D, D]); wkiT_d = di("wkiT", [D, D]); wviT_d = di("wviT", [D, D])
    wpiT_d = di("wpiT", [D, D])
    rT_d = di("rT", [128, 128])
    cosb_d = di("cosb", [128, T]); sinb_d = di("sinb", [128, T])
    rwT_d = di("rwT", [D, 8])
    wgT_d = db16("wgmT", [8 * D, 4096])
    wuT_d = db16("wumT", [8 * D, 4096])
    wdT_d = db16("wdmT", [8 * 4096, D])
    sel_d = di("sel", [8, 1024])

    xnT = dbuf(nc, "xnT", [D, PAD + T], BF16)
    xn_tok = dbuf(nc, "xn_tok", [T, D], BF16)
    qT_g = dbuf(nc, "qT_g", [512, T], BF16)
    kT_g = dbuf(nc, "kT_g", [512, T], BF16)
    gkT_g = dbuf(nc, "gkT_g", [512, T], BF16)
    v_tok = dbuf(nc, "v_tok", [T, D], BF16)
    g_tok = dbuf(nc, "g_tok", [T, D], BF16)
    o_tok = dbuf(nc, "o_tok", [T, D], BF16)
    ooT_d = dbuf(nc, "ooT", [D, T], BF16)
    aout = dbuf(nc, "aout", [T, D], BF16)
    rms1T = dbuf(nc, "rms1T", [D, PAD + T], BF16)
    qTi = dbuf(nc, "qTi", [D, T], BF16)
    kTi = dbuf(nc, "kTi", [D, T], BF16)
    vaug = dbuf(nc, "vaug", [T, 1040], BF16)
    oTi = dbuf(nc, "oTi", [D, T], BF16)
    ai_d = dbuf(nc, "ai_d", [T, D], BF16)
    xmid = dbuf(nc, "xmid", [T, D], BF16)
    x2T = dbuf(nc, "x2T", [D, PAD + TL], BF16)
    xmh = dbuf(nc, "xmh", [TL, D], BF16)
    xg_d = dbuf(nc, "xg", [TL, D], F32)
    ag_d = dbuf(nc, "ag", [TL, D], BF16)
    gates_d = dbuf(nc, "gates_d", [TL, 8], F32)
    yT_d = dbuf(nc, "yT2", [D, TL], F32)

    if True:
        emit_zero_pad(P, C, [(xnT, D), (rms1T, D), (x2T, D)])
        emit_norm(P, C, x_in, xnT, xn_tok, kind="ln")

        esG = ExitStack()
        with esG:
            W = alloc_wpool(nc, esG, 2)
            load_w_into(P, W[0], wq_d.ap, D, 512)
            proj_fm(P, C, xnT, W[0], qT_g, 512, scale=float(128 ** -0.5))
            load_w_into(P, W[1], wk_d.ap, D, 512)
            proj_fm(P, C, xnT, W[1], kT_g, 512)
            load_w_into(P, W[0], wv_d.ap, D, D)
            proj_tm(P, C, xnT, W[0], make_store_cb(P, C, v_tok))
            load_w_into(P, W[1], wg_d.ap, D, D)
            proj_tm(P, C, xnT, W[1], make_store_cb(P, C, g_tok))
            wgk1_sb = load_w(P, nc, esG, "wgk1_sb", wgk1_d.ap, D, 16)
            t1T = sbuf(nc, esG, "t1T", [16, T], BF16)
            t1b = sbuf(nc, esG, "t1b", [128, 128], BF16)
            for tt in range(NT):
                lh = C["lhsP"][tt % 2]
                for c in range(NF):
                    op = P.emit("sync", lambda e, o=lh[c].ap, i=xnT.ap, c=c, tt=tt:
                                e.dma_start(out=o[:, :], in_=i[c * 128:(c + 1) * 128, PAD + tt * 128: PAD + (tt + 1) * 128]),
                                deps=lh[c].wr() + xnT.rd(), dma=True)
                    lh[c].did_write(op)
                    xnT.did_read(op)
                p = C["ps0"]
                mms = []
                for c in range(NF):
                    op = P.emit("tensor", lambda e, p_=p.ap, l=lh[c].ap, w=wgk1_sb.ap, c=c, first=(c == 0), last=(c == NF - 1):
                                e.matmul(p_[:, 0:16], l[:, :], w[:, c * 16:(c + 1) * 16], start=first, stop=last),
                                deps=(p.wr() if c == 0 else []) + lh[c].rd() + wgk1_sb.rd() + (mms[-1:] if mms else []))
                    mms.append(op)
                    lh[c].did_read(op)
                    wgk1_sb.did_read(op)
                p.did_write(mms[-1])
                ev = P.emit("vector", lambda e, o=t1b.ap, i=p.ap: e.tensor_copy(o[:, 0:16], i[:, 0:16]), deps=[mms[-1]] + t1b.wr())
                t1b.did_write(ev)
                p.did_read(ev)
                pt = C["psT1"]
                tr = tr128(P, C, t1b, 0, pt, 0, extra=[ev])
                ev2 = P.emit("vector", lambda e, o=t1T.ap, i=pt.ap, tt=tt: e.tensor_copy(o[0:16, tt * 128:(tt + 1) * 128], i[0:16, 0:128]), deps=[tr] + t1T.r)
                t1T.did_write(ev2, accum=True)
                pt.did_read(ev2)
            wgk2_sb = load_bf16(P, nc, esG, "wgk2_sb", wgk2_d.ap, [16, 512])
            bgk2_sb = load_f32(P, nc, esG, "bgk2_sb", bgk2_d.ap, [128, 4])
            for dk in range(4):
                for tc in range(T // 512):
                    p = C["ps1"]
                    mm = P.emit("tensor", lambda e, p_=p.ap, l=wgk2_sb.ap, r=t1T.ap, dk=dk, tc=tc:
                                e.matmul(p_[:, :], l[0:16, dk * 128:(dk + 1) * 128], r[0:16, tc * 512:(tc + 1) * 512], start=True, stop=True),
                                deps=p.wr() + wgk2_sb.rd() + t1T.rd())
                    wgk2_sb.did_read(mm)
                    t1T.did_read(mm)
                    p.did_write(mm)
                    xb = C["stD"][tc % 2]
                    o1 = P.emit("vector", lambda e, o=xb.ap, i=p.ap, b=bgk2_sb.ap, dk=dk:
                                e.tensor_scalar(o[:, 0:512], i[:, :], b[:, dk:dk + 1], None, OP.add), deps=[mm] + xb.wr() + bgk2_sb.rd())
                    bgk2_sb.did_read(o1)
                    xb.did_write(o1)
                    p.did_read(o1)
                    ab = C["stE"][tc % 2]
                    o2 = P.emit("scalar", lambda e, o=ab.ap, i=xb.ap: e.activation(o[:, 0:512], i[:, 0:512], AF.Abs), deps=[o1] + ab.wr())
                    ab.did_write(o2)
                    o3 = P.emit("scalar", lambda e, o=ab.ap: e.activation(o[:, 0:512], o[:, 0:512], AF.Exp, scale=-1.0), deps=[o2])
                    ab.did_write(o3)
                    o4 = P.emit("vector", lambda e, o=ab.ap: e.tensor_scalar(o[:, 0:512], o[:, 0:512], 1.0, None, OP.add), deps=[o3])
                    ab.did_write(o4)
                    o5 = P.emit("scalar", lambda e, o=ab.ap: e.activation(o[:, 0:512], o[:, 0:512], AF.Ln), deps=[o4])
                    ab.did_write(o5)
                    o6 = P.emit("vector", lambda e, o=xb.ap: e.tensor_scalar(o[:, 0:512], o[:, 0:512], 0.0, None, OP.min), deps=[o1, o2])
                    xb.did_write(o6)
                    o7 = P.emit("vector", lambda e, o=xb.ap, i=ab.ap: e.tensor_tensor(o[:, 0:512], o[:, 0:512], i[:, 0:512], OP.subtract), deps=[o5, o6])
                    xb.did_write(o7)
                    ab.did_read(o7)
                    gb = C["stB"][tc % 2]
                    o8 = P.emit("vector", lambda e, o=gb.ap, i=xb.ap: e.tensor_scalar(o[:, 0:512], i[:, 0:512], 1.0 / 16.0, None, OP.mult), deps=[o7] + gb.wr())
                    gb.did_write(o8)
                    xb.did_read(o8)
                    st = P.emit("scalar", lambda e, i=gb.ap, o=gkT_g.ap, dk=dk, tc=tc:
                                e.dma_start(out=o[dk * 128:(dk + 1) * 128, tc * 512:(tc + 1) * 512], in_=i[:, 0:512]),
                                deps=[o8] + gkT_g.r, dma=True)
                    gb.did_read(st)
                    gkT_g.did_write(st, accum=True)
            esG1 = ExitStack()
            esG1.__enter__()
            qTs = [sbuf(nc, esG1, f"qTs{h}", [128, T], BF16) for h in range(4)]
            kTs = [sbuf(nc, esG1, f"kTs{h}", [128, T], BF16) for h in range(4)]
            gTs = [sbuf(nc, esG1, f"gTs{h}", [128, T], BF16) for h in range(4)]
            for h in range(4):
                for src_d, dst in ((qT_g, qTs[h]), (kT_g, kTs[h]), (gkT_g, gTs[h])):
                    op = P.emit("sync", lambda e, o=dst.ap, i=src_d.ap, h=h: e.dma_start(out=o[:, :], in_=i[h * 128:(h + 1) * 128, :]),
                                deps=dst.wr() + src_d.rd(), dma=True)
                    dst.did_write(op)
                    src_d.did_read(op)
            Sst = [sbuf(nc, esG1, f"Sst{h}", [128, 256], F32) for h in range(4)]
            for h in range(4):
                op = P.emit("gpsimd", lambda e, o=Sst[h].ap: e.memset(o[:, :], 0.0), deps=Sst[h].wr())
                Sst[h].did_write(op)
            Sbf = sbuf(nc, esG1, "Sbf", [128, 256], BF16)
            zcol = sbuf(nc, esG1, "zcol", [128, 1], F32)
            zop = P.emit("gpsimd", lambda e, o=zcol.ap: e.memset(o[:, :], 0.0), deps=zcol.wr())
            zcol.did_write(zop)
            triu_sb = load_bf16(P, nc, esG1, "triu_sb", triu_d.ap, [64, 64])
            Gb = sbuf(nc, esG1, "Gb", [128, 64], F32)
            eGb = sbuf(nc, esG1, "eGb", [128, 64], F32)
            enGb = sbuf(nc, esG1, "enGb", [128, 64], F32)
            qg_sb = sbuf(nc, esG1, "qg_sb", [128, 64], BF16)
            ki_sb = sbuf(nc, esG1, "ki_sb", [128, 64], BF16)
            kg_sb = sbuf(nc, esG1, "kg_sb", [128, 64], BF16)
            kgt_sb = sbuf(nc, esG1, "kgt_sb", [64, 128], BF16)
            at_sb = sbuf(nc, esG1, "at_sb", [64, 64], BF16)
            vch = [sbuf(nc, esG1, f"vch{i}", [64, 1024], BF16) for i in range(2)]
            och = [sbuf(nc, esG1, f"och{i}", [64, 1024], BF16) for i in range(2)]
            for n in range(64):
                vc = vch[n % 2]
                lv = P.emit("sync", lambda e, o=vc.ap, i=v_tok.ap, n=n: e.dma_start(out=o[0:64, :], in_=i[n * 64:(n + 1) * 64, :]),
                            deps=vc.wr() + v_tok.rd(), dma=True)
                vc.did_write(lv)
                v_tok.did_read(lv)
                oc = och[n % 2]
                for h in range(4):
                    o1 = P.emit("vector", lambda e, o=Gb.ap, i=gTs[h].ap, z=zcol.ap, n=n:
                                e.tensor_tensor_scan(o[:, :], i[:, n * 64:(n + 1) * 64], i[:, n * 64:(n + 1) * 64], z[:, 0:1], OP.add, OP.bypass),
                                deps=gTs[h].rd() + Gb.wr() + zcol.rd())
                    zcol.did_read(o1)
                    gTs[h].did_read(o1)
                    Gb.did_write(o1)
                    o2 = P.emit("scalar", lambda e, o=eGb.ap, i=Gb.ap: e.activation(o[:, :], i[:, :], AF.Exp), deps=[o1] + eGb.wr())
                    eGb.did_write(o2)
                    o3 = P.emit("scalar", lambda e, o=enGb.ap, i=Gb.ap: e.activation(o[:, :], i[:, :], AF.Exp, scale=-1.0), deps=[o1] + enGb.wr())
                    enGb.did_write(o3)
                    Gb.did_read(o3)
                    o4 = P.emit("vector", lambda e, o=qg_sb.ap, i=qTs[h].ap, g=eGb.ap, n=n:
                                e.tensor_tensor(o[:, :], i[:, n * 64:(n + 1) * 64], g[:, :], OP.mult), deps=[o2] + qg_sb.wr() + qTs[h].rd())
                    qg_sb.did_write(o4)
                    qTs[h].did_read(o4)
                    eGb.did_read(o4)
                    o5 = P.emit("vector", lambda e, o=ki_sb.ap, i=kTs[h].ap, g=enGb.ap, n=n:
                                e.tensor_tensor(o[:, :], i[:, n * 64:(n + 1) * 64], g[:, :], OP.mult), deps=[o3] + ki_sb.wr() + kTs[h].rd())
                    ki_sb.did_write(o5)
                    kTs[h].did_read(o5)
                    enGb.did_read(o5)
                    o6 = P.emit("vector", lambda e, o=kg_sb.ap, i=ki_sb.ap, g=eGb.ap:
                                e.tensor_scalar(o[:, :], i[:, :], g[:, 63:64], None, OP.mult), deps=[o5, o2] + kg_sb.wr())
                    kg_sb.did_write(o6)
                    ki_sb.did_read(o6)
                    eGb.did_read(o6)
                    pt = C["psT1"]
                    o7 = P.emit("tensor", lambda e, o=pt.ap, i=kg_sb.ap, idd=C["ident"].ap:
                                e.transpose(o[0:64, 0:128], i[:, 0:64], idd[:, :]), deps=[o6] + pt.wr() + C["ident"].rd())
                    pt.did_write(o7)
                    kg_sb.did_read(o7)
                    o8 = P.emit("vector", lambda e, o=kgt_sb.ap, i=pt.ap: e.tensor_copy(o[0:64, :], i[0:64, 0:128]), deps=[o7] + kgt_sb.wr())
                    kgt_sb.did_write(o8)
                    pt.did_read(o8)
                    o9 = P.emit("tensor", lambda e, p=C["ps0"].ap, l=ki_sb.ap, r=qg_sb.ap:
                                e.matmul(p[0:64, 0:64], l[:, :], r[:, :], start=True, stop=True),
                                deps=C["ps0"].wr() + [o4, o5])
                    ki_sb.did_read(o9)
                    qg_sb.did_read(o9)
                    C["ps0"].did_write(o9)
                    o10 = P.emit("vector", lambda e, o=at_sb.ap, i=C["ps0"].ap, t=triu_sb.ap:
                                 e.tensor_tensor(o[0:64, :], i[0:64, 0:64], t[0:64, :], OP.mult), deps=[o9] + at_sb.wr() + triu_sb.rd())
                    at_sb.did_write(o10)
                    triu_sb.did_read(o10)
                    C["ps0"].did_read(o10)
                    o11 = P.emit("tensor", lambda e, p=C["ps1"].ap, l=at_sb.ap, r=vc.ap, h=h:
                                 e.matmul(p[0:64, 0:256], l[0:64, :], r[0:64, h * 256:(h + 1) * 256], start=True, stop=False),
                                 deps=C["ps1"].wr() + [o10] + vc.rd())
                    at_sb.did_read(o11)
                    vc.did_read(o11)
                    o12 = P.emit("vector", lambda e, o=Sbf.ap, i=Sst[h].ap: e.tensor_copy(o[:, :], i[:, :]), deps=Sst[h].rd() + Sbf.wr())
                    Sbf.did_write(o12)
                    Sst[h].did_read(o12)
                    o13 = P.emit("tensor", lambda e, p=C["ps1"].ap, l=qg_sb.ap, r=Sbf.ap:
                                 e.matmul(p[0:64, 0:256], l[:, 0:64], r[:, :], start=False, stop=True),
                                 deps=[o11, o12, o4])
                    qg_sb.did_read(o13)
                    Sbf.did_read(o13)
                    C["ps1"].did_write(o13)
                    o14 = P.emit("vector", lambda e, o=oc.ap, i=C["ps1"].ap, h=h: e.tensor_copy(o[0:64, h * 256:(h + 1) * 256], i[0:64, 0:256]),
                                 deps=[o13] + (oc.wr() if h == 0 else []))
                    oc.did_write(o14, accum=True)
                    C["ps1"].did_read(o14)
                    o15 = P.emit("tensor", lambda e, p=C["ps0"].ap, l=kgt_sb.ap, r=vc.ap, h=h:
                                 e.matmul(p[:, 0:256], l[0:64, :], r[0:64, h * 256:(h + 1) * 256], start=True, stop=True),
                                 deps=C["ps0"].wr() + [o8] + vc.rd())
                    kgt_sb.did_read(o15)
                    vc.did_read(o15)
                    C["ps0"].did_write(o15)
                    o16 = P.emit("vector", lambda e, o=Sst[h].ap, g=eGb.ap: e.tensor_scalar(o[:, :], o[:, :], g[:, 63:64], None, OP.mult),
                                 deps=[o12, o2] + Sst[h].wr())
                    Sst[h].did_write(o16)
                    eGb.did_read(o16)
                    o17 = P.emit("vector", lambda e, o=Sst[h].ap, i=C["ps0"].ap: e.tensor_tensor(o[:, :], o[:, :], i[:, 0:256], OP.add),
                                 deps=[o15, o16])
                    Sst[h].did_write(o17)
                    C["ps0"].did_read(o17)
                so = P.emit("scalar", lambda e, i=oc.ap, o=o_tok.ap, n=n: e.dma_start(out=o[n * 64:(n + 1) * 64, :], in_=i[0:64, :]),
                            deps=oc.rd() + oc.w + o_tok.r, dma=True)
                oc.did_read(so)
                o_tok.did_write(so, accum=True)
            esG1.__exit__(None, None, None)
            fG1 = P.fence()
            gnwb = load_f32(P, nc, esG, "gnwb_s", gnwb_d.ap, [128, D])
            gnwb.w += fG1
            for tt in range(NT):
                ot = C["stB"][tt % 2]
                gt = C["stC"][tt % 2]
                l1 = P.emit("sync", lambda e, o=ot.ap, i=o_tok.ap, tt=tt: e.dma_start(out=o[:, :], in_=i[tt * 128:(tt + 1) * 128, :]), deps=ot.wr() + o_tok.rd(), dma=True)
                ot.did_write(l1)
                o_tok.did_read(l1)
                l2 = P.emit("sync", lambda e, o=gt.ap, i=g_tok.ap, tt=tt: e.dma_start(out=o[:, :], in_=i[tt * 128:(tt + 1) * 128, :]), deps=gt.wr() + g_tok.rd(), dma=True)
                gt.did_write(l2)
                g_tok.did_read(l2)
                sqb = C["stD"][tt % 2]
                o1 = P.emit("scalar", lambda e, o=sqb.ap, i=ot.ap: e.activation(o[:, :], i[:, :], AF.Square), deps=[l1] + sqb.wr())
                sqb.did_write(o1)
                ot.did_read(o1)
                s = C["sm"][tt % 4]
                hops = []
                for h in range(4):
                    r1 = P.emit("vector", lambda e, o=s.ap, i=sqb.ap, h=h: e.reduce_sum(o[:, h:h + 1], i[:, h * 256:(h + 1) * 256], axis=AX),
                                deps=[o1] + (s.wr() if h == 0 else [hops[-1]]))
                    hops.append(r1)
                s.did_write(hops[-1])
                sqb.did_read(hops[-1])
                o2 = P.emit("vector", lambda e, o=s.ap: e.tensor_scalar(o[:, 4:8], o[:, 0:4], 1.0 / 256.0, 1e-5, OP.mult, OP.add), deps=hops)
                s.did_write(o2)
                o3 = P.emit("scalar", lambda e, o=s.ap: e.activation(o[:, 4:8], o[:, 4:8], AF.Sqrt), deps=[o2])
                s.did_write(o3)
                o4 = P.emit("vector", lambda e, o=s.ap: e.reciprocal(o[:, 4:8], o[:, 4:8]), deps=[o3])
                s.did_write(o4)
                sg = C["stE"][tt % 2]
                o5 = P.emit("scalar", lambda e, o=sg.ap, i=gt.ap: e.activation(o[:, :], i[:, :], AF.Silu), deps=[l2] + sg.wr())
                sg.did_write(o5)
                gt.did_read(o5)
                on = C["stA"][tt % 2]
                stps = []
                for h in range(4):
                    p1 = P.emit("vector", lambda e, o=on.ap, i=ot.ap, s_=s.ap, g=gnwb.ap, h=h:
                                e.scalar_tensor_tensor(o[:, h * 256:(h + 1) * 256], i[:, h * 256:(h + 1) * 256], s_[:, 4 + h:5 + h], g[:, h * 256:(h + 1) * 256], OP.mult, OP.mult),
                                deps=[o4, l1] + (on.wr() if h == 0 else []) + gnwb.rd())
                    gnwb.did_read(p1)
                    on.did_write(p1, accum=True)
                    ot.did_read(p1)
                    s.did_read(p1)
                    stps.append(p1)
                oo = C["stB"][(tt + 1) % 2]
                o6 = P.emit("vector", lambda e, o=oo.ap, i=on.ap, g=sg.ap: e.tensor_tensor(o[:, :], i[:, :], g[:, :], OP.mult),
                            deps=stps + [o5] + oo.wr())
                oo.did_write(o6)
                on.did_read(o6)
                sg.did_read(o6)
                pt = [C["psT1"], C["psT2"]][tt % 2]
                trs = [tr128(P, C, oo, f * 128, pt, f * 128, extra=[o6]) for f in range(NF)]
                evb = C["stC"][(tt + 1) % 2]
                o7 = P.emit("vector", lambda e, o=evb.ap, i=pt.ap: e.tensor_copy(o[:, :], i[:, :]), deps=[trs[-1]] + evb.wr())
                evb.did_write(o7)
                pt.did_read(o7)
                for f in range(NF):
                    st = P.emit("scalar", lambda e, i=evb.ap, o=ooT_d.ap, f=f, tt=tt:
                                e.dma_start(out=o[f * 128:(f + 1) * 128, tt * 128:(tt + 1) * 128], in_=i[:, f * 128:(f + 1) * 128]),
                                deps=evb.rd() + [o7] + ooT_d.r, dma=True)
                    evb.did_read(st)
                    ooT_d.did_write(st, accum=True)
            load_w_into(P, W[0], wo_d.ap, D, D)
            proj_tm(P, C, ooT_d, W[0], make_store_cb(P, C, aout), src_pad=0)
            fG = P.fence()

        esI = ExitStack()
        with esI:
            W = alloc_wpool(nc, esI, 2)
            for wb in W:
                wb.w += fG
            n1b = load_f32(P, nc, esI, "n1b_s", n1b_d.ap, [128, D])
            n1b.w += fG
            emit_norm(P, C, xn_tok, rms1T, None, kind="rms", gamma_bc=n1b, eps=1e-6, src_bf16=True)
            rT_sb = load_bf16(P, nc, esI, "rT_sb", rT_d.ap, [128, 128])
            rT_sb.w += fG
            cosb = load_bf16(P, nc, esI, "cosb_s", cosb_d.ap, [128, T])
            cosb.w += fG
            sinb = load_bf16(P, nc, esI, "sinb_s", sinb_d.ap, [128, T])
            sinb.w += fG
            masks = []
            for m in range(4):
                mb = sbuf(nc, esI, f"mask{m}", [128, 512], BF16, init=fG)
                op = P.emit("gpsimd", lambda e, o=mb.ap, i=masks_d.ap, m=m: e.dma_start(out=o[:, :], in_=i[m * 128:(m + 1) * 128, :], accum_op=OP.bypass), deps=mb.wr(), dma=True)
                mb.did_write(op)
                masks.append(mb)

            def rope_cb(do, tc, eb, op):
                ps = C["psS2"]
                m1 = P.emit("tensor", lambda e, p=ps.ap, l=rT_sb.ap, r=eb.ap:
                            e.matmul(p[:, 0:512], l[:, :], r[:, 0:512], start=True, stop=True),
                            deps=ps.wr() + rT_sb.rd() + [op])
                rT_sb.did_read(m1)
                eb.did_read(m1)
                ps.did_write(m1)
                t1 = C["stD"][do % 2]
                o1 = P.emit("vector", lambda e, o=t1.ap, i=ps.ap, s=sinb.ap, tc=tc:
                            e.tensor_tensor(o[:, 0:512], i[:, 0:512], s[:, tc * 512:(tc + 1) * 512], OP.mult),
                            deps=[m1] + t1.wr() + sinb.rd())
                sinb.did_read(o1)
                t1.did_write(o1)
                ps.did_read(o1)
                t2 = C["stE"][do % 2]
                o2 = P.emit("vector", lambda e, o=t2.ap, i=eb.ap, s=cosb.ap, tc=tc:
                            e.tensor_tensor(o[:, 0:512], i[:, 0:512], s[:, tc * 512:(tc + 1) * 512], OP.mult),
                            deps=[op] + t2.wr() + cosb.rd())
                cosb.did_read(o2)
                t2.did_write(o2)
                eb.did_read(o2)
                nb = C["stB"][do % 2]
                o3 = P.emit("vector", lambda e, o=nb.ap, a=t2.ap, b=t1.ap: e.tensor_tensor(o[:, 0:512], a[:, 0:512], b[:, 0:512], OP.add),
                            deps=[o1, o2] + nb.wr())
                nb.did_write(o3)
                t1.did_read(o3)
                t2.did_read(o3)
                return nb, o3

            load_w_into(P, W[0], wqiT_d.ap, D, D)
            proj_fm(P, C, rms1T, W[0], qTi, D, scale=0.125, post_cb=rope_cb)
            load_w_into(P, W[1], wkiT_d.ap, D, D)
            proj_fm(P, C, rms1T, W[1], kTi, D, post_cb=rope_cb)
            load_w_into(P, W[0], wviT_d.ap, D, D)
            emit_vproj(P, C, rms1T, W[0], None, vaug)
            emit_attention(P, C, nc, esI, qTi, kTi, vaug, oTi, masks, tag="a2")
            load_w_into(P, W[1], wpiT_d.ap, D, D)
            proj_tm(P, C, oTi, W[1], make_store_cb(P, C, ai_d), src_pad=0)
            for tt in range(NT):
                a = C["stB"][tt % 2]
                b = C["stC"][tt % 2]
                l1 = P.emit("sync", lambda e, o=a.ap, i=xn_tok.ap, tt=tt: e.dma_start(out=o[:, :], in_=i[tt * 128:(tt + 1) * 128, :]), deps=a.wr() + xn_tok.rd(), dma=True)
                a.did_write(l1)
                xn_tok.did_read(l1)
                l2 = P.emit("sync", lambda e, o=b.ap, i=ai_d.ap, tt=tt: e.dma_start(out=o[:, :], in_=i[tt * 128:(tt + 1) * 128, :]), deps=b.wr() + ai_d.rd(), dma=True)
                b.did_write(l2)
                ai_d.did_read(l2)
                o1 = P.emit("vector", lambda e, o=a.ap, i=b.ap: e.tensor_tensor(o[:, :], o[:, :], i[:, :], OP.add), deps=[l1, l2])
                a.did_write(o1)
                b.did_read(o1)
                st = P.emit("scalar", lambda e, i=a.ap, o=xmid.ap, tt=tt: e.dma_start(out=o[tt * 128:(tt + 1) * 128, :], in_=i[:, :]), deps=[o1] + xmid.r, dma=True)
                a.did_read(st)
                xmid.did_write(st, accum=True)
            fI = P.fence()

        esM = ExitStack()
        with esM:
            sb2 = lambda n, s, d: sbuf(nc, esM, n, s, d, init=fI)
            idx_sb = sb2("idx_sb", [128, TL // 512], I32)
            iop = P.emit("sync", lambda e, o=idx_sb.ap, i=gidx_d.ap: e.dma_start(out=o[:, :], in_=i), deps=idx_sb.wr(), dma=True)
            idx_sb.did_write(iop)
            gt = sb2("gth", [128, 4096], BF16)
            for st in range(TL // 512):
                mzo = P.emit("gpsimd", lambda e, o=gt.ap: e.memset(o[:, :], 0.0), deps=gt.wr())
                gt.did_write(mzo)
                gop = P.emit("gpsimd", lambda e, o=gt.ap, i=xmid.ap.rearrange("(n k) d -> n (k d)", k=4), s=idx_sb.ap, st=st:
                             e.indirect_dma_start(out=o[:, :], out_offset=None, in_=i,
                                                  in_offset=bass.IndirectOffsetOnAxis(s[:, st:st + 1], 0),
                                                  bounds_check=T // 4 - 1, oob_is_err=False),
                             deps=[mzo] + idx_sb.rd() + xmid.rd(), dma=True)
                gt.did_write(gop, accum=True)
                idx_sb.did_read(gop)
                xmid.did_read(gop)
                sop = P.emit("scalar", lambda e, i=gt.ap, o=xmh.ap, st=st:
                             e.dma_start(out=o[st * 512:(st + 1) * 512, :].rearrange("(p k) d -> p k d", k=4),
                                         in_=i[:, :].rearrange("p (k d) -> p k d", d=D)),
                             deps=[gop] + xmh.r, dma=True)
                gt.did_read(sop)
                xmh.did_write(sop, accum=True)
            n2b = load_f32(P, nc, esM, "n2b_s", n2b_d.ap, [128, D])
            n2b.w += fI
            emit_norm(P, C, xmh, x2T, None, kind="rms", gamma_bc=n2b, eps=1e-6, src_bf16=True, n_tok=TL)
            rw_sb = load_w(P, nc, esM, "rw_sb", rwT_d.ap, D, 8)
            rw_sb.w += fI
            sel_s = load_bf16(P, nc, esM, "sel2_s", sel_d.ap, [8, 1024])
            sel_s.w += fI
            lg = sb2("lg2", [128, 8], F32)
            m8 = sb2("m82", [128, 8], F32)
            w12 = sb2("w122", [128, 2], F32)
            mk1 = sb2("mk12", [128, 8], F32)
            mk2 = sb2("mk22", [128, 8], F32)
            cwb = sb2("cwb2", [128, 128], BF16)
            gatesT = sb2("gatesT", [8, TL], BF16)
            x2f = [sb2(f"x2f{c}", [128, 1024], BF16) for c in range(NF)]
            for tt in range(TL // 128):
                lh = C["lhsP"][tt % 2]
                for c in range(NF):
                    op = P.emit("sync", lambda e, o=lh[c].ap, i=x2T.ap, c=c, tt=tt:
                                e.dma_start(out=o[:, :], in_=i[c * 128:(c + 1) * 128, PAD + tt * 128: PAD + (tt + 1) * 128]),
                                deps=lh[c].wr() + x2T.rd(), dma=True)
                    lh[c].did_write(op)
                    x2T.did_read(op)
                p = C["ps0"]
                mms = []
                for c in range(NF):
                    op = P.emit("tensor", lambda e, p_=p.ap, l=lh[c].ap, w=rw_sb.ap, c=c, first=(c == 0), last=(c == NF - 1):
                                e.matmul(p_[:, 0:8], l[:, :], w[:, c * 8:(c + 1) * 8], start=first, stop=last),
                                deps=(p.wr() if c == 0 else []) + lh[c].rd() + rw_sb.rd() + (mms[-1:] if mms else []))
                    mms.append(op)
                    lh[c].did_read(op)
                    rw_sb.did_read(op)
                p.did_write(mms[-1])
                o1 = P.emit("vector", lambda e, o=lg.ap, i=p.ap: e.tensor_copy(o[:, :], i[:, 0:8]), deps=[mms[-1]] + lg.wr())
                lg.did_write(o1)
                p.did_read(o1)
                o2 = P.emit("vector", lambda e, o=m8.ap, i=lg.ap: e.max(out=o[:, :], in_=i[:, :]), deps=[o1] + m8.wr())
                m8.did_write(o2)
                lg.did_read(o2)
                o3 = P.emit("vector", lambda e, o=w12.ap, i=m8.ap: e.tensor_tensor(o[:, 0:1], i[:, 0:1], i[:, 1:2], OP.subtract), deps=[o2] + w12.wr())
                o4 = P.emit("vector", lambda e, o=w12.ap, i=m8.ap: e.tensor_tensor(o[:, 1:2], i[:, 1:2], i[:, 0:1], OP.subtract), deps=[o2, o3])
                w12.did_write(o4)
                m8.did_read(o4)
                o5 = P.emit("scalar", lambda e, o=w12.ap: e.activation(o[:, 0:1], o[:, 0:1], AF.Sigmoid), deps=[o3, o4])
                o6 = P.emit("scalar", lambda e, o=w12.ap: e.activation(o[:, 1:2], o[:, 1:2], AF.Sigmoid), deps=[o4, o5])
                w12.did_write(o6)
                o7 = P.emit("vector", lambda e, o=mk1.ap, i=lg.ap, m=m8.ap: e.tensor_scalar(o[:, :], i[:, :], m[:, 0:1], None, OP.is_ge), deps=[o2] + mk1.wr())
                mk1.did_write(o7)
                o8 = P.emit("vector", lambda e, o=mk2.ap, i=lg.ap, m=m8.ap: e.tensor_scalar(o[:, :], i[:, :], m[:, 1:2], None, OP.is_ge), deps=[o2, o7] + mk2.wr())
                mk2.did_write(o8)
                lg.did_read(o8)
                m8.did_read(o8)
                o9 = P.emit("vector", lambda e, o=mk2.ap, i=mk1.ap: e.tensor_tensor(o[:, :], o[:, :], i[:, :], OP.subtract), deps=[o7, o8])
                mk2.did_write(o9)
                mk1.did_read(o9)
                oa = P.emit("vector", lambda e, o=mk1.ap, w=w12.ap: e.tensor_scalar(o[:, :], o[:, :], w[:, 0:1], None, OP.mult), deps=[o9, o6])
                mk1.did_write(oa)
                ob = P.emit("vector", lambda e, o=mk2.ap, w=w12.ap: e.tensor_scalar(o[:, :], o[:, :], w[:, 1:2], None, OP.mult), deps=[o9, o6, oa])
                mk2.did_write(ob)
                w12.did_read(ob)
                oc = P.emit("vector", lambda e, o=cwb.ap, a=mk1.ap, b=mk2.ap: e.tensor_tensor(o[:, 0:8], a[:, :], b[:, :], OP.add), deps=[oa, ob] + cwb.wr())
                cwb.did_write(oc)
                mk1.did_read(oc)
                mk2.did_read(oc)
                og = P.emit("vector", lambda e, o=lg.ap, a=mk1.ap, b=mk2.ap: e.tensor_tensor(o[:, :], a[:, :], b[:, :], OP.add), deps=[oa, ob] + lg.wr() + lg.r)
                lg.did_write(og)
                mk1.did_read(og)
                mk2.did_read(og)
                sg = P.emit("scalar", lambda e, i=lg.ap, o=gates_d.ap, tt=tt: e.dma_start(out=o[tt * 128:(tt + 1) * 128, :], in_=i[:, :]), deps=[og] + gates_d.r, dma=True)
                lg.did_read(sg)
                gates_d.did_write(sg, accum=True)
                pt = C["psT1"]
                tr = tr128(P, C, cwb, 0, pt, 0, extra=[oc])
                ev = P.emit("vector", lambda e, o=gatesT.ap, i=pt.ap, tt=tt: e.tensor_copy(o[0:8, tt * 128:(tt + 1) * 128], i[0:8, 0:128]), deps=[tr] + gatesT.r)
                gatesT.did_write(ev, accum=True)
                pt.did_read(ev)
            sT = [sb2(f"sT2_{i}", [128, 1024], BF16) for i in range(32)]
            wgt = sb2("wgt", [128, 1024], BF16)
            wut = sb2("wut", [128, 1024], BF16)
            wdt = sb2("wdt", [128, 4096], BF16)
            gbx = sb2("gbx", [128, 1024], BF16)
            ypv = [sb2(f"ypv2{i}", [128, 512], F32) for i in range(2)]
            for tb_off, tb_w in ((0, 1024), (1024, 1024), (2048, 512)):
                for c in range(NF):
                    op = P.emit("sync", lambda e, o=x2f[c].ap, i=x2T.ap, c=c, tb_off=tb_off, tb_w=tb_w:
                                e.dma_start(out=o[:, 0:tb_w], in_=i[c * 128:(c + 1) * 128, PAD + tb_off: PAD + tb_off + tb_w]),
                                deps=x2f[c].wr() + x2T.rd(), dma=True)
                    x2f[c].did_write(op)
                    x2T.did_read(op)
                for e_ in range(8):
                    for ho in range(32):
                        lw1 = P.emit("sync", lambda e, o=wgt.ap, i=wgT_d.ap, e_=e_, ho=ho:
                                     e.dma_start(out=o[:, :].rearrange("p (c w) -> p c w", w=128),
                                                 in_=i[e_ * 1024:(e_ + 1) * 1024, ho * 128:(ho + 1) * 128].rearrange("(c p) w -> p c w", p=128)),
                                     deps=wgt.wr() + wgT_d.rd(), dma=True)
                        wgt.did_write(lw1)
                        wgT_d.did_read(lw1)
                        lw2 = P.emit("sync", lambda e, o=wut.ap, i=wuT_d.ap, e_=e_, ho=ho:
                                     e.dma_start(out=o[:, :].rearrange("p (c w) -> p c w", w=128),
                                                 in_=i[e_ * 1024:(e_ + 1) * 1024, ho * 128:(ho + 1) * 128].rearrange("(c p) w -> p c w", p=128)),
                                     deps=wut.wr() + wuT_d.rd(), dma=True)
                        wut.did_write(lw2)
                        wuT_d.did_read(lw2)
                        for tc2 in range(tb_w // 512):
                            pg = C["ps0"]
                            pu = C["ps1"]
                            mg, mu = [], []
                            for c in range(NF):
                                op = P.emit("tensor", lambda e, p_=pg.ap, l=wgt.ap, r=x2f[c].ap, c=c, tc2=tc2, first=(c == 0), last=(c == NF - 1):
                                            e.matmul(p_[:, :], l[:, c * 128:(c + 1) * 128], r[:, tc2 * 512:(tc2 + 1) * 512], start=first, stop=last),
                                            deps=(pg.wr() if c == 0 else []) + [lw1] + x2f[c].rd() + (mg[-1:] if mg else []))
                                mg.append(op)
                                wgt.did_read(op)
                                x2f[c].did_read(op)
                            pg.did_write(mg[-1])
                            for c in range(NF):
                                op = P.emit("tensor", lambda e, p_=pu.ap, l=wut.ap, r=x2f[c].ap, c=c, tc2=tc2, first=(c == 0), last=(c == NF - 1):
                                            e.matmul(p_[:, :], l[:, c * 128:(c + 1) * 128], r[:, tc2 * 512:(tc2 + 1) * 512], start=first, stop=last),
                                            deps=(pu.wr() if c == 0 else []) + [lw2] + x2f[c].rd() + (mu[-1:] if mu else []))
                                mu.append(op)
                                wut.did_read(op)
                                x2f[c].did_read(op)
                            pu.did_write(mu[-1])
                            gsil = C["stB"][tc2]
                            e1 = P.emit("scalar", lambda e, o=gsil.ap, i=pg.ap: e.activation(o[:, 0:512], i[:, :], AF.Silu), deps=[mg[-1]] + gsil.wr())
                            gsil.did_write(e1)
                            pg.did_read(e1)
                            e2 = P.emit("vector", lambda e, o=sT[ho].ap, a=gsil.ap, b=pu.ap, tc2=tc2:
                                        e.tensor_tensor(o[:, tc2 * 512:(tc2 + 1) * 512], a[:, 0:512], b[:, :], OP.mult),
                                        deps=[e1, mu[-1]] + sT[ho].wr())
                            sT[ho].did_write(e2, accum=True)
                            gsil.did_read(e2)
                            pu.did_read(e2)
                    for tc2 in range(tb_w // 512):
                        p = C["psS2"]
                        mm = P.emit("tensor", lambda e, p_=p.ap, l=sel_s.ap, r=gatesT.ap, e_=e_, tb_off=tb_off, tc2=tc2:
                                    e.matmul(p_[:, 0:512], l[0:8, e_ * 128:(e_ + 1) * 128], r[0:8, tb_off + tc2 * 512: tb_off + (tc2 + 1) * 512], start=True, stop=True),
                                    deps=p.wr() + sel_s.rd() + gatesT.rd())
                        sel_s.did_read(mm)
                        gatesT.did_read(mm)
                        p.did_write(mm)
                        ev = P.emit("vector", lambda e, o=gbx.ap, i=p.ap, tc2=tc2: e.tensor_copy(o[:, tc2 * 512:(tc2 + 1) * 512], i[:, 0:512]),
                                    deps=[mm] + (gbx.wr() if tc2 == 0 else []))
                        gbx.did_write(ev, accum=True)
                        p.did_read(ev)
                    for do in range(NF):
                        lw3 = P.emit("sync", lambda e, o=wdt.ap, i=wdT_d.ap, e_=e_, do=do:
                                     e.dma_start(out=o[:, :].rearrange("p (c w) -> p c w", w=128),
                                                 in_=i[e_ * 4096:(e_ + 1) * 4096, do * 128:(do + 1) * 128].rearrange("(c p) w -> p c w", p=128)),
                                     deps=wdt.wr() + wdT_d.rd(), dma=True)
                        wdt.did_write(lw3)
                        wdT_d.did_read(lw3)
                        for tc2 in range(tb_w // 512):
                            p = C["psS1"]
                            mms = []
                            for ho in range(32):
                                op = P.emit("tensor", lambda e, p_=p.ap, l=wdt.ap, r=sT[ho].ap, ho=ho, tc2=tc2, first=(ho == 0), last=(ho == 31):
                                            e.matmul(p_[:, 0:512], l[:, ho * 128:(ho + 1) * 128], r[:, tc2 * 512:(tc2 + 1) * 512], start=first, stop=last),
                                            deps=(p.wr() if ho == 0 else []) + [lw3] + sT[ho].rd() + (mms[-1:] if mms else []))
                                mms.append(op)
                                wdt.did_read(op)
                                sT[ho].did_read(op)
                            p.did_write(mms[-1])
                            eb = C["stE"][tc2]
                            o1 = P.emit("vector", lambda e, o=eb.ap, i=p.ap, g=gbx.ap, tc2=tc2:
                                        e.tensor_tensor(o[:, 0:512], i[:, 0:512], g[:, tc2 * 512:(tc2 + 1) * 512], OP.mult),
                                        deps=[mms[-1]] + eb.wr() + gbx.rd())
                            gbx.did_read(o1)
                            eb.did_write(o1)
                            p.did_read(o1)
                            if e_ > 0:
                                yp = ypv[tc2]
                                lp = P.emit("sync", lambda e, o=yp.ap, i=yT_d.ap, do=do, tb_off=tb_off, tc2=tc2:
                                            e.dma_start(out=o[:, :], in_=i[do * 128:(do + 1) * 128, tb_off + tc2 * 512: tb_off + (tc2 + 1) * 512]),
                                            deps=yp.wr() + yT_d.rd(), dma=True)
                                yp.did_write(lp)
                                yT_d.did_read(lp)
                                o2 = P.emit("vector", lambda e, o=eb.ap, i=yp.ap: e.tensor_tensor(o[:, 0:512], o[:, 0:512], i[:, :], OP.add), deps=[o1, lp])
                                eb.did_write(o2)
                                yp.did_read(o2)
                            else:
                                o2 = o1
                            st = P.emit("scalar", lambda e, i=eb.ap, o=yT_d.ap, do=do, tb_off=tb_off, tc2=tc2:
                                        e.dma_start(out=o[do * 128:(do + 1) * 128, tb_off + tc2 * 512: tb_off + (tc2 + 1) * 512], in_=i[:, 0:512]),
                                        deps=[o2] + yT_d.r, dma=True)
                            eb.did_read(st)
                            yT_d.w.append(st)
            fM = P.fence()

        idx2 = sbuf(nc, es, "idx2", [128, TL // 512], I32, init=fM)
        iop2 = P.emit("sync", lambda e, o=idx2.ap, i=gidx_d.ap: e.dma_start(out=o[:, :], in_=i), deps=idx2.wr(), dma=True)
        idx2.did_write(iop2)
        g4f = sbuf(nc, es, "g4f", [128, 4096], F32, init=fM)
        g4b = sbuf(nc, es, "g4b", [128, 4096], BF16, init=fM)
        for st in range(TL // 512):
            for (gbuf, srcb, dstb, kk) in ((g4f, x_in, xg_d, 4), (g4b, aout, ag_d, 4)):
                mz1 = P.emit("gpsimd", lambda e, o=gbuf.ap: e.memset(o[:, :], 0.0), deps=gbuf.wr())
                gbuf.did_write(mz1)
                l1 = P.emit("gpsimd", lambda e, o=gbuf.ap, i=srcb.ap.rearrange("(n k) d -> n (k d)", k=4), s=idx2.ap, st=st:
                            e.indirect_dma_start(out=o[:, :], out_offset=None, in_=i,
                                                 in_offset=bass.IndirectOffsetOnAxis(s[:, st:st + 1], 0),
                                                 bounds_check=T // 4 - 1, oob_is_err=False),
                            deps=[mz1] + idx2.rd() + srcb.rd(), dma=True)
                gbuf.did_write(l1, accum=True)
                idx2.did_read(l1)
                srcb.did_read(l1)
                s1 = P.emit("scalar", lambda e, i=gbuf.ap, o=dstb.ap, st=st:
                            e.dma_start(out=o[st * 512:(st + 1) * 512, :].rearrange("(p k) d -> p k d", k=4),
                                        in_=i[:, :].rearrange("p (k d) -> p k d", d=D)),
                            deps=[l1] + dstb.r, dma=True)
                gbuf.did_read(s1)
                dstb.did_write(s1, accum=True)
        for t in range(TL // 128):
            a = C["stA"][t % 2]
            u = C["stB"][t % 2]
            v = C["stC"][t % 2]
            ft = C["stD"][t % 2]
            res = C["stE"][t % 2]
            l1 = P.emit("sync", lambda e, o=a.ap, i=xg_d.ap, t=t: e.dma_start(out=o[:, :], in_=i[t * 128:(t + 1) * 128, :]), deps=a.wr() + xg_d.rd() + fM, dma=True)
            a.did_write(l1)
            xg_d.did_read(l1)
            l2 = P.emit("sync", lambda e, o=u.ap, i=ag_d.ap, t=t: e.dma_start(out=o[:, :], in_=i[t * 128:(t + 1) * 128, :]), deps=u.wr() + ag_d.rd() + fM, dma=True)
            u.did_write(l2)
            ag_d.did_read(l2)
            l3 = P.emit("sync", lambda e, o=v.ap, i=xmh.ap, t=t: e.dma_start(out=o[:, :], in_=i[t * 128:(t + 1) * 128, :]), deps=v.wr() + xmh.rd() + fM, dma=True)
            v.did_write(l3)
            xmh.did_read(l3)
            bstage = C["vstg"][t % 2] if False else None
            pt = [C["psT1"], C["psT2"]][t % 2]
            trs = []
            yb16 = C["stC"][(t + 1) % 2]
            for f in range(NF):
                yst = C["stD"][(t + 1) % 2]
                l4 = P.emit("sync", lambda e, o=yst.ap, i=yT_d.ap, f=f, t=t:
                            e.dma_start(out=o[:, 0:128], in_=i[f * 128:(f + 1) * 128, t * 128:(t + 1) * 128]),
                            deps=yst.wr() + yT_d.rd() + fM, dma=True)
                yst.did_write(l4)
                yT_d.did_read(l4)
                cv = P.emit("vector", lambda e, o=yb16.ap, i=yst.ap, f=f: e.tensor_copy(o[:, f * 128:(f + 1) * 128], i[:, 0:128]),
                            deps=[l4] + (yb16.wr() if f == 0 else []))
                yb16.did_write(cv, accum=True)
                yst.did_read(cv)
                trs.append(tr128(P, C, yb16, f * 128, pt, f * 128, extra=[cv]))
            ev = P.emit("vector", lambda e, o=ft.ap, i=pt.ap: e.tensor_copy(o[:, :], i[:, :]), deps=[trs[-1]] + ft.wr())
            ft.did_write(ev)
            pt.did_read(ev)
            o1 = P.emit("vector", lambda e, o=res.ap, i0=a.ap, i1=u.ap: e.tensor_tensor(o[:, :], i0[:, :], i1[:, :], OP.add), deps=[l1, l2] + res.wr())
            res.did_write(o1)
            a.did_read(o1)
            u.did_read(o1)
            o2 = P.emit("vector", lambda e, o=res.ap, i1=v.ap: e.tensor_tensor(o[:, :], o[:, :], i1[:, :], OP.add), deps=[o1, l3])
            res.did_write(o2)
            v.did_read(o2)
            o3 = P.emit("vector", lambda e, o=res.ap, i1=ft.ap: e.tensor_tensor(o[:, :], o[:, :], i1[:, :], OP.add), deps=[o2, ev])
            res.did_write(o3)
            ft.did_read(o3)
            st = P.emit("scalar", lambda e, i=res.ap, o=y.ap, t=t: e.dma_start(out=o[t * 128:(t + 1) * 128, :], in_=i[:, :]), deps=[o3] + y.r, dma=True)
            res.did_read(st)
            y.did_write(st, accum=True)


def build_layer2():
    return _build_single(emit_layer2)


def prep_l2_inputs(ins, xb):
    import ml_dtypes
    ident, masks = _l0_consts()
    HD = 64
    inv_freq = 1.0 / (10000.0 ** (np.arange(0, HD, 2, dtype=np.float64) / HD))
    tpos = np.arange(T, dtype=np.float64)
    freqs = np.einsum("i,j->ij", tpos, inv_freq)
    emb = np.concatenate([freqs, freqs], axis=-1)
    cosb = np.tile(np.cos(emb).T.astype(np.float32), (2, 1))
    sinb = np.tile(np.sin(emb).T.astype(np.float32), (2, 1))
    R = np.zeros((128, 128), np.float32)
    for base in (0, 64):
        for dd in range(32):
            R[base + dd, base + 32 + dd] = -1.0
            R[base + 32 + dd, base + dd] = 1.0
    triu = np.triu(np.ones((64, 64), np.float32))
    qkv = np.asarray(ins["L2_ffn_qkv_w"], np.float32)
    wg_m = np.asarray(ins["L2_ffn_wg"], np.float32)
    wu_m = np.asarray(ins["L2_ffn_wu"], np.float32)
    wd_m = np.asarray(ins["L2_ffn_wd"], np.float32)
    bf = ml_dtypes.bfloat16
    sel = np.zeros((8, 1024), np.float32)
    for e in range(8):
        sel[e, e * 128:(e + 1) * 128] = 1.0
    return {
        "x": np.ascontiguousarray(xb, dtype=np.float32),
        "ident": ident, "masks": masks, "triu": triu,
        "sel4": _SEL4, "sel64": _SEL64,
        "wq": np.asarray(ins["L2_attn_Wq"], np.float32),
        "wk": np.asarray(ins["L2_attn_Wk"], np.float32),
        "wv": np.asarray(ins["L2_attn_Wv"], np.float32),
        "wg": np.asarray(ins["L2_attn_Wg"], np.float32),
        "wgk1": np.asarray(ins["L2_attn_Wgk1"], np.float32),
        "wgk2": np.asarray(ins["L2_attn_Wgk2"], np.float32),
        "bgk2": np.ascontiguousarray(np.asarray(ins["L2_attn_bgk2"], np.float32).reshape(4, 128).T),
        "wo": np.asarray(ins["L2_attn_Wo"], np.float32),
        "gnwb": brow(np.tile(np.asarray(ins["L2_attn_gnw"], np.float32), 4)),
        "n1b": brow(np.asarray(ins["L2_ffn_norm1_w"], np.float32)),
        "n2b": brow(np.asarray(ins["L2_ffn_norm2_w"], np.float32)),
        "wqiT": t_(qkv[0:1024]),
        "wkiT": t_(qkv[1024:2048]),
        "wviT": t_(qkv[2048:3072]),
        "wpiT": t_(np.asarray(ins["L2_ffn_proj_w"], np.float32)),
        "rT": t_(R),
        "cosb": cosb, "sinb": sinb,
        "rwT": t_(np.asarray(ins["L2_ffn_router_w"], np.float32)),
        "wgmT": np.ascontiguousarray(np.transpose(wg_m, (0, 2, 1)).reshape(8 * 1024, 4096)).astype(bf),
        "wumT": np.ascontiguousarray(np.transpose(wu_m, (0, 2, 1)).reshape(8 * 1024, 4096)).astype(bf),
        "wdmT": np.ascontiguousarray(np.transpose(wd_m, (0, 2, 1)).reshape(8 * 4096, 1024)).astype(bf),
        "sel": sel,
    }


def build_fused():
    global CUR_FENCE
    _reset_build_state()
    nc = bass.Bass("TRN2", target_bir_lowering=False)
    x_in = Buf(nc.dram_tensor("x", [T, D], F32, kind="ExternalInput").ap())
    ident_d = Buf(nc.dram_tensor("ident", [128, 128], F32, kind="ExternalInput").ap())
    masks_d = Buf(nc.dram_tensor("masks", [512, 512], F32, kind="ExternalInput").ap())
    y = Buf(nc.dram_tensor("y", [2048, D], F32, kind="ExternalOutput").ap())
    xa = dbuf(nc, "xa", [T, D], F32)
    xb = dbuf(nc, "xb", [T, D], F32)
    xc = dbuf(nc, "xc", [TL3, D], F32)
    sel4_d = Buf(nc.dram_tensor("sel4", [4, 260], F32, kind="ExternalInput").ap())
    sel64_d = Buf(nc.dram_tensor("sel64", [65, 65], F32, kind="ExternalInput").ap())
    es = ExitStack()
    with es:
        P = Prog(nc)
        C = alloc_common(nc, es)
        load_ident(P, C, ident_d.ap)
        C["sel4"] = load_bf16(P, nc, es, "sel4_s", sel4_d.ap, [4, 260])
        C["sel64"] = load_bf16(P, nc, es, "sel64_s", sel64_d.ap, [65, 65])
        stages = [
            (emit_layer0, "l0_", x_in, xa),
            (emit_layer1, "l1_", xa, xb),
            (emit_layer2, "l2_", xb, xc),
            (emit_layer3, "l3_", xc, y),
        ]
        for em, pfx, xi, xo in stages:
            eL = ExitStack()
            with eL:
                em(nc, eL, P, C, pfx, xi, xo, masks_d)
            CUR_FENCE = P.fence()
        final_waits(P)
        P.finalize(es)
    CUR_FENCE = []
    return nc


_GLOBAL_KEYS = ("x", "ident", "masks", "sel4", "sel64")


def prep_fused_inputs(ins, xb_, half):
    ident, masks = _l0_consts()
    gidx = np.arange(half * 2048 - TOFF, half * 2048 - TOFF + TL3, 4, dtype=np.int64)
    gidx = np.where((gidx < 0) | (gidx >= T), 4 << 20, gidx // 4).astype(np.int32)
    m = {"x": np.ascontiguousarray(xb_, dtype=np.float32), "ident": ident, "masks": masks,
         "sel4": _SEL4, "sel64": _SEL64,
         "l2_gidx": np.ascontiguousarray(gidx.reshape(TL3 // 512, 128).T)}
    for pfx, prep in (("l0_", prep_l0_inputs), ("l1_", prep_l1_inputs),
                      ("l2_", prep_l2_inputs), ("l3_", prep_l3_inputs)):
        sub = prep(ins, xb_)
        for k, v in sub.items():
            if k in _GLOBAL_KEYS:
                continue
            m[pfx + k] = v
    return m


def kernel(**inputs):
    x = np.asarray(inputs["L0_attn_x"], np.float32)
    nc = build_fused()
    in_maps = [prep_fused_inputs(inputs, x[c // 2], c % 2) for c in range(8)]
    res = run_bass_kernel_spmd(nc, in_maps, core_ids=list(range(8)))
    out = np.empty((B, T, D), np.float32)
    for b in range(B):
        out[b, 0:2048] = res.results[2 * b]["y"]
        out[b, 2048:4096] = res.results[2 * b + 1]["y"]
    return out
